# version 06
# baseline (speedup 1.0000x reference)
import numpy as np
import ml_dtypes
import concourse.bass as bass
import concourse.mybir as mybir
from concourse.bass_utils import run_bass_kernel_spmd

F32 = mybir.dt.float32
BF16 = mybir.dt.bfloat16
I32 = mybir.dt.int32
AF = mybir.ActivationFunctionType
ALU = mybir.AluOpType
AX = mybir.AxisListType
BF = ml_dtypes.bfloat16

D = 4096
T = 1024
NT = 32
EPS = 1e-5


def pack_lhsT(W):
    K, M = W.shape
    w = W.reshape(K // 128, 128, M // 128, 128)
    return np.ascontiguousarray(w.transpose(2, 1, 0, 3))


class Ctx:
    def __init__(self, nc):
        self.nc = nc
        self.q = {"tensor": [], "vector": [], "scalar": [], "gpsimd": [], "sync": []}
        self.sem = {}
        self.cnt = {}
        self.buf = {}
        self.mark = {}
        self.static_loads = []

    def add_sem(self, name):
        self.sem[name] = self.nc.alloc_semaphore(name)
        self.cnt[name] = 0

    def inc(self, name, n=1):
        self.cnt[name] += n
        return self.cnt[name]

    def on(self, eng):
        def deco(f):
            self.q[eng].append(f)
        return deco

    def sbuf(self, name, shape, dtype):
        h = self.nc.alloc_sbuf_tensor(name, shape, dtype)
        self.buf[name] = h
        return h

    def init_arena(self):
        self.arena_base = (int(self.nc.sbuf_base) + 63) & ~63
        self.arena_off = 0
        self.arena_peak = 0

    def arena_reset(self):
        self.arena_off = 0

    def ssbuf(self, name, shape, dtype):
        nb = int(shape[1]) * (2 if dtype == BF16 else 4)
        off = self.arena_base + self.arena_off
        self.arena_off += (nb + 31) & ~31
        self.arena_peak = max(self.arena_peak, self.arena_off)
        h = self.nc.alloc_sbuf_tensor_at(name, shape, dtype, offset=off)
        self.buf[name] = h
        return h

    def stage_load_emit(self, pairs):
        tot = self.inc("sd", 16 * len(pairs))
        S = self.sem

        @self.on("gpsimd")
        def _(g):
            for dst, src in pairs:
                g.dma_start(out=dst, in_=src).then_inc(S["sd"], 16)
        return tot

    def psum(self, name, shape, dtype=F32):
        h = self.nc.alloc_psum_tensor(name, shape, dtype)
        self.buf[name] = h
        return h

    def static_load(self, dst_ap, src_ap):
        self.static_loads.append((dst_ap, src_ap))

    def sc_total(self):
        return 16 * len(self.static_loads)



def _emit_ln_finalize(ctx, stats_pe, first):
    xcen = ctx.buf["xcen"]
    psS, psM = ctx.buf["pA2"], ctx.buf["pB2"]
    dmu, var, r = ctx.buf["dmu"], ctx.buf["var"], ctx.buf["r"]
    dmub, mu_acc, tmpT = ctx.buf["dmub"], ctx.buf["mu_acc"], ctx.buf["tmpT"]
    S, V = ctx.sem, ctx.inc

    v_dmu = V("dv"); v_ex2 = V("dv"); v_d2 = V("dv"); v_var = V("dv")
    a_dmub = V("ac"); a_ln = V("ac"); a_r = V("ac"); a_rbf = V("ac")
    v_mu = V("dv")
    cent = [V("dv") for _ in range(NT)]

    @ctx.on("vector")
    def _(v):
        v.wait_ge(S["pe"], stats_pe)
        v.tensor_scalar(dmu[:], psM[:, 0:T], 1.0 / D, None, ALU.mult).then_inc(S["dv"], 1)
        v.tensor_scalar(var[:], psS[:, 0:T], 1.0 / D, None, ALU.mult).then_inc(S["dv"], 1)
        v.wait_ge(S["dv"], v_ex2)
        v.tensor_tensor(tmpT[:], dmu[:], dmu[:], ALU.mult).then_inc(S["dv"], 1)
        v.wait_ge(S["dv"], v_d2)
        v.tensor_tensor(var[:], var[:], tmpT[:], ALU.subtract).then_inc(S["dv"], 1)
        v.wait_ge(S["ac"], a_dmub)
        if first:
            v.tensor_copy(mu_acc[:], dmub[:]).then_inc(S["dv"], 1)
        else:
            v.tensor_tensor(mu_acc[:], mu_acc[:], dmub[:], ALU.add).then_inc(S["dv"], 1)
        v.wait_ge(S["dv"], v_mu)
        for i in range(NT):
            v.tensor_tensor(xcen[:, i * T:(i + 1) * T], xcen[:, i * T:(i + 1) * T],
                            dmub[:], ALU.subtract).then_inc(S["dv"], 1)

    @ctx.on("scalar")
    def _(a):
        a.wait_ge(S["dv"], v_dmu)
        a.activation(dmub[:], dmu[:], AF.Identity).then_inc(S["ac"], 1)
        a.wait_ge(S["dv"], v_var)
        a.activation(tmpT[:], var[:], AF.Ln, bias=ctx.buf["eps_sb"][:, 0:1]).then_inc(S["ac"], 1)
        a.wait_ge(S["ac"], a_ln)
        a.activation(r[:], tmpT[:], AF.Exp, scale=-0.5).then_inc(S["ac"], 1)
        a.wait_ge(S["ac"], a_r)
        a.activation(ctx.buf["rbf"][:], r[:], AF.Identity).then_inc(S["ac"], 1)

    ctx.mark["center_dv"] = cent[-1]
    ctx.mark["r_ac"] = a_rbf


def _emit_stats(ctx, entry_dv):
    xcen, ones = ctx.buf["xcen"], ctx.buf["ones128"]
    xsq = ctx.buf["scr2T"]
    psS, psM = ctx.buf["pA2"], ctx.buf["pB2"]
    S, V = ctx.sem, ctx.inc

    sq_done = [V("ac") for _ in range(NT)]
    sq_used = []
    pe_last = None
    for i in range(NT):
        pe_last = V("pe")
        sq_used.append(pe_last)

    @ctx.on("scalar")
    def _(a):
        if entry_dv is None:
            a.wait_ge(S["swa"], ctx.mark["xin_swa"])
            a.wait_ge(S["swb"], ctx.mark["xin_swb"])
        else:
            a.wait_ge(S["dv"], entry_dv)
        for i in range(NT):
            if i >= 2:
                a.wait_ge(S["pe"], sq_used[i - 2])
            a.activation(xsq[:, (i % 2) * T:(i % 2) * T + T],
                         xcen[:, i * T:(i + 1) * T], AF.Square).then_inc(S["ac"], 1)

    @ctx.on("tensor")
    def _(t):
        if entry_dv is None:
            t.wait_ge(S["swa"], ctx.mark["xin_swa"])
            t.wait_ge(S["swb"], ctx.mark["xin_swb"])
        else:
            t.wait_ge(S["dv"], entry_dv)
        for i in range(NT):
            t.wait_ge(S["ac"], sq_done[i])
            for half in range(2):
                sl = slice(half * 512, half * 512 + 512)
                xs = slice(i * T + half * 512, i * T + half * 512 + 512)
                qs = slice((i % 2) * T + half * 512, (i % 2) * T + half * 512 + 512)
                t.matmul(psM[:, sl], ones[:], xcen[:, xs], start=(i == 0), stop=(i == NT - 1))
                ins = t.matmul(psS[:, sl], ones[:], xsq[:, qs], start=(i == 0), stop=(i == NT - 1))
            ins.then_inc(S["pe"], 1)

    return pe_last


def emit_load_and_ln1(ctx, xT_h):
    xcen = ctx.buf["xcen"]
    S, V = ctx.sem, ctx.inc

    ld_done = [V("swa" if i % 2 == 0 else "swb", 16) for i in range(NT)]

    @ctx.on("gpsimd")
    def _(g):
        for i in range(NT):
            g.dma_start(out=xcen[:, i * T:(i + 1) * T],
                        in_=xT_h.ap()[i * 128:(i + 1) * 128, :]
                        ).then_inc(S["swa" if i % 2 == 0 else "swb"], 16)

    ctx.mark["xin_swa"] = ctx.cnt["swa"]
    ctx.mark["xin_swb"] = ctx.cnt["swb"]
    pe_last = _emit_stats(ctx, None)
    _emit_ln_finalize(ctx, pe_last, first=True)


def emit_ln(ctx, first=False):
    pe_last = _emit_stats(ctx, ctx.mark["xcen_done"])
    _emit_ln_finalize(ctx, pe_last, first=first)



def emit_bcast_to_tok(ctx, src_f32, dst_tok, wait_sem, wait_val):
    id_sb, pEb = ctx.buf["id_sb"], ctx.buf["pEb"]
    tmpb = ctx.buf["scr2T"][:, 0:T]
    S, V = ctx.sem, ctx.inc

    a_b = V("ac")
    pe_t = [V("pe") for _ in range(8)]
    v_t = [V("dv") for _ in range(8)]

    @ctx.on("scalar")
    def _(a):
        a.wait_ge(S[wait_sem], wait_val)
        a.activation(tmpb[:], src_f32[:], AF.Identity).then_inc(S["ac"], 1)

    @ctx.on("tensor")
    def _(t):
        t.wait_ge(S["ac"], a_b)
        for tc in range(8):
            if tc >= 2:
                t.wait_ge(S["dv"], v_t[tc - 2])
            t.transpose(pEb[:, (tc % 2) * T: (tc % 2) * T + 128],
                        tmpb[:, tc * 128:(tc + 1) * 128],
                        id_sb[:]).then_inc(S["pe"], 1)

    @ctx.on("vector")
    def _(v):
        for tc in range(8):
            v.wait_ge(S["pe"], pe_t[tc])
            v.tensor_copy(dst_tok[:, tc:tc + 1],
                          pEb[:, (tc % 2) * T:(tc % 2) * T + 1]).then_inc(S["dv"], 1)

    return v_t



def emit_l1_ffn(ctx, wrT_h, aT_h, bT_h):
    xcen, r = ctx.buf["xcen"], ctx.buf["r"]
    center_dv, r_ac = ctx.mark["center_dv"], ctx.mark["r_ac"]
    id_sb = ctx.buf["id_sb"]
    pA2, pB2, pC2, pEb = ctx.buf["pA2"], ctx.buf["pB2"], ctx.buf["pC2"], ctx.buf["pEb"]
    S, V = ctx.sem, ctx.inc

    wr_sb = ctx.ssbuf("wr_sb", [128, 32 * 4], BF16)
    aT_sb = ctx.ssbuf("aT_sb", [128, 32 * 64], BF16)
    bT_sb = ctx.ssbuf("bT_sb", [64, D], BF16)
    lg = ctx.ssbuf("lg_sb", [128, 8 * 4], F32)
    et = ctx.ssbuf("et_sb", [128, 8 * 4], F32)
    sel = ctx.ssbuf("sel_sb", [128, 8 * 4], F32)
    m1 = ctx.ssbuf("m1_sb", [128, 8], F32)
    nm1 = ctx.ssbuf("nm1_sb", [128, 8], F32)
    m2 = ctx.ssbuf("m2_sb", [128, 8], F32)
    gsc = ctx.ssbuf("gsc_sb", [128, 8], F32)
    gates = ctx.ssbuf("gates_sb", [128, 8 * 4], BF16)
    gatesT = ctx.ssbuf("gatesT_sb", [4, T], BF16)
    ghat = ctx.ssbuf("ghat_sb", [64, T], BF16)
    midT = ctx.ssbuf("midT_sb", [64, T], BF16)
    gmidT = ctx.ssbuf("gmidT_sb", [64, T], BF16)
    rtok = ctx.ssbuf("rtok_sb", [128, 8], F32)

    sd_tot = ctx.stage_load_emit([
        (wr_sb[:].rearrange("p (a f) -> p a f", f=4),
         wrT_h.ap().rearrange("(a p) f -> p a f", p=128)),
        (aT_sb[:].rearrange("p (a f) -> p a f", f=64),
         aT_h.ap().rearrange("(a p) f -> p a f", p=128)),
        (bT_sb[:], bT_h.ap())])

    v_rtok = emit_bcast_to_tok(ctx, r, rtok, "ac", r_ac)

    lg_pe = []
    v_lg = []
    for tc in range(8):
        lg_pe.append(V("pe"))
    gate_dv = []
    for tc in range(8):
        v_lg.append(V("dv"))
        _v = {}
        for k in ["m1", "nm1", "sel1", "l2", "m2", "d21", "p1", "rv", "sel", "tt", "g"]:
            _v[k] = V("dv")
        _a = {"e21": V("ac"), "et": V("ac")}
        gate_dv.append((_v, _a))

    @ctx.on("tensor")
    def _(t):
        t.wait_ge(S["sd"], sd_tot)
        t.wait_ge(S["dv"], center_dv)
        for tc in range(8):
            if tc >= 2:
                t.wait_ge(S["dv"], v_lg[tc - 2])
            psl = (pB2 if tc % 2 == 0 else pC2)[:, 0:4]
            for dc in range(32):
                ins = t.matmul(psl,
                               xcen[:, dc * T + tc * 128: dc * T + tc * 128 + 128],
                               wr_sb[:, dc * 4: dc * 4 + 4],
                               start=(dc == 0), stop=(dc == 31))
            ins.then_inc(S["pe"], 1)

    for tc in range(8):
        sl4 = slice(tc * 4, tc * 4 + 4)
        sl1 = slice(tc, tc + 1)
        _v, _a = gate_dv[tc]

        @ctx.on("vector")
        def _(v, tc=tc, sl4=sl4, sl1=sl1, _v=_v, _a=_a):
            v.wait_ge(S["pe"], lg_pe[tc])
            v.wait_ge(S["dv"], v_rtok[tc])
            psl = (pB2 if tc % 2 == 0 else pC2)[:, 0:4]
            v.tensor_scalar(lg[:, sl4], psl, rtok[:, sl1], None, ALU.mult).then_inc(S["dv"], 1)
            v.wait_ge(S["dv"], v_lg[tc])
            v.reduce_max(m1[:, sl1].unsqueeze(2), lg[:, sl4].unsqueeze(1), axis=AX.X).then_inc(S["dv"], 1)
            v.wait_ge(S["dv"], _v["m1"])
            v.tensor_scalar(nm1[:, sl1], m1[:, sl1], -1.0, None, ALU.mult).then_inc(S["dv"], 1)
            v.wait_ge(S["dv"], _v["nm1"])
            v.tensor_tensor(sel[:, sl4].unsqueeze(1), lg[:, sl4].unsqueeze(1),
                            m1[:, sl1].unsqueeze(2).broadcast_to((128, 1, 4)),
                            ALU.is_ge).then_inc(S["dv"], 1)
            v.wait_ge(S["dv"], _v["sel1"])
            v.scalar_tensor_tensor(et[:, sl4], sel[:, sl4], -1e30, lg[:, sl4],
                                   ALU.mult, ALU.add).then_inc(S["dv"], 1)
            v.wait_ge(S["dv"], _v["l2"])
            v.reduce_max(m2[:, sl1].unsqueeze(2), et[:, sl4].unsqueeze(1), axis=AX.X).then_inc(S["dv"], 1)
            v.wait_ge(S["dv"], _v["m2"])
            v.tensor_tensor(gsc[:, sl1], m2[:, sl1], m1[:, sl1], ALU.subtract).then_inc(S["dv"], 1)
            v.wait_ge(S["ac"], _a["e21"])
            v.tensor_scalar(gsc[:, sl1], gsc[:, sl1], 1.0, None, ALU.add).then_inc(S["dv"], 1)
            v.wait_ge(S["dv"], _v["p1"])
            v.reciprocal(gsc[:, sl1], gsc[:, sl1]).then_inc(S["dv"], 1)
            v.wait_ge(S["dv"], _v["rv"])
            v.tensor_tensor(sel[:, sl4].unsqueeze(1), lg[:, sl4].unsqueeze(1),
                            m2[:, sl1].unsqueeze(2).broadcast_to((128, 1, 4)),
                            ALU.is_ge).then_inc(S["dv"], 1)
            v.wait_ge(S["ac"], _a["et"])
            v.wait_ge(S["dv"], _v["sel"])
            v.tensor_tensor(et[:, sl4], et[:, sl4], sel[:, sl4], ALU.mult).then_inc(S["dv"], 1)
            v.wait_ge(S["dv"], _v["tt"])
            v.tensor_scalar(gates[:, sl4], et[:, sl4], gsc[:, sl1], None, ALU.mult).then_inc(S["dv"], 1)

        @ctx.on("scalar")
        def _(a, tc=tc, sl4=sl4, sl1=sl1, _v=_v, _a=_a):
            a.wait_ge(S["dv"], _v["d21"])
            a.activation(gsc[:, sl1], gsc[:, sl1], AF.Exp).then_inc(S["ac"], 1)
            a.wait_ge(S["dv"], _v["rv"])
            a.activation(et[:, sl4], lg[:, sl4], AF.Exp, bias=nm1[:, sl1]).then_inc(S["ac"], 1)

    pe_mid = V("pe")

    @ctx.on("tensor")
    def _(t):
        t.wait_ge(S["sd"], sd_tot)
        t.wait_ge(S["dv"], center_dv)
        for dc in range(32):
            for half in range(2):
                ins = t.matmul(pA2[0:64, half * 512: half * 512 + 512],
                               aT_sb[:, dc * 64:(dc + 1) * 64],
                               xcen[:, dc * T + half * 512: dc * T + half * 512 + 512],
                               start=(dc == 0), stop=(dc == 31))
        ins.then_inc(S["pe"], 1)

    v_mid = V("dv")

    @ctx.on("vector")
    def _(v):
        v.wait_ge(S["pe"], pe_mid)
        v.wait_ge(S["ac"], r_ac)
        v.tensor_tensor(midT[:], pA2[0:64, 0:T], r[0:64, :], ALU.mult).then_inc(S["dv"], 1)

    gT_pe = [V("pe") for _ in range(8)]
    v_gT = [V("dv") for _ in range(8)]

    @ctx.on("tensor")
    def _(t):
        for tc in range(8):
            t.wait_ge(S["dv"], gate_dv[tc][0]["g"])
            if tc >= 2:
                t.wait_ge(S["dv"], v_gT[tc - 2])
            t.transpose(pEb[0:4, (tc % 2) * T:(tc % 2) * T + 128],
                        gates[:, tc * 4:(tc + 1) * 4], id_sb[:]).then_inc(S["pe"], 1)

    @ctx.on("vector")
    def _(v):
        for tc in range(8):
            v.wait_ge(S["pe"], gT_pe[tc])
            v.tensor_copy(gatesT[:, tc * 128:(tc + 1) * 128],
                          pEb[0:4, (tc % 2) * T:(tc % 2) * T + 128]).then_inc(S["dv"], 1)

    pe_ghat = V("pe")

    @ctx.on("tensor")
    def _(t):
        for tc in range(8):
            t.wait_ge(S["dv"], v_gT[tc])
        for half in range(2):
            ins = t.matmul(pC2[0:64, half * 512: half * 512 + 512],
                           ctx.buf["rep_sb"][:], gatesT[:, half * 512: half * 512 + 512],
                           start=True, stop=True)
        ins.then_inc(S["pe"], 1)

    a_ghat = V("ac")

    @ctx.on("scalar")
    def _(a):
        a.wait_ge(S["pe"], pe_ghat)
        a.activation(ghat[:], pC2[0:64, 0:T], AF.Identity).then_inc(S["ac"], 1)

    v_gmid = V("dv")

    @ctx.on("vector")
    def _(v):
        v.wait_ge(S["ac"], a_ghat)
        v.wait_ge(S["dv"], v_mid)
        v.tensor_tensor(gmidT[:], midT[:], ghat[:], ALU.mult).then_inc(S["dv"], 1)

    res_dv = []
    for oc in range(32):
        pe_oc = V("pe")
        v_oc = V("dv")

        @ctx.on("tensor")
        def _(t, oc=oc):
            if oc == 0:
                t.wait_ge(S["dv"], v_gmid)
            if oc >= 2:
                t.wait_ge(S["dv"], res_dv[oc - 2])
            ps = pA2 if oc % 2 == 0 else pC2
            for half in range(2):
                ins = t.matmul(ps[:, half * 512: half * 512 + 512],
                               bT_sb[:, oc * 128:(oc + 1) * 128],
                               gmidT[:, half * 512: half * 512 + 512],
                               start=True, stop=True)
            ins.then_inc(S["pe"], 1)

        @ctx.on("vector")
        def _(v, oc=oc, pe_oc=pe_oc):
            v.wait_ge(S["pe"], pe_oc)
            ps = pA2 if oc % 2 == 0 else pC2
            v.tensor_tensor(xcen[:, oc * T:(oc + 1) * T], xcen[:, oc * T:(oc + 1) * T],
                            ps[:, 0:T], ALU.add).then_inc(S["dv"], 1)
        res_dv.append(v_oc)

    ctx.mark["xcen_done"] = res_dv[-1]






def emit_l0_attn(ctx, wq_h, wo_h, wk_h, wv_h, xhalo_h, ck_h, sk_h,
                 denm_h, alA_h, alB_h, hsel_h):
    nc = ctx.nc
    xcen, r = ctx.buf["xcen"], ctx.buf["r"]
    ones, id_sb = ctx.buf["ones128"], ctx.buf["id_sb"]
    pA2, pB2, pC2, pEb = ctx.buf["pA2"], ctx.buf["pB2"], ctx.buf["pC2"], ctx.buf["pEb"]
    wstr = ctx.buf["wstr"]
    prott = ctx.buf["prott_sb"]
    tmpT = ctx.buf["tmpT"]
    S, V = ctx.sem, ctx.inc
    center_dv, r_ac = ctx.mark["center_dv"], ctx.mark["r_ac"]

    QA = ctx.ssbuf("QA_sb", [128, 32 * T], BF16)
    xhal = ctx.ssbuf("xhal_sb", [128, 2 * 256], BF16)
    hsq = ctx.ssbuf("hsq_sb", [128, 2 * 256], BF16)
    rh = ctx.ssbuf("rh_sb", [128, 256], F32)
    dmuh = ctx.ssbuf("dmuh_sb", [128, 256], BF16)
    KxT = ctx.ssbuf("KxT_sb", [128, 12 * 128], BF16)
    kraw = ctx.ssbuf("kraw_sb", [128, 12 * 128], BF16)
    Vx = kraw
    vraw = ctx.ssbuf("vraw_sb", [128, 1280], BF16)
    cosk = ctx.ssbuf("cosk_sb", [128, 1536], BF16)
    sink = ctx.ssbuf("sink_sb", [128, 1536], BF16)
    denm = ctx.ssbuf("denm_sb", [128, 12 * 128], BF16)
    alA = ctx.ssbuf("alA_sb", [128, 4 * 256], BF16)
    alB = ctx.ssbuf("alB_sb", [128, 4 * 256], BF16)
    hsel = ctx.ssbuf("hsel_sb", [128, 2], F32)
    qtmp = ctx.ssbuf("qtmp_sb", [128, 2 * T], BF16)
    kcs = qtmp[:, 0:1536]
    P0 = ctx.buf["scr2T"]
    rbw = ctx.buf["oscr"]
    batmp = hsq
    cosq = cosk[:, 256:1280]
    sinq = sink[:, 256:1280]

    ctx.static_load(cosk[:], ck_h.ap())
    ctx.static_load(sink[:], sk_h.ap())
    ctx.static_load(denm[:], denm_h.ap())
    ctx.static_load(alA[:], alA_h.ap())
    ctx.static_load(alB[:], alB_h.ap())
    ctx.static_load(hsel[:], hsel_h.ap())

    wk_ld = V("ga", 16)
    wv_ld = V("gb", 16)

    @ctx.on("gpsimd")
    def _(g):
        g.dma_start(out=wstr[:, 0:4096].rearrange("p (a f) -> p a f", f=128),
                    in_=wk_h.ap()[0]).then_inc(S["ga"], 16)
        g.dma_start(out=wstr[:, 4096:8192].rearrange("p (a f) -> p a f", f=128),
                    in_=wv_h.ap()[0]).then_inc(S["gb"], 16)

    h1_ld = [V("ha" if i % 2 == 0 else "hb", 16) for i in range(32)]
    hsq_ac = [V("ac") for _ in range(32)]
    hst_pe = [V("pe") for _ in range(32)]

    @ctx.on("gpsimd")
    def _(g):
        for i in range(32):
            if i >= 2:
                g.wait_ge(S["pe"], hst_pe[i - 2])
            g.dma_start(out=xhal[:, (i % 2) * 256:(i % 2) * 256 + 256],
                        in_=xhalo_h.ap()[i * 128:(i + 1) * 128, :]
                        ).then_inc(S["ha" if i % 2 == 0 else "hb"], 16)

    @ctx.on("scalar")
    def _(a):
        for i in range(32):
            a.wait_ge(S["ha" if i % 2 == 0 else "hb"], h1_ld[i])
            if i >= 2:
                a.wait_ge(S["pe"], hst_pe[i - 2])
            a.activation(hsq[:, (i % 2) * 256:(i % 2) * 256 + 256],
                         xhal[:, (i % 2) * 256:(i % 2) * 256 + 256],
                         AF.Square).then_inc(S["ac"], 1)

    @ctx.on("tensor")
    def _(t):
        for i in range(32):
            t.wait_ge(S["ac"], hsq_ac[i])
            t.matmul(pC2[:, 0:256], ones[:], xhal[:, (i % 2) * 256:(i % 2) * 256 + 256],
                     start=(i == 0), stop=(i == 31))
            ins = t.matmul(pC2[:, 512:768], ones[:],
                           hsq[:, (i % 2) * 256:(i % 2) * 256 + 256],
                           start=(i == 0), stop=(i == 31))
            ins.then_inc(S["pe"], 1)

    vh = {k: V("dv") for k in ["dmu", "ex2", "d2", "var"]}
    ah = {k: V("ac") for k in ["dmub", "ln", "r"]}

    @ctx.on("vector")
    def _(v):
        v.wait_ge(S["pe"], hst_pe[31])
        v.tensor_scalar(tmpT[:, 0:256], pC2[:, 0:256], 1.0 / D, None,
                        ALU.mult).then_inc(S["dv"], 1)
        v.tensor_scalar(rh[:], pC2[:, 512:768], 1.0 / D, None, ALU.mult).then_inc(S["dv"], 1)
        v.wait_ge(S["dv"], vh["ex2"])
        v.tensor_tensor(tmpT[:, 256:512], tmpT[:, 0:256], tmpT[:, 0:256],
                        ALU.mult).then_inc(S["dv"], 1)
        v.wait_ge(S["dv"], vh["d2"])
        v.tensor_tensor(rh[:], rh[:], tmpT[:, 256:512], ALU.subtract).then_inc(S["dv"], 1)

    @ctx.on("scalar")
    def _(a):
        a.wait_ge(S["dv"], vh["dmu"])
        a.activation(dmuh[:], tmpT[:, 0:256], AF.Identity).then_inc(S["ac"], 1)
        a.wait_ge(S["dv"], vh["var"])
        a.activation(tmpT[:, 256:512], rh[:], AF.Ln,
                     bias=ctx.buf["eps_sb"][:, 0:1]).then_inc(S["ac"], 1)
        a.wait_ge(S["ac"], ah["ln"])
        a.activation(rh[:], tmpT[:, 256:512], AF.Exp, scale=-0.5).then_inc(S["ac"], 1)

    kvo = V("pe")
    kvv = V("pe")

    @ctx.on("tensor")
    def _(t):
        t.wait_ge(S["dv"], center_dv)
        t.wait_ge(S["ga"], wk_ld)
        for dc in range(32):
            for half in range(2):
                ins = t.matmul(pA2[:, half * 512: half * 512 + 512],
                               wstr[:, dc * 128:(dc + 1) * 128],
                               xcen[:, dc * T + half * 512: dc * T + half * 512 + 512],
                               start=(dc == 0), stop=(dc == 31))
        ins.then_inc(S["pe"], 1)
        t.wait_ge(S["gb"], wv_ld)
        t.wait_ge(S["dv"], vh["ex2"])
        for dc in range(32):
            for half in range(2):
                ins = t.matmul(pC2[:, half * 512: half * 512 + 512],
                               wstr[:, 4096 + dc * 128: 4096 + (dc + 1) * 128],
                               xcen[:, dc * T + half * 512: dc * T + half * 512 + 512],
                               start=(dc == 0), stop=(dc == 31))
        ins.then_inc(S["pe"], 1)

    h2_ld = [V("ha" if i % 2 == 0 else "hb", 16) for i in range(32)]
    h2_cen = [V("dv") for _ in range(32)]
    h2_pe = [V("pe") for _ in range(32)]

    @ctx.on("gpsimd")
    def _(g):
        for i in range(32):
            if i < 2:
                g.wait_ge(S["pe"], hst_pe[30 + i])
            else:
                g.wait_ge(S["pe"], h2_pe[i - 2])
            g.dma_start(out=xhal[:, (i % 2) * 256:(i % 2) * 256 + 256],
                        in_=xhalo_h.ap()[i * 128:(i + 1) * 128, :]
                        ).then_inc(S["ha" if i % 2 == 0 else "hb"], 16)

    @ctx.on("vector")
    def _(v):
        v.wait_ge(S["ac"], ah["dmub"])
        for i in range(32):
            v.wait_ge(S["ha" if i % 2 == 0 else "hb"], h2_ld[i])
            v.tensor_tensor(hsq[:, (i % 2) * 256:(i % 2) * 256 + 256],
                            xhal[:, (i % 2) * 256:(i % 2) * 256 + 256],
                            dmuh[:], ALU.subtract).then_inc(S["dv"], 1)

    @ctx.on("tensor")
    def _(t):
        for i in range(32):
            t.wait_ge(S["dv"], h2_cen[i])
            t.matmul(pB2[:, 0:256], wstr[:, i * 128:(i + 1) * 128],
                     hsq[:, (i % 2) * 256:(i % 2) * 256 + 256],
                     start=(i == 0), stop=(i == 31))
            ins = t.matmul(pB2[:, 512:768], wstr[:, 4096 + i * 128: 4096 + (i + 1) * 128],
                           hsq[:, (i % 2) * 256:(i % 2) * 256 + 256],
                           start=(i == 0), stop=(i == 31))
            ins.then_inc(S["pe"], 1)

    halo_pe = h2_pe[31]

    kv_dv = {k: V("dv") for k in ["ko", "vo", "khs", "klo", "khi", "vhs"]}

    @ctx.on("vector")
    def _(v):
        v.wait_ge(S["ac"], r_ac)
        v.wait_ge(S["ac"], ah["r"])
        v.wait_ge(S["pe"], kvo)
        v.tensor_tensor(kraw[:, 256:1280], pA2[:, 0:T], r[:], ALU.mult).then_inc(S["dv"], 1)
        v.wait_ge(S["pe"], kvv)
        v.tensor_tensor(vraw[:, 0:T], pC2[:, 0:T], r[:], ALU.mult).then_inc(S["dv"], 1)
        v.wait_ge(S["pe"], halo_pe)
        v.tensor_tensor(rbw[:, 0:256], pB2[:, 0:256], rh[:], ALU.mult).then_inc(S["dv"], 1)
        v.wait_ge(S["dv"], kv_dv["khs"])
        v.tensor_scalar(kraw[:, 0:256], rbw[:, 0:256], hsel[:, 0:1], None,
                        ALU.mult).then_inc(S["dv"], 1)
        v.tensor_scalar(kraw[:, 1280:1536], rbw[:, 0:256], hsel[:, 1:2], None,
                        ALU.mult).then_inc(S["dv"], 1)
        v.tensor_tensor(vraw[:, T:1280], pB2[:, 512:768], rh[:], ALU.mult).then_inc(S["dv"], 1)

    krot = [V("pe") for _ in range(3)]
    kct = [V("po") for _ in range(3)]
    kt1 = []
    kx = []
    for c in range(3):
        kt1.append(V("dv"))
        kx.append(V("dv"))

    @ctx.on("tensor")
    def _(t):
        t.wait_ge(S["dv"], kv_dv["vhs"])
        for c in range(3):
            if c == 2:
                t.wait_ge(S["dv"], kt1[0])
            t.matmul(pB2[:, (c % 2) * 512:(c % 2) * 512 + 512], prott[:],
                     kraw[:, c * 512:(c + 1) * 512], start=True,
                     stop=True).then_inc(S["pe"], 1)

    @ctx.on("gpsimd")
    def _(g):
        g.wait_ge(S["dv"], kv_dv["vhs"])
        for c in range(3):
            g.tensor_tensor(kcs[:, c * 512:(c + 1) * 512], kraw[:, c * 512:(c + 1) * 512],
                            cosk[:, c * 512:(c + 1) * 512], ALU.mult).then_inc(S["po"], 1)

    @ctx.on("vector")
    def _(v):
        for c in range(3):
            v.wait_ge(S["pe"], krot[c])
            v.tensor_tensor(P0[:, (c % 2) * 512:(c % 2) * 512 + 512],
                            pB2[:, (c % 2) * 512:(c % 2) * 512 + 512],
                            sink[:, c * 512:(c + 1) * 512], ALU.mult).then_inc(S["dv"], 1)
            v.wait_ge(S["po"], kct[c])
            v.wait_ge(S["dv"], kt1[c])
            v.tensor_tensor(KxT[:, c * 512:(c + 1) * 512],
                            P0[:, (c % 2) * 512:(c % 2) * 512 + 512],
                            kcs[:, c * 512:(c + 1) * 512], ALU.add).then_inc(S["dv"], 1)

    vt_pe = [V("pe") for _ in range(12)]
    vt_ac = [V("ac") for _ in range(12)]
    vsel = [V("dv") for _ in range(4)]

    @ctx.on("tensor")
    def _(t):
        t.wait_ge(S["dv"], kx[2])
        for j in range(12):
            if 2 <= j < 10:
                src = vraw[:, (j - 2) * 128:(j - 1) * 128]
            else:
                src = vraw[:, T + (j % 2) * 128: T + (j % 2) * 128 + 128]
            if j >= 2:
                t.wait_ge(S["ac"], vt_ac[j - 2])
            t.transpose(pEb[:, (j % 2) * T:(j % 2) * T + 128], src,
                        id_sb[:]).then_inc(S["pe"], 1)

    @ctx.on("scalar")
    def _(a):
        for j in range(12):
            a.wait_ge(S["pe"], vt_pe[j])
            if j < 2:
                a.wait_ge(S["dv"], kx[2])
            a.activation(Vx[:, j * 128:(j + 1) * 128],
                         pEb[:, (j % 2) * T:(j % 2) * T + 128],
                         AF.Identity).then_inc(S["ac"], 1)

    @ctx.on("vector")
    def _(v):
        for n, j in enumerate([0, 1, 10, 11]):
            v.wait_ge(S["ac"], vt_ac[j])
            v.tensor_scalar(Vx[:, j * 128:(j + 1) * 128], Vx[:, j * 128:(j + 1) * 128],
                            hsel[:, 0:1] if j < 2 else hsel[:, 1:2], None,
                            ALU.mult).then_inc(S["dv"], 1)

    wq_ld = [V(("ga" if i % 2 == 0 else "gb"), 16) for i in range(32)]
    hm_list = []
    for head in range(32):
        hm_list.append({"qpe": V("pe"), "rot1": V("pe"), "rot2": V("pe")})
    for head in range(32):
        hm = hm_list[head]
        hm["qev"] = V("dv")
        hm["t1a"] = V("dv")
        hm["t1b"] = V("dv")
        hm["qra"] = V("dv")
        hm["qrb"] = V("dv")
    for head in range(32):
        hm_list[head]["t2"] = V("po")

    @ctx.on("gpsimd")
    def _(g):
        for head in range(-2, 32):
            if head >= 0:
                hm = hm_list[head]
                g.wait_ge(S["dv"], hm["qev"])
                if head >= 1:
                    g.wait_ge(S["dv"], hm_list[head - 1]["qrb"])
                g.tensor_tensor(P0[:, 512:1024],
                                qtmp[:, (head % 2) * T:(head % 2) * T + 512],
                                cosq[:, 0:512], ALU.mult)
                g.tensor_tensor(P0[:, 1536:2048],
                                qtmp[:, (head % 2) * T + 512:(head % 2) * T + T],
                                cosq[:, 512:1024], ALU.mult).then_inc(S["po"], 1)
            mc = head + 2
            if 0 <= mc < 32:
                if mc == 0:
                    g.wait_ge(S["pe"], kvo)
                    g.wait_ge(S["pe"], halo_pe)
                elif mc == 1:
                    g.wait_ge(S["pe"], kvv)
                    g.wait_ge(S["pe"], halo_pe)
                else:
                    g.wait_ge(S["pe"], hm_list[mc - 2]["qpe"])
                g.dma_start(out=wstr[:, (mc % 2) * 4096:(mc % 2) * 4096 + 4096]
                            .rearrange("p (a f) -> p a f", f=128),
                            in_=wq_h.ap()[mc]).then_inc(S["ga" if mc % 2 == 0 else "gb"], 16)

    @ctx.on("tensor")
    def _(t):
        for head in range(32):
            hm = hm_list[head]
            t.wait_ge(S["ga" if head % 2 == 0 else "gb"], wq_ld[head])
            if head == 0:
                t.wait_ge(S["dv"], kv_dv["klo"])
            elif head == 1:
                t.wait_ge(S["dv"], kv_dv["vo"])
            else:
                t.wait_ge(S["dv"], hm_list[head - 2]["qev"])
            ps = pA2 if head % 2 == 0 else pC2
            for dc in range(32):
                for half in range(2):
                    ins = t.matmul(ps[:, half * 512: half * 512 + 512],
                                   wstr[:, (head % 2) * 4096 + dc * 128:(head % 2) * 4096 + (dc + 1) * 128],
                                   xcen[:, dc * T + half * 512: dc * T + half * 512 + 512],
                                   start=(dc == 0), stop=(dc == 31))
            ins.then_inc(S["pe"], 1)
            t.wait_ge(S["dv"], hm["qev"])
            if head == 0:
                t.wait_ge(S["ac"], vt_ac[11])
                t.wait_ge(S["dv"], kx[2])
            else:
                t.wait_ge(S["dv"], hm_list[head - 1]["t1a"])
            t.matmul(pB2[:, 0:512], prott[:],
                     qtmp[:, (head % 2) * T:(head % 2) * T + 512],
                     start=True, stop=True).then_inc(S["pe"], 1)
            if head >= 1:
                t.wait_ge(S["dv"], hm_list[head - 1]["t1b"])
            t.matmul(pB2[:, 512:1024], prott[:],
                     qtmp[:, (head % 2) * T + 512:(head % 2) * T + T],
                     start=True, stop=True).then_inc(S["pe"], 1)

    @ctx.on("vector")
    def _(v):
        for head in range(32):
            hm = hm_list[head]
            v.wait_ge(S["pe"], hm["qpe"])
            if head >= 2:
                v.wait_ge(S["po"], hm_list[head - 2]["t2"])
            else:
                v.wait_ge(S["po"], kct[2])
                v.wait_ge(S["dv"], kx[2])
            ps = pA2 if head % 2 == 0 else pC2
            v.tensor_tensor(qtmp[:, (head % 2) * T:(head % 2) * T + T],
                            ps[:, 0:T], r[:], ALU.mult).then_inc(S["dv"], 1)
            v.wait_ge(S["pe"], hm["rot1"])
            if head >= 1:
                v.wait_ge(S["dv"], hm_list[head - 1]["qra"])
            v.tensor_tensor(P0[:, 0:512], pB2[:, 0:512],
                            sinq[:, 0:512], ALU.mult).then_inc(S["dv"], 1)
            v.wait_ge(S["pe"], hm["rot2"])
            if head >= 1:
                v.wait_ge(S["dv"], hm_list[head - 1]["qrb"])
            v.tensor_tensor(P0[:, T:T + 512], pB2[:, 512:1024],
                            sinq[:, 512:1024], ALU.mult).then_inc(S["dv"], 1)
            v.wait_ge(S["po"], hm["t2"])
            v.wait_ge(S["dv"], hm["t1b"])
            v.tensor_tensor(QA[:, head * T: head * T + 512],
                            P0[:, 0:512], P0[:, 512:1024], ALU.add).then_inc(S["dv"], 1)
            v.wait_ge(S["dv"], hm["qra"])
            v.tensor_tensor(QA[:, head * T + 512: head * T + T],
                            P0[:, T:T + 512], P0[:, 1536:2048], ALU.add).then_inc(S["dv"], 1)

    qa_done = hm_list[31]["qrb"]

    wo_ld = [V(("ga" if i % 2 == 0 else "gb"), 16) for i in range(2)]

    @ctx.on("gpsimd")
    def _(g):
        for oc in range(2):
            g.wait_ge(S["pe"], hm_list[30 + oc]["qpe"])
            g.dma_start(out=wstr[:, (oc % 2) * 4096:(oc % 2) * 4096 + 4096]
                        .rearrange("p (a f) -> p a f", f=128),
                        in_=wo_h.ap()[oc]).then_inc(S["ga" if oc % 2 == 0 else "gb"], 16)

    NW = 32 * 4 * 8
    spe = [None] * NW
    dpe = [None] * NW
    sig = [None] * NW
    for it in range(NW + 1):
        if it < NW:
            spe[it] = V("pe")
        if it >= 1:
            dpe[it - 1] = V("pe")
    wmk = [[{} for _ in range(4)] for _ in range(32)]
    for head in range(32):
        for blk in range(4):
            bm = wmk[head][blk]
            bm["rb1"] = V("dv")
            bm["rb2"] = V("dv")
            bm["w1"] = V("dv")
            bm["w2"] = V("dv")
            bm["att"] = V("dv")
            bm["po1"] = V("po")
            bm["po2"] = V("po")
    for u in range(NW):
        sig[u] = V("ac")

    def _widx(u):
        head = u // 32
        blk = (u // 8) % 4
        kidx = u % 8
        return head, blk, kidx

    @ctx.on("tensor")
    def _(t):
        t.wait_ge(S["dv"], qa_done)
        for it in range(NW + 1):
            if it < NW:
                u = it
                head, blk, kidx = _widx(u)
                wsel, kc = kidx // 4, kidx % 4
                c = 2 * blk + 2 * wsel + kc
                if u >= 2:
                    t.wait_ge(S["ac"], sig[u - 2])
                t.matmul(pA2[:, (u % 2) * 512:(u % 2) * 512 + 256],
                         KxT[:, c * 128:(c + 1) * 128],
                         QA[:, head * T + blk * 256: head * T + blk * 256 + 256],
                         start=True, stop=True).then_inc(S["pe"], 1)
            if it >= 1:
                u = it - 1
                head, blk, kidx = _widx(u)
                wsel, kc = kidx // 4, kidx % 4
                c = 2 * blk + 2 * wsel + kc
                t.wait_ge(S["ac"], sig[u])
                if kidx == 0 and u >= 8:
                    t.wait_ge(S["dv"], wmk[(u - 8) // 32][((u - 8) // 8) % 4]["att"])
                t.matmul(pB2[:, wsel * 512: wsel * 512 + 256],
                         denm[:, c * 128:(c + 1) * 128],
                         P0[:, (u % 2) * T:(u % 2) * T + 256],
                         start=(kc == 0), stop=(kc == 3))
                t.matmul(pC2[:, wsel * 512: wsel * 512 + 256],
                         Vx[:, c * 128:(c + 1) * 128],
                         P0[:, (u % 2) * T:(u % 2) * T + 256],
                         start=(kc == 0), stop=(kc == 3)).then_inc(S["pe"], 1)

    @ctx.on("scalar")
    def _(a):
        for u in range(NW):
            a.wait_ge(S["pe"], spe[u])
            if u >= 2:
                a.wait_ge(S["pe"], dpe[u - 2])
            elif u == 0:
                a.wait_ge(S["dv"], qa_done)
            a.activation(P0[:, (u % 2) * T:(u % 2) * T + 256],
                         pA2[:, (u % 2) * 512:(u % 2) * 512 + 256],
                         AF.Sigmoid, scale=SCALE128).then_inc(S["ac"], 1)

    @ctx.on("vector")
    def _(v):
        for head in range(32):
            for blk in range(4):
                bm = wmk[head][blk]
                base = (head * 4 + blk) * 8
                v.wait_ge(S["pe"], dpe[base + 3])
                v.reciprocal(rbw[:, 0:256], pB2[:, 0:256]).then_inc(S["dv"], 1)
                v.wait_ge(S["pe"], dpe[base + 7])
                v.reciprocal(rbw[:, 256:512], pB2[:, 512:768]).then_inc(S["dv"], 1)
                v.wait_ge(S["po"], bm["po1"])
                v.tensor_tensor(batmp[:, 0:256], pC2[:, 0:256],
                                rbw[:, 512:768], ALU.mult).then_inc(S["dv"], 1)
                v.wait_ge(S["po"], bm["po2"])
                v.tensor_tensor(batmp[:, 256:512], pC2[:, 512:768],
                                rbw[:, 768:1024], ALU.mult).then_inc(S["dv"], 1)
                v.wait_ge(S["dv"], bm["w2"])
                v.tensor_tensor(QA[:, head * T + blk * 256: head * T + blk * 256 + 256],
                                batmp[:, 0:256], batmp[:, 256:512],
                                ALU.add).then_inc(S["dv"], 1)

    @ctx.on("gpsimd")
    def _(g):
        for head in range(32):
            for blk in range(4):
                bm = wmk[head][blk]
                g.wait_ge(S["dv"], bm["rb1"])
                g.tensor_tensor(rbw[:, 512:768], rbw[:, 0:256],
                                alB[:, blk * 256:(blk + 1) * 256], ALU.mult).then_inc(S["po"], 1)
                g.wait_ge(S["dv"], bm["rb2"])
                g.tensor_tensor(rbw[:, 768:1024], rbw[:, 256:512],
                                alA[:, blk * 256:(blk + 1) * 256], ALU.mult).then_inc(S["po"], 1)

    att_done = wmk[31][3]["att"]

    wo_ld += [V(("ga" if i % 2 == 0 else "gb"), 16) for i in range(2, 32)]
    ope = [V("pe") for _ in range(32)]
    res_dv = [V("dv") for _ in range(32)]

    @ctx.on("gpsimd")
    def _(g):
        for oc in range(2, 32):
            g.wait_ge(S["pe"], ope[oc - 2])
            g.dma_start(out=wstr[:, (oc % 2) * 4096:(oc % 2) * 4096 + 4096]
                        .rearrange("p (a f) -> p a f", f=128),
                        in_=wo_h.ap()[oc]).then_inc(S["ga" if oc % 2 == 0 else "gb"], 16)

    @ctx.on("tensor")
    def _(t):
        t.wait_ge(S["dv"], att_done)
        for oc in range(32):
            t.wait_ge(S["ga" if oc % 2 == 0 else "gb"], wo_ld[oc])
            if oc >= 2:
                t.wait_ge(S["dv"], res_dv[oc - 2])
            ps = pA2 if oc % 2 == 0 else pC2
            for hd in range(32):
                for half in range(2):
                    ins = t.matmul(ps[:, half * 512: half * 512 + 512],
                                   wstr[:, (oc % 2) * 4096 + hd * 128:(oc % 2) * 4096 + (hd + 1) * 128],
                                   QA[:, hd * T + half * 512: hd * T + half * 512 + 512],
                                   start=(hd == 0), stop=(hd == 31))
            ins.then_inc(S["pe"], 1)

    @ctx.on("vector")
    def _(v):
        for oc in range(32):
            v.wait_ge(S["pe"], ope[oc])
            ps = pA2 if oc % 2 == 0 else pC2
            v.tensor_tensor(xcen[:, oc * T:(oc + 1) * T], xcen[:, oc * T:(oc + 1) * T],
                            ps[:, 0:T], ALU.add).then_inc(S["dv"], 1)

    ctx.mark["xcen_done"] = res_dv[-1]





def emit_l0_ffn(ctx, c1t_h, c2T_h, w1tt_h, w2tt_h, ffnacc_h):
    nc = ctx.nc
    xcen, r, rbf = ctx.buf["xcen"], ctx.buf["r"], ctx.buf["rbf"]
    ones, id_sb = ctx.buf["ones128"], ctx.buf["id_sb"]
    pA2, pB2, pC2, pEb = ctx.buf["pA2"], ctx.buf["pB2"], ctx.buf["pC2"], ctx.buf["pEb"]
    wstr = ctx.buf["wstr"]
    scr = ctx.buf["scr2T"]
    S, V = ctx.sem, ctx.inc
    center_dv, r_ac = ctx.mark["center_dv"], ctx.mark["r_ac"]

    c1t = ctx.ssbuf("c1t_sb", [128, 32 * 16], BF16)
    u_sb = ctx.ssbuf("u_sb", [16, T], BF16)
    c2s = ctx.ssbuf("c2s_sb", [16, 2 * 2048], BF16)
    hmg = ctx.ssbuf("hmg_sb", [128, 28 * T], BF16)
    zrb = ctx.ssbuf("zrb_sb", [128, 2 * T], BF16)
    mk = ctx.ssbuf("mk_sb", [128, 2 * T], BF16)
    mhat = ctx.ssbuf("mhat_sb", [128, 2 * T], BF16)
    itf = ctx.ssbuf("itf_sb", [128, 2 * T], BF16)
    m_tm = ctx.ssbuf("mtm_sb", [128, 8 * 64], F32)
    m_tb = ctx.ssbuf("mtb_sb", [128, 8 * 64], BF16)
    mT = ctx.ssbuf("mT_sb", [64, T], BF16)
    fstg = ctx.ssbuf("fstg_sb", [128, 2 * T], BF16)

    sd_tot = ctx.stage_load_emit([(c1t[:].rearrange("p (a f) -> p a f", f=16), c1t_h.ap())])

    NG = 4
    GI = 28

    u_pe = V("pe")
    u_dv = V("dv")
    m0_dv = V("dv")

    @ctx.on("tensor")
    def _(t):
        t.wait_ge(S["dv"], center_dv)
        t.wait_ge(S["sd"], sd_tot)
        for dc in range(32):
            for half in range(2):
                ins = t.matmul(pA2[0:16, half * 512: half * 512 + 512],
                               c1t[:, dc * 16:(dc + 1) * 16],
                               xcen[:, dc * T + half * 512: dc * T + half * 512 + 512],
                               start=(dc == 0), stop=(dc == 31))
        ins.then_inc(S["pe"], 1)

    @ctx.on("vector")
    def _(v):
        v.wait_ge(S["pe"], u_pe)
        v.tensor_copy(u_sb[:], pA2[0:16, 0:T]).then_inc(S["dv"], 1)
        v.memset(m_tm[:], 0.0).then_inc(S["dv"], 1)

    c2a_ld = [V("ka" if k % 2 == 0 else "kb", 16) for k in range(7)]
    it_pe = [[None] * 32 for _ in range(7)]
    it_ac = [[None] * 32 for _ in range(7)]
    it_dv = [[None] * 32 for _ in range(7)]
    for k in range(7):
        for z_ in range(32):
            it_pe[k][z_] = V("pe")
    for k in range(7):
        for z_ in range(32):
            it_ac[k][z_] = V("ac")
    for k in range(7):
        for z_ in range(32):
            it_dv[k][z_] = V("dv")

    @ctx.on("gpsimd")
    def _(g):
        for k in range(7):
            if k >= 2:
                g.wait_ge(S["pe"], it_pe[k - 2][31])
            g.dma_start(out=c2s[:, (k % 2) * 2048:(k % 2) * 2048 + 2048],
                        in_=c2T_h.ap()[:, k * 2048:(k + 1) * 2048]
                        ).then_inc(S["ka" if k % 2 == 0 else "kb"], 16)

    @ctx.on("tensor")
    def _(t):
        t.wait_ge(S["dv"], u_dv)
        for k in range(7):
            t.wait_ge(S["ka" if k % 2 == 0 else "kb"], c2a_ld[k])
            for tc in range(8):
                for q in range(4):
                    z_ = tc * 4 + q
                    if k * 32 + z_ >= 2:
                        pk, pz = divmod(k * 32 + z_ - 2, 32)
                        t.wait_ge(S["ac"], it_ac[pk][pz])
                    t.matmul(pC2[:, (z_ % 2) * 512:(z_ % 2) * 512 + 512],
                             u_sb[:, tc * 128:(tc + 1) * 128],
                             c2s[:, (k % 2) * 2048 + q * 512:(k % 2) * 2048 + q * 512 + 512],
                             start=True, stop=True).then_inc(S["pe"], 1)

    @ctx.on("scalar")
    def _(a):
        for k in range(7):
            for z_ in range(32):
                a.wait_ge(S["pe"], it_pe[k][z_])
                if k * 32 + z_ >= 4:
                    pk, pz = divmod(k * 32 + z_ - 4, 32)
                    a.wait_ge(S["dv"], it_dv[pk][pz])
                a.activation(scr[:, (z_ % 4) * 512:(z_ % 4) * 512 + 512],
                             pC2[:, (z_ % 2) * 512:(z_ % 2) * 512 + 512],
                             AF.Identity).then_inc(S["ac"], 1)

    @ctx.on("vector")
    def _(v):
        v.wait_ge(S["dv"], m0_dv)
        for k in range(7):
            for z_ in range(32):
                tc, q = divmod(z_, 4)
                v.wait_ge(S["ac"], it_ac[k][z_])
                blk0 = k * 4 + q
                v.reduce_max(m_tm[:, tc * 64 + (k * 4 + q) * 2: tc * 64 + (k * 4 + q) * 2 + 2].unsqueeze(2),
                             scr[:, (z_ % 4) * 512:(z_ % 4) * 512 + 512]
                             .rearrange("p (s j) -> p s j", s=2),
                             axis=AX.X).then_inc(S["dv"], 1)

    mcast_dv = V("dv")
    mt_pe = [V("pe") for _ in range(8)]
    mt_ac = [V("ac") for _ in range(8)]

    @ctx.on("vector")
    def _(v):
        v.wait_ge(S["dv"], it_dv[6][31])
        v.tensor_copy(m_tb[:], m_tm[:]).then_inc(S["dv"], 1)

    @ctx.on("tensor")
    def _(t):
        t.wait_ge(S["dv"], mcast_dv)
        for tc in range(8):
            if tc >= 2:
                t.wait_ge(S["ac"], mt_ac[tc - 2])
            t.transpose(pEb[0:64, (tc % 2) * T:(tc % 2) * T + 128],
                        m_tb[:, tc * 64:(tc + 1) * 64], id_sb[:]).then_inc(S["pe"], 1)

    @ctx.on("scalar")
    def _(a):
        for tc in range(8):
            a.wait_ge(S["pe"], mt_pe[tc])
            a.activation(mT[:, tc * 128:(tc + 1) * 128],
                         pEb[0:64, (tc % 2) * T:(tc % 2) * T + 128],
                         AF.Identity).then_inc(S["ac"], 1)

    mt_done = mt_ac[-1]

    w1_ld = [None] * 112
    c2b_ld = [V("ka" if k % 2 == 0 else "kb", 16) for k in range(7)]
    mh_pe = [None] * 56
    mh_ac = [None] * 56
    itb_pe = [None] * 112
    z_pe = [None] * 112
    rel_ac = [None] * 112
    cmp_dv = [None] * 112
    mkr_po = [None] * 112
    hm_dv = [None] * 112
    o_pe = [[None] * 32 for _ in range(NG)]
    o_dv = [[None] * 32 for _ in range(NG)]
    o_st = [[None] * 32 for _ in range(NG)]
    w2_ld = [[None] * 32 for _ in range(NG)]

    for g_ in range(NG):
        for i_ in range(GI):
            ic = g_ * GI + i_
            if ic % 2 == 0:
                mh_pe[ic // 2] = V("pe")
            itb_pe[ic] = V("pe")
            z_pe[ic] = V("pe")
        for oc in range(32):
            o_pe[g_][oc] = V("pe")
    for g_ in range(NG):
        for i_ in range(GI):
            w1_ld[g_ * GI + i_] = V(("ga" if (g_ * GI + i_) % 2 == 0 else "gb"), 16)
        for oc in range(32):
            w2_ld[g_][oc] = V(("ga" if oc % 2 == 0 else "gb"), 16)
    itf_ac = [None] * 112
    for g_ in range(NG):
        for i_ in range(GI):
            ic = g_ * GI + i_
            if ic % 2 == 0:
                mh_ac[ic // 2] = V("ac")
            itf_ac[ic] = V("ac")
            rel_ac[ic] = V("ac")
    for g_ in range(NG):
        for i_ in range(GI):
            ic = g_ * GI + i_
            cmp_dv[ic] = V("dv")
            hm_dv[ic] = V("dv")
        for oc in range(32):
            o_dv[g_][oc] = V("dv")
    for g_ in range(NG):
        for oc in range(32):
            o_st[g_][oc] = V("xka" if oc % 2 == 0 else "xkb", 16)

    @ctx.on("gpsimd")
    def _(g):
        kc_loaded = -1
        for g_ in range(NG):
            for i_ in range(GI):
                ic = g_ * GI + i_
                k = ic // 16
                if k > kc_loaded:
                    kc_loaded = k
                    if k >= 2:
                        g.wait_ge(S["pe"], itb_pe[k * 16 - 16 - 1])
                    else:
                        g.wait_ge(S["pe"], it_pe[6 - k][31])
                    g.dma_start(out=c2s[:, (k % 2) * 2048:(k % 2) * 2048 + 2048],
                                in_=c2T_h.ap()[:, k * 2048:(k + 1) * 2048]
                                ).then_inc(S["ka" if k % 2 == 0 else "kb"], 16)
                if ic >= 2:
                    g.wait_ge(S["pe"], z_pe[ic - 2])
                elif g_ == 0 and i_ < 2:
                    g.wait_ge(S["pe"], u_pe)
                g.dma_start(out=wstr[:, (ic % 2) * 4096:(ic % 2) * 4096 + 4096]
                            .rearrange("p (a f) -> p a f", f=128),
                            in_=w1tt_h.ap()[ic]).then_inc(S["ga" if ic % 2 == 0 else "gb"], 16)
            for oc in range(34):
                if oc < 32:
                    if oc >= 2:
                        g.wait_ge(S["pe"], o_pe[g_][oc - 2])
                    else:
                        g.wait_ge(S["pe"], z_pe[g_ * GI + GI - 2 + oc])
                    g.dma_start(out=wstr[:, (oc % 2) * 4096:(oc % 2) * 4096 + 3584]
                                .rearrange("p (a f) -> p a f", f=128),
                                in_=w2tt_h.ap()[g_][oc]).then_inc(S["ga" if oc % 2 == 0 else "gb"], 16)
                if oc >= 2:
                    ac = oc - 2
                    g.wait_ge(S["dv"], o_dv[g_][ac])
                    g.dma_start(out=ffnacc_h.ap()[ac * 128:(ac + 1) * 128, :],
                                in_=fstg[:, (ac % 2) * T:(ac % 2) * T + T],
                                accum_op=(ALU.bypass if g_ == 0 else ALU.add)
                                ).then_inc(S["xka" if ac % 2 == 0 else "xkb"], 16)

    @ctx.on("tensor")
    def _(t):
        for g_ in range(NG):
            for i_ in range(GI):
                ic = g_ * GI + i_
                j = ic // 2
                k = ic // 16
                if ic % 2 == 0:
                    t.wait_ge(S["ac"], mt_done)
                    if ic >= 1:
                        t.wait_ge(S["ac"], itf_ac[ic - 1])
                    for half in range(2):
                        ins = t.matmul(pB2[:, half * 512: half * 512 + 512],
                                       id_sb[0:56, j:j + 1].broadcast_to((56, 128)),
                                       mT[0:56, half * 512: half * 512 + 512],
                                       start=True, stop=True)
                    ins.then_inc(S["pe"], 1)
                t.wait_ge(S["ka" if k % 2 == 0 else "kb"], c2b_ld[k])
                t.wait_ge(S["ac"], mh_ac[j])
                if ic % 2 == 1:
                    t.wait_ge(S["ac"], itf_ac[ic - 1])
                for half in range(2):
                    ins = t.matmul(pB2[:, half * 512: half * 512 + 512],
                                   c2s[:, (k % 2) * 2048 + (ic % 16) * 128:(k % 2) * 2048 + (ic % 16) * 128 + 128],
                                   u_sb[:, half * 512: half * 512 + 512],
                                   start=True, stop=True)
                ins.then_inc(S["pe"], 1)
                t.wait_ge(S["ga" if ic % 2 == 0 else "gb"], w1_ld[ic])
                if ic >= 2:
                    t.wait_ge(S["ac"], rel_ac[ic - 2])
                elif g_ > 0:
                    pass
                ps = pA2 if ic % 2 == 0 else pC2
                for dc in range(32):
                    for half in range(2):
                        ins = t.matmul(ps[:, half * 512: half * 512 + 512],
                                       wstr[:, (ic % 2) * 4096 + dc * 128:(ic % 2) * 4096 + (dc + 1) * 128],
                                       xcen[:, dc * T + half * 512: dc * T + half * 512 + 512],
                                       start=(dc == 0), stop=(dc == 31))
                ins.then_inc(S["pe"], 1)
            for oc in range(32):
                t.wait_ge(S["ga" if oc % 2 == 0 else "gb"], w2_ld[g_][oc])
                if oc == 0:
                    t.wait_ge(S["dv"], hm_dv[g_ * GI + GI - 1])
                if oc >= 2:
                    t.wait_ge(S["dv"], o_dv[g_][oc - 2])
                elif g_ > 0:
                    t.wait_ge(S["dv"], o_dv[g_ - 1][30 + oc])
                else:
                    t.wait_ge(S["ac"], rel_ac[g_ * GI + GI - 2 + oc])
                ps = pA2 if oc % 2 == 0 else pC2
                for i_ in range(GI):
                    for half in range(2):
                        ins = t.matmul(ps[:, half * 512: half * 512 + 512],
                                       wstr[:, (oc % 2) * 4096 + i_ * 128:(oc % 2) * 4096 + (i_ + 1) * 128],
                                       hmg[:, i_ * T + half * 512: i_ * T + half * 512 + 512],
                                       start=(i_ == 0), stop=(i_ == GI - 1))
                ins.then_inc(S["pe"], 1)

    @ctx.on("scalar")
    def _(a):
        for g_ in range(NG):
            for i_ in range(GI):
                ic = g_ * GI + i_
                j = ic // 2
                if ic % 2 == 0:
                    a.wait_ge(S["pe"], mh_pe[j])
                    if j >= 2:
                        a.wait_ge(S["dv"], cmp_dv[2 * j - 3])
                    a.activation(mhat[:, (j % 2) * T:(j % 2) * T + T],
                                 pB2[:, 0:T], AF.Identity).then_inc(S["ac"], 1)
                a.wait_ge(S["pe"], itb_pe[ic])
                if ic >= 2:
                    a.wait_ge(S["dv"], cmp_dv[ic - 2])
                a.activation(itf[:, (ic % 2) * T:(ic % 2) * T + T],
                             pB2[:, 0:T], AF.Identity).then_inc(S["ac"], 1)
                a.wait_ge(S["pe"], z_pe[ic])
                if ic >= 2:
                    a.wait_ge(S["dv"], hm_dv[ic - 2])
                ps = pA2 if ic % 2 == 0 else pC2
                a.activation(zrb[:, (ic % 2) * T:(ic % 2) * T + T],
                             ps[:, 0:T], AF.Relu).then_inc(S["ac"], 1)

    @ctx.on("vector")
    def _(v):
        for g_ in range(NG):
            for i_ in range(GI):
                ic = g_ * GI + i_
                j = ic // 2
                v.wait_ge(S["ac"], itf_ac[ic])
                v.wait_ge(S["ac"], mh_ac[j])
                v.tensor_tensor(mk[:, (ic % 2) * T:(ic % 2) * T + T],
                                itf[:, (ic % 2) * T:(ic % 2) * T + T],
                                mhat[:, (j % 2) * T:(j % 2) * T + T],
                                ALU.is_ge).then_inc(S["dv"], 1)
                v.tensor_tensor(mk[:, (ic % 2) * T:(ic % 2) * T + T],
                                mk[:, (ic % 2) * T:(ic % 2) * T + T],
                                rbf[:], ALU.mult)
                v.wait_ge(S["ac"], rel_ac[ic])
                v.tensor_tensor(hmg[:, i_ * T:(i_ + 1) * T],
                                zrb[:, (ic % 2) * T:(ic % 2) * T + T],
                                mk[:, (ic % 2) * T:(ic % 2) * T + T],
                                ALU.mult).then_inc(S["dv"], 1)
            for oc in range(32):
                v.wait_ge(S["pe"], o_pe[g_][oc])
                if g_ * 32 + oc >= 2:
                    pg, po_ = divmod(g_ * 32 + oc - 2, 32)
                    v.wait_ge(S["xka" if po_ % 2 == 0 else "xkb"], o_st[pg][po_])
                ps = pA2 if oc % 2 == 0 else pC2
                v.tensor_copy(fstg[:, (oc % 2) * T:(oc % 2) * T + T],
                              ps[:, 0:T]).then_inc(S["dv"], 1)

    xa_tot = ctx.cnt["xka"]
    xb_tot = ctx.cnt["xkb"]
    rb_ld = [V("ra" if oc % 2 == 0 else "rb", 16) for oc in range(32)]
    sc_dv = [None] * 32
    res_dv = [None] * 32
    for oc in range(32):
        sc_dv[oc] = V("dv")
        res_dv[oc] = V("dv")

    @ctx.on("sync")
    def _(s):
        s.wait_ge(S["xka"], xa_tot)
        s.wait_ge(S["xkb"], xb_tot)
        for oc in range(32):
            if oc >= 2:
                s.wait_ge(S["dv"], sc_dv[oc - 2])
            s.dma_start(out=fstg[:, (oc % 2) * T:(oc % 2) * T + T],
                        in_=ffnacc_h.ap()[oc * 128:(oc + 1) * 128, :]
                        ).then_inc(S["ra" if oc % 2 == 0 else "rb"], 16)

    @ctx.on("vector")
    def _(v):
        for oc in range(32):
            v.wait_ge(S["ra" if oc % 2 == 0 else "rb"], rb_ld[oc])
            v.tensor_tensor(scr[:, (oc % 2) * T:(oc % 2) * T + T],
                            fstg[:, (oc % 2) * T:(oc % 2) * T + T],
                            rbf[:], ALU.mult).then_inc(S["dv"], 1)
            v.wait_ge(S["dv"], sc_dv[oc])
            v.tensor_tensor(xcen[:, oc * T:(oc + 1) * T], xcen[:, oc * T:(oc + 1) * T],
                            scr[:, (oc % 2) * T:(oc % 2) * T + T],
                            ALU.add).then_inc(S["dv"], 1)

    ctx.mark["xcen_done"] = res_dv[-1]



SCALE128 = 0.08838834764831845


def emit_l1_attn(ctx, wq_h, wk_h, wv_h, wo_h, meta_h, sh_k, sh_v, sh_flag):
    nc = ctx.nc
    xcen, r = ctx.buf["xcen"], ctx.buf["r"]
    ones, id_sb = ctx.buf["ones128"], ctx.buf["id_sb"]
    pA2, pB2, pC2, pEb = ctx.buf["pA2"], ctx.buf["pB2"], ctx.buf["pC2"], ctx.buf["pEb"]
    wstr = ctx.buf["wstr"]
    S, V = ctx.sem, ctx.inc
    center_dv, r_ac = ctx.mark["center_dv"], ctx.mark["r_ac"]

    QeT = ctx.ssbuf("QeT_sb", [128, 8 * T], BF16)
    AtT = ctx.ssbuf("AtT_sb", [128, 8 * T], BF16)
    kvbf = ctx.ssbuf("kvbf_sb", [128, 2 * T], BF16)
    vtt = ctx.ssbuf("vtt_sb", [128, 2 * T], BF16)
    Khsb = ctx.ssbuf("Khsb", [128, 2 * 2048], BF16)
    Vhsb = ctx.ssbuf("Vhsb", [128, 2 * 2048], BF16)
    PTsb = ctx.ssbuf("PTsb", [128, 4 * 512], BF16)
    rbh = ctx.ssbuf("rbh_sb", [128, 2 * 512], F32)

    def gsem(i):
        return "ga" if i % 2 == 0 else "gb"

    def xsem(i):
        return "xka" if i % 2 == 0 else "xkb"

    wk_ld = [V(gsem(i), 16) for i in range(8)]
    kpe = [V("pe") for _ in range(8)]
    kev = [V("dv") for _ in range(8)]
    k_st = [V(xsem(i), 16) for i in range(8)]

    @ctx.on("gpsimd")
    def _(g):
        for kdc in range(8):
            if kdc >= 2:
                g.wait_ge(S["pe"], kpe[kdc - 2])
            g.dma_start(out=wstr[:, (kdc % 2) * 4096:(kdc % 2) * 4096 + 4096]
                        .rearrange("p (a f) -> p a f", f=128),
                        in_=wk_h.ap()[kdc]).then_inc(S[gsem(kdc)], 16)

    @ctx.on("tensor")
    def _(t):
        t.wait_ge(S["dv"], center_dv)
        for kdc in range(8):
            t.wait_ge(S[gsem(kdc)], wk_ld[kdc])
            if kdc >= 2:
                t.wait_ge(S["dv"], kev[kdc - 2])
            ps = pA2 if kdc % 2 == 0 else pC2
            for dc in range(32):
                for half in range(2):
                    ins = t.matmul(ps[:, half * 512: half * 512 + 512],
                                   wstr[:, (kdc % 2) * 4096 + dc * 128:(kdc % 2) * 4096 + (dc + 1) * 128],
                                   xcen[:, dc * T + half * 512: dc * T + half * 512 + 512],
                                   start=(dc == 0), stop=(dc == 31))
            ins.then_inc(S["pe"], 1)

    @ctx.on("vector")
    def _(v):
        v.wait_ge(S["ac"], r_ac)
        for kdc in range(8):
            v.wait_ge(S["pe"], kpe[kdc])
            if kdc >= 2:
                v.wait_ge(S[xsem(kdc - 2)], k_st[kdc - 2])
            ps = pA2 if kdc % 2 == 0 else pC2
            v.tensor_tensor(kvbf[:, (kdc % 2) * T:(kdc % 2) * T + T],
                            ps[:, 0:T], r[:], ALU.mult).then_inc(S["dv"], 1)

    @ctx.on("sync")
    def _(g):
        with g.register("preg_k") as preg:
            g.reg_load(preg, meta_h.ap()[0:1, 0:1])
            for kdc in range(8):
                g.wait_ge(S["dv"], kev[kdc])
                src = kvbf[:, (kdc % 2) * T:(kdc % 2) * T + T]
                with g.If_ne(preg, 0):
                    g.dma_start(out=sh_k.ap()[1][kdc * 128:(kdc + 1) * 128, :],
                                in_=src).then_inc(S[xsem(kdc)], 16)
                with g.Else():
                    g.dma_start(out=sh_k.ap()[0][kdc * 128:(kdc + 1) * 128, :],
                                in_=src).then_inc(S[xsem(kdc)], 16)

    wv_ld = [V(gsem(i), 16) for i in range(8)]
    vpe = [None] * 8
    vtr = [[None] * 8 for _ in range(8)]
    vpe[0] = V("pe")
    vpe[1] = V("pe")
    for vdc in range(2, 8):
        vtr[vdc - 2] = [V("pe") for _ in range(8)]
        vpe[vdc] = V("pe")
    vtr[6] = [V("pe") for _ in range(8)]
    vtr[7] = [V("pe") for _ in range(8)]
    vev = [V("dv") for _ in range(8)]
    vte = [[V("ac") for _ in range(8)] for _ in range(8)]
    v_st = [V(xsem(i), 16) for i in range(8)]

    @ctx.on("gpsimd")
    def _(g):
        for vdc in range(8):
            if vdc < 2:
                g.wait_ge(S["pe"], kpe[6 + vdc])
            else:
                g.wait_ge(S["pe"], vpe[vdc - 2])
            g.dma_start(out=wstr[:, (vdc % 2) * 4096:(vdc % 2) * 4096 + 4096]
                        .rearrange("p (a f) -> p a f", f=128),
                        in_=wv_h.ap()[vdc]).then_inc(S[gsem(vdc)], 16)

    def _emit_v_mm(t, vdc):
        t.wait_ge(S[gsem(vdc)], wv_ld[vdc])
        if vdc >= 2:
            t.wait_ge(S["dv"], vev[vdc - 2])
        else:
            t.wait_ge(S["dv"], kev[6 + vdc])
        ps = pA2 if vdc % 2 == 0 else pC2
        for dc in range(32):
            for half in range(2):
                ins = t.matmul(ps[:, half * 512: half * 512 + 512],
                               wstr[:, (vdc % 2) * 4096 + dc * 128:(vdc % 2) * 4096 + (dc + 1) * 128],
                               xcen[:, dc * T + half * 512: dc * T + half * 512 + 512],
                               start=(dc == 0), stop=(dc == 31))
        ins.then_inc(S["pe"], 1)

    def _emit_v_tr(t, vdc):
        for tc in range(8):
            gi = vdc * 8 + tc
            t.wait_ge(S["dv"], vev[vdc])
            if gi >= 2:
                t.wait_ge(S["ac"], vte[(gi - 2) // 8][(gi - 2) % 8])
            t.transpose(pEb[:, (gi % 2) * T:(gi % 2) * T + 128],
                        kvbf[:, (vdc % 2) * T + tc * 128:(vdc % 2) * T + (tc + 1) * 128],
                        id_sb[:]).then_inc(S["pe"], 1)

    @ctx.on("tensor")
    def _(t):
        _emit_v_mm(t, 0)
        _emit_v_mm(t, 1)
        for vdc in range(2, 8):
            _emit_v_tr(t, vdc - 2)
            _emit_v_mm(t, vdc)
        _emit_v_tr(t, 6)
        _emit_v_tr(t, 7)

    @ctx.on("vector")
    def _(v):
        for vdc in range(8):
            v.wait_ge(S["pe"], vpe[vdc])
            if vdc < 2:
                v.wait_ge(S[xsem(vdc)], k_st[6 + vdc])
            else:
                v.wait_ge(S[xsem(vdc)], v_st[vdc - 2])
            ps = pA2 if vdc % 2 == 0 else pC2
            v.tensor_tensor(kvbf[:, (vdc % 2) * T:(vdc % 2) * T + T],
                            ps[:, 0:T], r[:], ALU.mult).then_inc(S["dv"], 1)

    @ctx.on("scalar")
    def _(a):
        for vdc in range(8):
            for tc in range(8):
                gi = vdc * 8 + tc
                a.wait_ge(S["pe"], vtr[vdc][tc])
                if tc == 0 and vdc >= 2:
                    a.wait_ge(S[xsem(vdc)], v_st[vdc - 2])
                a.activation(vtt[:, (vdc % 2) * T + tc * 128:(vdc % 2) * T + (tc + 1) * 128],
                             pEb[:, (gi % 2) * T:(gi % 2) * T + 128],
                             AF.Identity).then_inc(S["ac"], 1)

    @ctx.on("sync")
    def _(g):
        with g.register("preg_v") as preg:
            g.reg_load(preg, meta_h.ap()[0:1, 0:1])
            for vdc in range(8):
                g.wait_ge(S["ac"], vte[vdc][7])
                src = vtt[:, (vdc % 2) * T:(vdc % 2) * T + T].rearrange(
                    "p (c f) -> p c f", f=128)
                with g.If_ne(preg, 0):
                    g.dma_start(out=sh_v.ap()[1].rearrange("(c p) v -> p c v", p=128)
                                [:, :, vdc * 128:(vdc + 1) * 128],
                                in_=src).then_inc(S[xsem(vdc)], 16)
                with g.Else():
                    g.dma_start(out=sh_v.ap()[0].rearrange("(c p) v -> p c v", p=128)
                                [:, :, vdc * 128:(vdc + 1) * 128],
                                in_=src).then_inc(S[xsem(vdc)], 16)

    xka_total = ctx.cnt["xka"]
    xkb_total = ctx.cnt["xkb"]

    wq_ld = [V(gsem(i), 16) for i in range(32)]
    qspe = [V("pe") for _ in range(32)]
    qev = [V("dv") for _ in range(8)]

    @ctx.on("gpsimd")
    def _(g):
        for mc in range(32):
            if mc < 2:
                g.wait_ge(S["pe"], vpe[6 + mc])
            else:
                g.wait_ge(S["pe"], qspe[mc - 2])
            g.dma_start(out=wstr[:, (mc % 2) * 4096:(mc % 2) * 4096 + 4096]
                        .rearrange("p (a f) -> p a f", f=128),
                        in_=wq_h.ap()[mc]).then_inc(S[gsem(mc)], 16)

    @ctx.on("tensor")
    def _(t):
        for hkv in range(8):
            ps = pA2 if hkv % 2 == 0 else pC2
            if hkv < 2:
                t.wait_ge(S["dv"], vev[6 + hkv])
            else:
                t.wait_ge(S["dv"], qev[hkv - 2])
            for gq in range(4):
                mc = hkv * 4 + gq
                t.wait_ge(S[gsem(mc)], wq_ld[mc])
                for dc in range(32):
                    for half in range(2):
                        ins = t.matmul(ps[:, half * 512: half * 512 + 512],
                                       wstr[:, (mc % 2) * 4096 + dc * 128:(mc % 2) * 4096 + (dc + 1) * 128],
                                       xcen[:, dc * T + half * 512: dc * T + half * 512 + 512],
                                       start=(gq == 0 and dc == 0), stop=(gq == 3 and dc == 31))
                ins.then_inc(S["pe"], 1)

    @ctx.on("vector")
    def _(v):
        for hkv in range(8):
            v.wait_ge(S["pe"], qspe[hkv * 4 + 3])
            ps = pA2 if hkv % 2 == 0 else pC2
            v.tensor_tensor(QeT[:, hkv * T:(hkv + 1) * T], ps[:, 0:T], r[:],
                            ALU.mult).then_inc(S["dv"], 1)

    v_xc = V("xc")

    @ctx.on("sync")
    def _(g):
        with (g.register("preg_x") as preg, g.register("nonce_x") as nonce,
              g.register("ne_x") as ne, g.register("freg_x") as freg):
            g.reg_load(preg, meta_h.ap()[0:1, 0:1])
            g.reg_load(nonce, meta_h.ap()[0:1, 1:2])
            g.wait_ge(S["xka"], xka_total)
            g.wait_ge(S["xkb"], xkb_total)
            with g.If_ne(preg, 0):
                g.reg_save(sh_flag.ap()[1:2, 0:1], nonce)
            with g.Else():
                g.reg_save(sh_flag.ap()[0:1, 0:1], nonce)
            g.reg_mov(ne, 1)
            with g.While(ne):
                with g.If_ne(preg, 0):
                    g.reg_load(freg, sh_flag.ap()[0:1, 0:1])
                with g.Else():
                    g.reg_load(freg, sh_flag.ap()[1:2, 0:1])
                g.reg_alu(ne, freg, nonce, ALU.not_equal)
            g.sem_inc(S["xc"], 1)

    kh_ld = [V("ka" if i % 2 == 0 else "kb", 64) for i in range(8)]
    NA = 8 * 2 * 16
    aspe = [None] * NA
    adpe = [None] * NA
    aexp = [None] * NA
    for it in range(NA + 1):
        if it < NA:
            aspe[it] = V("pe")
        if it >= 1:
            adpe[it - 1] = V("pe")
    att_marks = []
    for hkv in range(8):
        per_q = []
        for qh in range(2):
            per_q.append({"rb": V("dv"), "att": V("dv")})
        att_marks.append(per_q)
    for u in range(NA):
        aexp[u] = V("ac")

    @ctx.on("sync")
    def _(s):
        s.wait_ge(S["xc"], v_xc)
        for hkv in range(8):
            if hkv >= 2:
                s.wait_ge(S["dv"], att_marks[hkv - 2][1]["att"])
            hp = hkv % 2
            sem = S["ka" if hkv % 2 == 0 else "kb"]
            for slot in range(2):
                s.dma_start(out=Khsb[:, hp * 2048 + slot * T: hp * 2048 + (slot + 1) * T],
                            in_=sh_k.ap()[slot][hkv * 128:(hkv + 1) * 128, :]).then_inc(sem, 16)
                s.dma_start(out=Vhsb[:, hp * 2048 + slot * T: hp * 2048 + (slot + 1) * T]
                            .rearrange("p (c f) -> p c f", f=128),
                            in_=sh_v.ap()[slot].rearrange("(c p) v -> p c v", p=128)
                            [:, :, hkv * 128:(hkv + 1) * 128]).then_inc(sem, 16)

    @ctx.on("tensor")
    def _(t):
        t.wait_ge(S["dv"], qev[7])
        for it in range(NA + 1):
            if it < NA:
                u = it
                hkv, qh, kc = u // 32, (u // 16) % 2, u % 16
                hp = hkv % 2
                if kc == 0 and qh == 0:
                    t.wait_ge(S["ka" if hkv % 2 == 0 else "kb"], kh_ld[hkv])
                    t.wait_ge(S["dv"], qev[hkv])
                if u >= 2:
                    t.wait_ge(S["ac"], aexp[u - 2])
                t.matmul(pA2[:, (u % 2) * 512:(u % 2) * 512 + 512],
                         Khsb[:, hp * 2048 + kc * 128: hp * 2048 + (kc + 1) * 128],
                         QeT[:, hkv * T + qh * 512: hkv * T + qh * 512 + 512],
                         start=True, stop=True).then_inc(S["pe"], 1)
            if it >= 1:
                u = it - 1
                hkv, qh, kc = u // 32, (u // 16) % 2, u % 16
                hp = hkv % 2
                t.wait_ge(S["ac"], aexp[u])
                if kc == 0 and hkv > 0:
                    t.wait_ge(S["dv"], att_marks[hkv - 1][qh]["att"])
                t.matmul(pB2[:, qh * 512: qh * 512 + 512],
                         ones[:], PTsb[:, (u % 4) * 512:(u % 4) * 512 + 512],
                         start=(kc == 0), stop=(kc == 15))
                t.matmul(pC2[:, qh * 512: qh * 512 + 512],
                         Vhsb[:, hp * 2048 + kc * 128: hp * 2048 + (kc + 1) * 128],
                         PTsb[:, (u % 4) * 512:(u % 4) * 512 + 512],
                         start=(kc == 0), stop=(kc == 15)).then_inc(S["pe"], 1)

    @ctx.on("scalar")
    def _(a):
        for u in range(NA):
            a.wait_ge(S["pe"], aspe[u])
            if u >= 4:
                a.wait_ge(S["pe"], adpe[u - 4])
            a.activation(PTsb[:, (u % 4) * 512:(u % 4) * 512 + 512],
                         pA2[:, (u % 2) * 512:(u % 2) * 512 + 512],
                         AF.Exp, scale=SCALE128).then_inc(S["ac"], 1)

    @ctx.on("vector")
    def _(v):
        for hkv in range(8):
            for qh in range(2):
                m = att_marks[hkv][qh]
                base = (hkv * 2 + qh) * 16
                v.wait_ge(S["pe"], adpe[base + 15])
                v.reciprocal(rbh[:, qh * 512: qh * 512 + 512],
                             pB2[:, qh * 512: qh * 512 + 512]).then_inc(S["dv"], 1)
                v.wait_ge(S["dv"], m["rb"])
                v.tensor_tensor(AtT[:, hkv * T + qh * 512: hkv * T + qh * 512 + 512],
                                pC2[:, qh * 512: qh * 512 + 512],
                                rbh[:, qh * 512: qh * 512 + 512],
                                ALU.mult).then_inc(S["dv"], 1)

    wo_ld = [V(gsem(i), 16) for i in range(32)]
    ope = [V("pe") for _ in range(32)]
    res_dv = [V("dv") for _ in range(32)]

    @ctx.on("gpsimd")
    def _(g):
        for oc in range(32):
            if oc < 2:
                g.wait_ge(S["pe"], qspe[30 + oc])
            else:
                g.wait_ge(S["pe"], ope[oc - 2])
            g.dma_start(out=wstr[:, (oc % 2) * 4096:(oc % 2) * 4096 + 1024]
                        .rearrange("p (a f) -> p a f", f=128),
                        in_=wo_h.ap()[oc]).then_inc(S[gsem(oc)], 16)

    @ctx.on("tensor")
    def _(t):
        t.wait_ge(S["dv"], att_marks[7][1]["att"])
        for oc in range(32):
            t.wait_ge(S[gsem(oc)], wo_ld[oc])
            if oc >= 2:
                t.wait_ge(S["dv"], res_dv[oc - 2])
            ps = pA2 if oc % 2 == 0 else pC2
            for kvc in range(8):
                for half in range(2):
                    ins = t.matmul(ps[:, half * 512: half * 512 + 512],
                                   wstr[:, (oc % 2) * 4096 + kvc * 128:(oc % 2) * 4096 + (kvc + 1) * 128],
                                   AtT[:, kvc * T + half * 512: kvc * T + half * 512 + 512],
                                   start=(kvc == 0), stop=(kvc == 7))
            ins.then_inc(S["pe"], 1)

    @ctx.on("vector")
    def _(v):
        for oc in range(32):
            v.wait_ge(S["pe"], ope[oc])
            ps = pA2 if oc % 2 == 0 else pC2
            v.tensor_tensor(xcen[:, oc * T:(oc + 1) * T], xcen[:, oc * T:(oc + 1) * T],
                            ps[:, 0:T], ALU.add).then_inc(S["dv"], 1)

    ctx.mark["xcen_done"] = res_dv[-1]



def emit_output(ctx, out_h):
    xcen, mu_acc = ctx.buf["xcen"], ctx.buf["mu_acc"]
    id_sb, pEb = ctx.buf["id_sb"], ctx.buf["pEb"]
    S, V = ctx.sem, ctx.inc
    entry = ctx.mark["xcen_done"]
    out_sb = ctx.buf["oscr"]
    mutok = ctx.ssbuf("mutok_sb", [128, 8], F32)

    v_mutok = emit_bcast_to_tok(ctx, mu_acc, mutok, "dv", entry)

    ev_marks = {}
    st_done = {}
    for tc in range(8):
        tr_pe = [V("pe") for _ in range(8)]
        ac_ev = [V("ac") for _ in range(8)]
        for dgrp in range(8):
            ev_marks[tc * 8 + dgrp] = ac_ev[dgrp]
            st_done[tc * 8 + dgrp] = V("sta" if dgrp % 2 == 0 else "stb", 16)

        @ctx.on("tensor")
        def _(t, tc=tc):
            for dgrp in range(8):
                gi = tc * 8 + dgrp
                if gi < 2:
                    t.wait_ge(S["dv"], v_mutok[-1])
                else:
                    t.wait_ge(S["ac"], ev_marks[gi - 2])
                for j in range(4):
                    dc = dgrp * 4 + j
                    ins = t.transpose(pEb[:, (dgrp % 2) * T + j * 128:(dgrp % 2) * T + j * 128 + 128],
                                      xcen[:, dc * T + tc * 128: dc * T + tc * 128 + 128],
                                      id_sb[:])
                ins.then_inc(S["pe"], 1)

        @ctx.on("scalar")
        def _(a, tc=tc, tr_pe=tr_pe):
            a.wait_ge(S["dv"], v_mutok[tc])
            for dgrp in range(8):
                gi = tc * 8 + dgrp
                a.wait_ge(S["pe"], tr_pe[dgrp])
                if gi >= 2:
                    a.wait_ge(S["sta" if dgrp % 2 == 0 else "stb"], st_done[gi - 2])
                a.activation(out_sb[:, (dgrp % 2) * 512:(dgrp % 2) * 512 + 512],
                             pEb[:, (dgrp % 2) * T:(dgrp % 2) * T + 512],
                             AF.Identity, bias=mutok[:, tc:tc + 1]).then_inc(S["ac"], 1)

        @ctx.on("sync")
        def _(s, tc=tc, ac_ev=ac_ev):
            for dgrp in range(8):
                s.wait_ge(S["ac"], ac_ev[dgrp])
                s.dma_start(out=out_h.ap()[tc * 128:(tc + 1) * 128, dgrp * 512:(dgrp + 1) * 512],
                            in_=out_sb[:, (dgrp % 2) * 512:(dgrp % 2) * 512 + 512]
                            ).then_inc(S["sta" if dgrp % 2 == 0 else "stb"], 16)

    @ctx.on("sync")
    def _(s):
        s.wait_ge(S["sta"], ctx.cnt["sta"])
        s.wait_ge(S["stb"], ctx.cnt["stb"])


def emit_output_xT(ctx, outT_h):
    xcen, mu_acc = ctx.buf["xcen"], ctx.buf["mu_acc"]
    S, V = ctx.sem, ctx.inc
    entry = ctx.mark["xcen_done"]
    xo = ctx.buf["oscr"]

    v_ev = [V("dv") for _ in range(NT)]
    st = [V("sta" if i % 2 == 0 else "stb", 16) for i in range(NT)]

    @ctx.on("vector")
    def _(v):
        v.wait_ge(S["dv"], entry)
        for i in range(NT):
            if i >= 1:
                v.wait_ge(S["sta" if i % 2 == 1 else "stb"], st[i - 1])
            v.tensor_tensor(xo[:], xcen[:, i * T:(i + 1) * T], mu_acc[:],
                            ALU.add).then_inc(S["dv"], 1)

    @ctx.on("sync")
    def _(s):
        for i in range(NT):
            s.wait_ge(S["dv"], v_ev[i])
            s.dma_start(out=outT_h.ap()[i * 128:(i + 1) * 128, :],
                        in_=xo[:]).then_inc(S["sta" if i % 2 == 0 else "stb"], 16)
        s.wait_ge(S["sta"], ctx.cnt["sta"])
        s.wait_ge(S["stb"], ctx.cnt["stb"])



def build_program(stages, out_mode="tokmajor", detect_races=True):
    nc = bass.Bass("TRN2", target_bir_lowering=False,
                   detect_race_conditions=detect_races)
    ctx = Ctx(nc)
    for s in ["swa", "swb", "sg", "ga", "gb", "sc", "sd", "pe", "dv", "ac",
              "po", "sta", "stb", "xc", "xka", "xkb", "ka", "kb", "ha", "hb",
              "ra", "rb"]:
        ctx.add_sem(s)

    xT_h = nc.dram_tensor("xT", [D, T], F32, kind="ExternalInput")

    ctx.sbuf("xcen", [128, NT * T], BF16)
    ctx.sbuf("scr2T", [128, 2 * T], BF16)
    ctx.sbuf("dmu", [128, T], F32)
    ctx.sbuf("dmub", [128, T], BF16)
    ctx.sbuf("var", [128, T], F32)
    ctx.sbuf("r", [128, T], F32)
    ctx.sbuf("mu_acc", [128, T], F32)
    ctx.sbuf("tmpT", [128, T], F32)
    ctx.sbuf("oscr", [128, T], F32)
    ctx.sbuf("rbf", [128, T], BF16)
    ctx.sbuf("ones128", [128, 128], BF16)
    ctx.sbuf("id_sb", [128, 128], BF16)
    ctx.sbuf("rep_sb", [4, 64], BF16)
    ctx.sbuf("eps_sb", [128, 1], F32)
    ctx.sbuf("wstr", [128, 2 * 4096], BF16)
    ctx.sbuf("prott_sb", [128, 128], BF16)
    ctx.psum("pA2", [128, T])
    ctx.psum("pB2", [128, T])
    ctx.psum("pC2", [128, T])
    ctx.psum("pEb", [128, 2 * T], BF16)

    rep = np.zeros((4, 64), dtype=BF)
    for e in range(4):
        rep[e, e * 16:(e + 1) * 16] = BF(4.0)
    rep4_ap = nc.inline_tensor(rep, name="rep4c")
    ident_ap = nc.inline_tensor(np.eye(128).astype(BF), name="identc")
    ones_ap = nc.inline_tensor(np.ones((128, 128), dtype=BF), name="ones128c")
    eps_ap = nc.inline_tensor(np.full((128, 1), EPS, dtype=np.float32), name="epsc")
    protm = np.zeros((128, 128), dtype=BF)
    for d_ in range(64):
        protm[64 + d_, d_] = BF(-1.0)
    for d_ in range(64, 128):
        protm[d_ - 64, d_] = BF(1.0)
    prot_ap = nc.inline_tensor(protm, name="protc")

    ctx.static_load(ctx.buf["ones128"][:], ones_ap.ap())
    ctx.static_load(ctx.buf["id_sb"][:], ident_ap.ap())
    ctx.static_load(ctx.buf["rep_sb"][:], rep4_ap.ap())
    ctx.static_load(ctx.buf["eps_sb"][:], eps_ap.ap())
    ctx.static_load(ctx.buf["prott_sb"][:], prot_ap.ap())

    ctx.init_arena()
    emit_load_and_ln1(ctx, xT_h)

    handles = {}
    for si, st in enumerate(stages):
        if si > 0:
            gate = ctx.mark["xcen_done"]
            for _eng in ["sync", "gpsimd", "tensor", "vector", "scalar"]:
                def _gatef(e, gate=gate):
                    e.wait_ge(ctx.sem["dv"], gate)
                ctx.q[_eng].append(_gatef)
            ctx.arena_reset()
            emit_ln(ctx)
        if st == "l1ffn":
            wrT_h = nc.dram_tensor("wrT", [D, 4], F32, kind="ExternalInput")
            aT_h = nc.dram_tensor("aT", [D, 64], F32, kind="ExternalInput")
            bT_h = nc.dram_tensor("bT", [64, D], F32, kind="ExternalInput")
            emit_l1_ffn(ctx, wrT_h, aT_h, bT_h)
        elif st == "l0attn":
            wq_h = nc.dram_tensor("wq0t", [32, 128, 32, 128], F32, kind="ExternalInput")
            wo_h = nc.dram_tensor("wo0t", [32, 128, 32, 128], F32, kind="ExternalInput")
            wk_h = nc.dram_tensor("wk0t", [1, 128, 32, 128], F32, kind="ExternalInput")
            wv_h = nc.dram_tensor("wv0t", [1, 128, 32, 128], F32, kind="ExternalInput")
            xhalo_h = nc.dram_tensor("xhaloT", [D, 256], F32, kind="ExternalInput")
            ck_h = nc.dram_tensor("coskT", [128, 1536], BF16, kind="ExternalInput")
            sk_h = nc.dram_tensor("sinkT", [128, 1536], BF16, kind="ExternalInput")
            denm_h = nc.dram_tensor("denmI", [128, 12 * 128], BF16, kind="ExternalInput")
            alA_h = nc.dram_tensor("alAI", [128, 4 * 256], BF16, kind="ExternalInput")
            alB_h = nc.dram_tensor("alBI", [128, 4 * 256], BF16, kind="ExternalInput")
            hsel_h = nc.dram_tensor("hselI", [128, 2], F32, kind="ExternalInput")
            emit_l0_attn(ctx, wq_h, wo_h, wk_h, wv_h, xhalo_h, ck_h,
                         sk_h, denm_h, alA_h, alB_h, hsel_h)
        elif st == "l0ffn":
            c1t_h = nc.dram_tensor("c1tI", [128, 32, 16], F32, kind="ExternalInput")
            c2T_h = nc.dram_tensor("c2TI", [16, 14336], F32, kind="ExternalInput")
            w1tt_h = nc.dram_tensor("w1tt", [112, 128, 32, 128], F32, kind="ExternalInput")
            w2tt_h = nc.dram_tensor("w2tt", [4, 32, 128, 28, 128], F32, kind="ExternalInput")
            ffnacc_h = nc.dram_tensor("ffnacc", [D, T], BF16)
            emit_l0_ffn(ctx, c1t_h, c2T_h, w1tt_h, w2tt_h, ffnacc_h)
        elif st == "l1attn":
            wq_h = nc.dram_tensor("wq1t", [32, 128, 32, 128], F32, kind="ExternalInput")
            wk_h = nc.dram_tensor("wk1t", [8, 128, 32, 128], F32, kind="ExternalInput")
            wv_h = nc.dram_tensor("wv1t", [8, 128, 32, 128], F32, kind="ExternalInput")
            wo_h = nc.dram_tensor("wo1t", [32, 128, 8, 128], F32, kind="ExternalInput")
            meta_h = nc.dram_tensor("meta", [1, 8], I32, kind="ExternalInput")
            sh_k = nc.dram_tensor("sh_k", [2, 1024, 1024], BF16, addr_space="Shared")
            sh_v = nc.dram_tensor("sh_v", [2, 1024, 1024], BF16, addr_space="Shared")
            sh_flag = nc.dram_tensor("sh_flag", [2, 16], I32, addr_space="Shared")
            emit_l1_attn(ctx, wq_h, wk_h, wv_h, wo_h, meta_h, sh_k, sh_v, sh_flag)
        else:
            raise ValueError(st)

    if out_mode == "tokmajor":
        out_h = nc.dram_tensor("out", [T, D], F32, kind="ExternalOutput")
        emit_output(ctx, out_h)
    else:
        outT_h = nc.dram_tensor("outT", [D, T], F32, kind="ExternalOutput")
        emit_output_xT(ctx, outT_h)

    assert ctx.arena_base + ctx.arena_peak <= int(nc.sbuf_top), \
        (ctx.arena_base, ctx.arena_peak, int(nc.sbuf_top))

    @ctx.on("gpsimd")
    def _(g):
        pass

    def _static(g):
        for dst, src in ctx.static_loads:
            g.dma_start(out=dst, in_=src).then_inc(ctx.sem["sc"], 16)
    ctx.q["gpsimd"].insert(0, _static)

    def _scwait(e):
        e.wait_ge(ctx.sem["sc"], ctx.sc_total())
    for _eng in ["sync", "tensor", "vector", "scalar"]:
        ctx.q[_eng].insert(0, _scwait)
    ctx.q["gpsimd"].insert(1, _scwait)

    with nc.Block() as block:
        @block.sync
        def _(e):
            for f in ctx.q["sync"]:
                f(e)

        @block.gpsimd
        def _(e):
            for f in ctx.q["gpsimd"]:
                f(e)

        @block.tensor
        def _(e):
            for f in ctx.q["tensor"]:
                f(e)

        @block.vector
        def _(e):
            for f in ctx.q["vector"]:
                f(e)

        @block.scalar
        def _(e):
            for f in ctx.q["scalar"]:
                f(e)

    return nc


def host_inputs_l1ffn(x_full, A, B_, Wr):
    wrT = np.ascontiguousarray(Wr.T)
    aT = np.ascontiguousarray(A.transpose(2, 0, 1).reshape(D, 64))
    bT = np.ascontiguousarray(B_.transpose(0, 2, 1).reshape(64, D))
    maps = []
    for c in range(8):
        b, h = c // 2, c % 2
        xT = np.ascontiguousarray(x_full[b, h * T:(h + 1) * T, :].T)
        maps.append({"xT": xT, "wrT": wrT, "aT": aT, "bT": bT})
    return maps


def host_inputs_l1attn(x_full, wq, wk, wv, wo, nonce):
    wq1t = pack_lhsT(wq)
    wk1t = pack_lhsT(wk)
    wv1t = pack_lhsT(wv)
    wo1t = pack_lhsT(wo)
    maps = []
    for c in range(8):
        b, h = c // 2, c % 2
        xT = np.ascontiguousarray(x_full[b, h * T:(h + 1) * T, :].T)
        meta = np.array([[h, nonce, 0, 0, 0, 0, 0, 0]], dtype=np.int32)
        maps.append({"xT": xT, "wq1t": wq1t, "wk1t": wk1t, "wv1t": wv1t,
                     "wo1t": wo1t, "meta": meta})
    return maps


def _rope_tables():
    dh = 128
    inv_freq = 1.0 / 10000.0 ** (np.arange(0, dh, 2, dtype=np.float32) / dh)
    t = np.arange(2048, dtype=np.float32)
    f = np.outer(t, inv_freq)
    cos = np.zeros((2048, dh), dtype=np.float32)
    sin = np.zeros((2048, dh), dtype=np.float32)
    cos[:, 0::2] = np.cos(f); cos[:, 1::2] = np.cos(f)
    sin[:, 0::2] = np.sin(f); sin[:, 1::2] = np.sin(f)
    return cos, sin


def host_inputs_l0attn(x_full, Wq, Wk, Wv, Wo):
    cos, sin = _rope_tables()
    wq0t = pack_lhsT(np.ascontiguousarray(Wq.T))
    wo0t = pack_lhsT(np.ascontiguousarray(Wo.T))
    wk0t = pack_lhsT(np.ascontiguousarray(Wk.T))
    wv0t = pack_lhsT(np.ascontiguousarray(Wv.T))
    lin = (np.arange(256, dtype=np.float32) / 255.0)
    maps = []
    for c in range(8):
        b, h = c // 2, c % 2
        xT = np.ascontiguousarray(x_full[b, h * T:(h + 1) * T, :].T)
        if h == 0:
            xhalo = x_full[b, 1024:1280, :]
        else:
            xhalo = x_full[b, 768:1024, :]
        xhaloT = np.ascontiguousarray(xhalo.T)
        lo = 1024 * h - 256
        extpos = np.clip(np.arange(1536) + lo, 0, 2047)
        cosk = np.ascontiguousarray(cos[extpos].T).astype(BF)
        sink = np.ascontiguousarray(sin[extpos].T).astype(BF)
        realc = np.zeros(12, dtype=np.float32)
        for ch in range(12):
            realc[ch] = 1.0 if (ch >= 2 if h == 0 else ch < 10) else 0.0
        denm = np.repeat(realc, 128)[None, :].repeat(128, axis=0).astype(BF)
        alA = np.zeros((4, 256), dtype=np.float32)
        alB = np.zeros((4, 256), dtype=np.float32)
        for blk in range(4):
            if h == 0 and blk == 0:
                alA[blk] = 1.0
                alB[blk] = 0.0
            else:
                alA[blk] = lin
                alB[blk] = 1.0 - lin
        alA_t = np.broadcast_to(alA.reshape(1, 4 * 256), (128, 1024)).astype(BF)
        alB_t = np.broadcast_to(alB.reshape(1, 4 * 256), (128, 1024)).astype(BF)
        hsel = np.zeros((128, 2), dtype=np.float32)
        hsel[:, 0] = 1.0 if h == 1 else 0.0
        hsel[:, 1] = 1.0 if h == 0 else 0.0
        maps.append({"xT": xT, "wq0t": wq0t, "wo0t": wo0t, "wk0t": wk0t,
                     "wv0t": wv0t, "xhaloT": xhaloT, "coskT": cosk, "sinkT": sink,
                     "denmI": np.ascontiguousarray(denm),
                     "alAI": np.ascontiguousarray(alA_t),
                     "alBI": np.ascontiguousarray(alB_t), "hselI": hsel})
    return maps


def host_inputs_l0ffn(x_full, c1_w, c2_w, w1, w2t):
    c1t = np.ascontiguousarray(c1_w.T.reshape(32, 128, 16).transpose(1, 0, 2))
    c2T = np.ascontiguousarray(c2_w.T)
    w1tt = pack_lhsT(np.ascontiguousarray(w1.T))
    w2tt = np.ascontiguousarray(
        w2t.reshape(4, 28, 128, 32, 128).transpose(0, 3, 2, 1, 4))
    maps = []
    for c in range(8):
        b, h = c // 2, c % 2
        xT = np.ascontiguousarray(x_full[b, h * T:(h + 1) * T, :].T)
        maps.append({"xT": xT, "c1tI": c1t, "c2TI": c2T, "w1tt": w1tt,
                     "w2tt": w2tt})
    return maps


def _assemble_xfull(res, key, featmaj):
    x = np.zeros((4, 2048, D), dtype=np.float32)
    for c in range(8):
        b, h = c // 2, c % 2
        o = res.results[c][key]
        x[b, h * T:(h + 1) * T, :] = o.T if featmaj else o
    return x


def host_inputs_fused(inputs, nonce):
    x = np.ascontiguousarray(inputs["L0_attn_x"], dtype=np.float32)
    m0 = host_inputs_l0attn(x, inputs["L0_attn_Wq"], inputs["L0_attn_Wk"],
                            inputs["L0_attn_Wv"], inputs["L0_attn_Wo"])
    m1 = host_inputs_l0ffn(x, inputs["L0_ffn_c1_w"], inputs["L0_ffn_c2_w"],
                           inputs["L0_ffn_w1"], inputs["L0_ffn_w2t"])
    m2 = host_inputs_l1attn(x, inputs["L1_attn_wq"], inputs["L1_attn_wk"],
                            inputs["L1_attn_wv"], inputs["L1_attn_wo"], nonce)
    m3 = host_inputs_l1ffn(x, inputs["L1_ffn_A"], inputs["L1_ffn_B"],
                           inputs["L1_ffn_Wr"])
    return [{**m0[c], **m1[c], **m2[c], **m3[c]} for c in range(8)]


def kernel(**inputs):
    nonce = int(np.random.default_rng().integers(1, 2 ** 30))
    nc = build_program(["l0attn", "l0ffn", "l1attn", "l1ffn"],
                       out_mode="tokmajor")
    maps = host_inputs_fused(inputs, nonce)
    res = run_bass_kernel_spmd(nc, maps, core_ids=list(range(8)))
    return _assemble_xfull(res, "out", False)


def kernel_unfused(**inputs):
    x = np.ascontiguousarray(inputs["L0_attn_x"], dtype=np.float32)
    cores = list(range(8))

    nc1 = build_program(["l0attn"], out_mode="xT")
    maps = host_inputs_l0attn(x, inputs["L0_attn_Wq"], inputs["L0_attn_Wk"],
                              inputs["L0_attn_Wv"], inputs["L0_attn_Wo"])
    res = run_bass_kernel_spmd(nc1, maps, core_ids=cores)
    x = _assemble_xfull(res, "outT", True)

    nc2 = build_program(["l0ffn"], out_mode="xT")
    maps = host_inputs_l0ffn(x, inputs["L0_ffn_c1_w"], inputs["L0_ffn_c2_w"],
                             inputs["L0_ffn_w1"], inputs["L0_ffn_w2t"])
    res = run_bass_kernel_spmd(nc2, maps, core_ids=cores)
    x = _assemble_xfull(res, "outT", True)

    nonce = int(np.random.default_rng().integers(1, 2 ** 30))
    nc3 = build_program(["l1attn"], out_mode="xT")
    maps = host_inputs_l1attn(x, inputs["L1_attn_wq"], inputs["L1_attn_wk"],
                              inputs["L1_attn_wv"], inputs["L1_attn_wo"], nonce)
    res = run_bass_kernel_spmd(nc3, maps, core_ids=cores)
    x = _assemble_xfull(res, "outT", True)

    nc4 = build_program(["l1ffn"], out_mode="tokmajor")
    maps = host_inputs_l1ffn(x, inputs["L1_ffn_A"], inputs["L1_ffn_B"],
                             inputs["L1_ffn_Wr"])
    res = run_bass_kernel_spmd(nc4, maps, core_ids=cores)
    return _assemble_xfull(res, "out", False)


# version 07
# speedup vs baseline: 1.1152x; 1.1152x over previous
import numpy as np
import ml_dtypes
import concourse.bass as bass
import concourse.mybir as mybir
from concourse.bass_utils import run_bass_kernel_spmd

F32 = mybir.dt.float32
BF16 = mybir.dt.bfloat16
I32 = mybir.dt.int32
AF = mybir.ActivationFunctionType
ALU = mybir.AluOpType
AX = mybir.AxisListType
BF = ml_dtypes.bfloat16

D = 4096
T = 1024
NT = 32
EPS = 1e-5


def pack_lhsT(W):
    K, M = W.shape
    w = W.reshape(K // 128, 128, M // 128, 128)
    return np.ascontiguousarray(w.transpose(2, 1, 0, 3))


class Ctx:
    def __init__(self, nc):
        self.nc = nc
        self.q = {"tensor": [], "vector": [], "scalar": [], "gpsimd": [], "sync": []}
        self.sem = {}
        self.cnt = {}
        self.buf = {}
        self.mark = {}
        self.static_loads = []

    def add_sem(self, name):
        self.sem[name] = self.nc.alloc_semaphore(name)
        self.cnt[name] = 0

    def inc(self, name, n=1):
        self.cnt[name] += n
        return self.cnt[name]

    def on(self, eng):
        def deco(f):
            self.q[eng].append(f)
        return deco

    def sbuf(self, name, shape, dtype):
        h = self.nc.alloc_sbuf_tensor(name, shape, dtype)
        self.buf[name] = h
        return h

    def init_arena(self):
        self.arena_base = (int(self.nc.sbuf_base) + 63) & ~63
        self.arena_off = 0
        self.arena_peak = 0

    def arena_reset(self):
        self.arena_off = 0

    def ssbuf(self, name, shape, dtype):
        nb = int(shape[1]) * (2 if dtype == BF16 else 4)
        off = self.arena_base + self.arena_off
        self.arena_off += (nb + 31) & ~31
        self.arena_peak = max(self.arena_peak, self.arena_off)
        h = self.nc.alloc_sbuf_tensor_at(name, shape, dtype, offset=off)
        self.buf[name] = h
        return h

    def stage_load_emit(self, pairs):
        tot = self.inc("sd", 16 * len(pairs))
        S = self.sem

        @self.on("gpsimd")
        def _(g):
            for dst, src in pairs:
                g.dma_start(out=dst, in_=src).then_inc(S["sd"], 16)
        return tot

    def psum(self, name, shape, dtype=F32):
        h = self.nc.alloc_psum_tensor(name, shape, dtype)
        self.buf[name] = h
        return h

    def static_load(self, dst_ap, src_ap):
        self.static_loads.append((dst_ap, src_ap))

    def sc_total(self):
        return 16 * len(self.static_loads)



def _emit_ln_finalize(ctx, stats_pe, first):
    xcen = ctx.buf["xcen"]
    psS, psM = ctx.buf["pA2"], ctx.buf["pB2"]
    dmu, var, r = ctx.buf["dmu"], ctx.buf["var"], ctx.buf["r"]
    dmub, mu_acc, tmpT = ctx.buf["dmub"], ctx.buf["mu_acc"], ctx.buf["tmpT"]
    S, V = ctx.sem, ctx.inc

    v_dmu = V("dv"); v_ex2 = V("dv"); v_d2 = V("dv"); v_var = V("dv")
    a_dmub = V("ac"); a_ln = V("ac"); a_r = V("ac"); a_rbf = V("ac")
    v_mu = V("dv")
    cent = [V("dv") for _ in range(NT)]

    @ctx.on("vector")
    def _(v):
        v.wait_ge(S["pe"], stats_pe)
        v.tensor_scalar(dmu[:], psM[:, 0:T], 1.0 / D, None, ALU.mult).then_inc(S["dv"], 1)
        v.tensor_scalar(var[:], psS[:, 0:T], 1.0 / D, None, ALU.mult).then_inc(S["dv"], 1)
        v.wait_ge(S["dv"], v_ex2)
        v.tensor_tensor(tmpT[:], dmu[:], dmu[:], ALU.mult).then_inc(S["dv"], 1)
        v.wait_ge(S["dv"], v_d2)
        v.tensor_tensor(var[:], var[:], tmpT[:], ALU.subtract).then_inc(S["dv"], 1)
        v.wait_ge(S["ac"], a_dmub)
        if first:
            v.tensor_copy(mu_acc[:], dmub[:]).then_inc(S["dv"], 1)
        else:
            v.tensor_tensor(mu_acc[:], mu_acc[:], dmub[:], ALU.add).then_inc(S["dv"], 1)
        v.wait_ge(S["dv"], v_mu)
        for i in range(NT):
            v.tensor_tensor(xcen[:, i * T:(i + 1) * T], xcen[:, i * T:(i + 1) * T],
                            dmub[:], ALU.subtract).then_inc(S["dv"], 1)

    @ctx.on("scalar")
    def _(a):
        a.wait_ge(S["dv"], v_dmu)
        a.activation(dmub[:], dmu[:], AF.Identity).then_inc(S["ac"], 1)
        a.wait_ge(S["dv"], v_var)
        a.activation(tmpT[:], var[:], AF.Ln, bias=ctx.buf["eps_sb"][:, 0:1]).then_inc(S["ac"], 1)
        a.wait_ge(S["ac"], a_ln)
        a.activation(r[:], tmpT[:], AF.Exp, scale=-0.5).then_inc(S["ac"], 1)
        a.wait_ge(S["ac"], a_r)
        a.activation(ctx.buf["rbf"][:], r[:], AF.Identity).then_inc(S["ac"], 1)

    ctx.mark["center_dv"] = cent[-1]
    ctx.mark["r_ac"] = a_rbf


def _emit_stats(ctx, entry_dv):
    xcen, ones = ctx.buf["xcen"], ctx.buf["ones128"]
    xsq = ctx.buf["scr2T"]
    psS, psM = ctx.buf["pA2"], ctx.buf["pB2"]
    S, V = ctx.sem, ctx.inc

    sq_done = [V("ac") for _ in range(NT)]
    sq_used = []
    pe_last = None
    for i in range(NT):
        pe_last = V("pe")
        sq_used.append(pe_last)

    @ctx.on("scalar")
    def _(a):
        if entry_dv is None:
            a.wait_ge(S["swa"], ctx.mark["xin_swa"])
            a.wait_ge(S["swb"], ctx.mark["xin_swb"])
        else:
            a.wait_ge(S["dv"], entry_dv)
        for i in range(NT):
            if i >= 2:
                a.wait_ge(S["pe"], sq_used[i - 2])
            a.activation(xsq[:, (i % 2) * T:(i % 2) * T + T],
                         xcen[:, i * T:(i + 1) * T], AF.Square).then_inc(S["ac"], 1)

    @ctx.on("tensor")
    def _(t):
        if entry_dv is None:
            t.wait_ge(S["swa"], ctx.mark["xin_swa"])
            t.wait_ge(S["swb"], ctx.mark["xin_swb"])
        else:
            t.wait_ge(S["dv"], entry_dv)
        for i in range(NT):
            t.wait_ge(S["ac"], sq_done[i])
            for half in range(2):
                sl = slice(half * 512, half * 512 + 512)
                xs = slice(i * T + half * 512, i * T + half * 512 + 512)
                qs = slice((i % 2) * T + half * 512, (i % 2) * T + half * 512 + 512)
                t.matmul(psM[:, sl], ones[:], xcen[:, xs], start=(i == 0), stop=(i == NT - 1))
                ins = t.matmul(psS[:, sl], ones[:], xsq[:, qs], start=(i == 0), stop=(i == NT - 1))
            ins.then_inc(S["pe"], 1)

    return pe_last


def emit_load_and_ln1(ctx, xT_h):
    xcen = ctx.buf["xcen"]
    S, V = ctx.sem, ctx.inc

    ld_done = [V("swa" if i % 2 == 0 else "swb", 16) for i in range(NT)]

    @ctx.on("gpsimd")
    def _(g):
        for i in range(NT):
            g.dma_start(out=xcen[:, i * T:(i + 1) * T],
                        in_=xT_h.ap()[i * 128:(i + 1) * 128, :]
                        ).then_inc(S["swa" if i % 2 == 0 else "swb"], 16)

    ctx.mark["xin_swa"] = ctx.cnt["swa"]
    ctx.mark["xin_swb"] = ctx.cnt["swb"]
    pe_last = _emit_stats(ctx, None)
    _emit_ln_finalize(ctx, pe_last, first=True)


def emit_ln(ctx, first=False):
    pe_last = _emit_stats(ctx, ctx.mark["xcen_done"])
    _emit_ln_finalize(ctx, pe_last, first=first)



def emit_bcast_to_tok(ctx, src_f32, dst_tok, wait_sem, wait_val):
    id_sb, pEb = ctx.buf["id_sb"], ctx.buf["pEb"]
    tmpb = ctx.buf["scr2T"][:, 0:T]
    S, V = ctx.sem, ctx.inc

    a_b = V("ac")
    pe_t = [V("pe") for _ in range(8)]
    v_t = [V("dv") for _ in range(8)]

    @ctx.on("scalar")
    def _(a):
        a.wait_ge(S[wait_sem], wait_val)
        a.activation(tmpb[:], src_f32[:], AF.Identity).then_inc(S["ac"], 1)

    @ctx.on("tensor")
    def _(t):
        t.wait_ge(S["ac"], a_b)
        for tc in range(8):
            if tc >= 2:
                t.wait_ge(S["dv"], v_t[tc - 2])
            t.transpose(pEb[:, (tc % 2) * T: (tc % 2) * T + 128],
                        tmpb[:, tc * 128:(tc + 1) * 128],
                        id_sb[:]).then_inc(S["pe"], 1)

    @ctx.on("vector")
    def _(v):
        for tc in range(8):
            v.wait_ge(S["pe"], pe_t[tc])
            v.tensor_copy(dst_tok[:, tc:tc + 1],
                          pEb[:, (tc % 2) * T:(tc % 2) * T + 1]).then_inc(S["dv"], 1)

    return v_t



def emit_l1_ffn(ctx, wrT_h, aT_h, bT_h):
    xcen, r = ctx.buf["xcen"], ctx.buf["r"]
    center_dv, r_ac = ctx.mark["center_dv"], ctx.mark["r_ac"]
    id_sb = ctx.buf["id_sb"]
    pA2, pB2, pC2, pEb = ctx.buf["pA2"], ctx.buf["pB2"], ctx.buf["pC2"], ctx.buf["pEb"]
    S, V = ctx.sem, ctx.inc

    wr_sb = ctx.ssbuf("wr_sb", [128, 32 * 4], BF16)
    aT_sb = ctx.ssbuf("aT_sb", [128, 32 * 64], BF16)
    bT_sb = ctx.ssbuf("bT_sb", [64, D], BF16)
    lg = ctx.ssbuf("lg_sb", [128, 8 * 4], F32)
    et = ctx.ssbuf("et_sb", [128, 8 * 4], F32)
    sel = ctx.ssbuf("sel_sb", [128, 8 * 4], F32)
    m1 = ctx.ssbuf("m1_sb", [128, 8], F32)
    nm1 = ctx.ssbuf("nm1_sb", [128, 8], F32)
    m2 = ctx.ssbuf("m2_sb", [128, 8], F32)
    gsc = ctx.ssbuf("gsc_sb", [128, 8], F32)
    gates = ctx.ssbuf("gates_sb", [128, 8 * 4], BF16)
    gatesT = ctx.ssbuf("gatesT_sb", [4, T], BF16)
    ghat = ctx.ssbuf("ghat_sb", [64, T], BF16)
    midT = ctx.ssbuf("midT_sb", [64, T], BF16)
    gmidT = ctx.ssbuf("gmidT_sb", [64, T], BF16)
    rtok = ctx.ssbuf("rtok_sb", [128, 8], F32)

    sd_tot = ctx.stage_load_emit([
        (wr_sb[:].rearrange("p (a f) -> p a f", f=4),
         wrT_h.ap().rearrange("(a p) f -> p a f", p=128)),
        (aT_sb[:].rearrange("p (a f) -> p a f", f=64),
         aT_h.ap().rearrange("(a p) f -> p a f", p=128)),
        (bT_sb[:], bT_h.ap())])

    v_rtok = emit_bcast_to_tok(ctx, r, rtok, "ac", r_ac)
    ctx.mark["rtok_done"] = v_rtok[-1]

    lg_pe = []
    v_lg = []
    for tc in range(8):
        lg_pe.append(V("pe"))
    gate_dv = []
    for tc in range(8):
        v_lg.append(V("dv"))
        _v = {}
        for k in ["m1", "nm1", "sel1", "l2", "m2", "d21", "p1", "rv", "sel", "tt", "g"]:
            _v[k] = V("dv")
        _a = {"e21": V("ac"), "et": V("ac")}
        gate_dv.append((_v, _a))

    @ctx.on("tensor")
    def _(t):
        t.wait_ge(S["sd"], sd_tot)
        t.wait_ge(S["dv"], center_dv)
        for tc in range(8):
            if tc >= 2:
                t.wait_ge(S["dv"], v_lg[tc - 2])
            psl = (pB2 if tc % 2 == 0 else pC2)[:, 0:4]
            for dc in range(32):
                ins = t.matmul(psl,
                               xcen[:, dc * T + tc * 128: dc * T + tc * 128 + 128],
                               wr_sb[:, dc * 4: dc * 4 + 4],
                               start=(dc == 0), stop=(dc == 31))
            ins.then_inc(S["pe"], 1)

    for tc in range(8):
        sl4 = slice(tc * 4, tc * 4 + 4)
        sl1 = slice(tc, tc + 1)
        _v, _a = gate_dv[tc]

        @ctx.on("vector")
        def _(v, tc=tc, sl4=sl4, sl1=sl1, _v=_v, _a=_a):
            v.wait_ge(S["pe"], lg_pe[tc])
            v.wait_ge(S["dv"], v_rtok[tc])
            psl = (pB2 if tc % 2 == 0 else pC2)[:, 0:4]
            v.tensor_scalar(lg[:, sl4], psl, rtok[:, sl1], None, ALU.mult).then_inc(S["dv"], 1)
            v.wait_ge(S["dv"], v_lg[tc])
            v.reduce_max(m1[:, sl1].unsqueeze(2), lg[:, sl4].unsqueeze(1), axis=AX.X).then_inc(S["dv"], 1)
            v.wait_ge(S["dv"], _v["m1"])
            v.tensor_scalar(nm1[:, sl1], m1[:, sl1], -1.0, None, ALU.mult).then_inc(S["dv"], 1)
            v.wait_ge(S["dv"], _v["nm1"])
            v.tensor_tensor(sel[:, sl4].unsqueeze(1), lg[:, sl4].unsqueeze(1),
                            m1[:, sl1].unsqueeze(2).broadcast_to((128, 1, 4)),
                            ALU.is_ge).then_inc(S["dv"], 1)
            v.wait_ge(S["dv"], _v["sel1"])
            v.scalar_tensor_tensor(et[:, sl4], sel[:, sl4], -1e30, lg[:, sl4],
                                   ALU.mult, ALU.add).then_inc(S["dv"], 1)
            v.wait_ge(S["dv"], _v["l2"])
            v.reduce_max(m2[:, sl1].unsqueeze(2), et[:, sl4].unsqueeze(1), axis=AX.X).then_inc(S["dv"], 1)
            v.wait_ge(S["dv"], _v["m2"])
            v.tensor_tensor(gsc[:, sl1], m2[:, sl1], m1[:, sl1], ALU.subtract).then_inc(S["dv"], 1)
            v.wait_ge(S["ac"], _a["e21"])
            v.tensor_scalar(gsc[:, sl1], gsc[:, sl1], 1.0, None, ALU.add).then_inc(S["dv"], 1)
            v.wait_ge(S["dv"], _v["p1"])
            v.reciprocal(gsc[:, sl1], gsc[:, sl1]).then_inc(S["dv"], 1)
            v.wait_ge(S["dv"], _v["rv"])
            v.tensor_tensor(sel[:, sl4].unsqueeze(1), lg[:, sl4].unsqueeze(1),
                            m2[:, sl1].unsqueeze(2).broadcast_to((128, 1, 4)),
                            ALU.is_ge).then_inc(S["dv"], 1)
            v.wait_ge(S["ac"], _a["et"])
            v.wait_ge(S["dv"], _v["sel"])
            v.tensor_tensor(et[:, sl4], et[:, sl4], sel[:, sl4], ALU.mult).then_inc(S["dv"], 1)
            v.wait_ge(S["dv"], _v["tt"])
            v.tensor_scalar(gates[:, sl4], et[:, sl4], gsc[:, sl1], None, ALU.mult).then_inc(S["dv"], 1)

        @ctx.on("scalar")
        def _(a, tc=tc, sl4=sl4, sl1=sl1, _v=_v, _a=_a):
            a.wait_ge(S["dv"], _v["d21"])
            a.activation(gsc[:, sl1], gsc[:, sl1], AF.Exp).then_inc(S["ac"], 1)
            a.wait_ge(S["dv"], _v["rv"])
            a.activation(et[:, sl4], lg[:, sl4], AF.Exp, bias=nm1[:, sl1]).then_inc(S["ac"], 1)

    pe_mid = V("pe")

    @ctx.on("tensor")
    def _(t):
        t.wait_ge(S["sd"], sd_tot)
        t.wait_ge(S["dv"], center_dv)
        for dc in range(32):
            for half in range(2):
                ins = t.matmul(pA2[0:64, half * 512: half * 512 + 512],
                               aT_sb[:, dc * 64:(dc + 1) * 64],
                               xcen[:, dc * T + half * 512: dc * T + half * 512 + 512],
                               start=(dc == 0), stop=(dc == 31))
        ins.then_inc(S["pe"], 1)

    v_mid = V("dv")

    @ctx.on("vector")
    def _(v):
        v.wait_ge(S["pe"], pe_mid)
        v.wait_ge(S["ac"], r_ac)
        v.tensor_tensor(midT[:], pA2[0:64, 0:T], r[0:64, :], ALU.mult).then_inc(S["dv"], 1)

    gT_pe = [V("pe") for _ in range(8)]
    v_gT = [V("dv") for _ in range(8)]

    @ctx.on("tensor")
    def _(t):
        for tc in range(8):
            t.wait_ge(S["dv"], gate_dv[tc][0]["g"])
            if tc >= 2:
                t.wait_ge(S["dv"], v_gT[tc - 2])
            t.transpose(pEb[0:4, (tc % 2) * T:(tc % 2) * T + 128],
                        gates[:, tc * 4:(tc + 1) * 4], id_sb[:]).then_inc(S["pe"], 1)

    @ctx.on("vector")
    def _(v):
        for tc in range(8):
            v.wait_ge(S["pe"], gT_pe[tc])
            v.tensor_copy(gatesT[:, tc * 128:(tc + 1) * 128],
                          pEb[0:4, (tc % 2) * T:(tc % 2) * T + 128]).then_inc(S["dv"], 1)

    pe_ghat = V("pe")

    @ctx.on("tensor")
    def _(t):
        for tc in range(8):
            t.wait_ge(S["dv"], v_gT[tc])
        for half in range(2):
            ins = t.matmul(pC2[0:64, half * 512: half * 512 + 512],
                           ctx.buf["rep_sb"][:], gatesT[:, half * 512: half * 512 + 512],
                           start=True, stop=True)
        ins.then_inc(S["pe"], 1)

    a_ghat = V("ac")

    @ctx.on("scalar")
    def _(a):
        a.wait_ge(S["pe"], pe_ghat)
        a.activation(ghat[:], pC2[0:64, 0:T], AF.Identity).then_inc(S["ac"], 1)

    v_gmid = V("dv")

    @ctx.on("vector")
    def _(v):
        v.wait_ge(S["ac"], a_ghat)
        v.wait_ge(S["dv"], v_mid)
        v.tensor_tensor(gmidT[:], midT[:], ghat[:], ALU.mult).then_inc(S["dv"], 1)

    res_dv = []
    for oc in range(32):
        pe_oc = V("pe")
        v_oc = V("dv")

        @ctx.on("tensor")
        def _(t, oc=oc):
            if oc == 0:
                t.wait_ge(S["dv"], v_gmid)
            if oc >= 2:
                t.wait_ge(S["dv"], res_dv[oc - 2])
            ps = pA2 if oc % 2 == 0 else pC2
            for half in range(2):
                ins = t.matmul(ps[:, half * 512: half * 512 + 512],
                               bT_sb[:, oc * 128:(oc + 1) * 128],
                               gmidT[:, half * 512: half * 512 + 512],
                               start=True, stop=True)
            ins.then_inc(S["pe"], 1)

        @ctx.on("vector")
        def _(v, oc=oc, pe_oc=pe_oc):
            v.wait_ge(S["pe"], pe_oc)
            ps = pA2 if oc % 2 == 0 else pC2
            v.tensor_tensor(xcen[:, oc * T:(oc + 1) * T], xcen[:, oc * T:(oc + 1) * T],
                            ps[:, 0:T], ALU.add).then_inc(S["dv"], 1)
        res_dv.append(v_oc)

    ctx.mark["xcen_done"] = res_dv[-1]
    ctx.mark["res_list"] = res_dv






def emit_l0_attn(ctx, wq_h, wo_h, wk_h, wv_h, xhalo_h, ck_h, sk_h,
                 denm_h, alA_h, alB_h, hsel_h):
    nc = ctx.nc
    xcen, r = ctx.buf["xcen"], ctx.buf["r"]
    ones, id_sb = ctx.buf["ones128"], ctx.buf["id_sb"]
    pA2, pB2, pC2, pEb = ctx.buf["pA2"], ctx.buf["pB2"], ctx.buf["pC2"], ctx.buf["pEb"]
    wstr = ctx.buf["wstr"]
    prott = ctx.buf["prott_sb"]
    tmpT = ctx.buf["tmpT"]
    S, V = ctx.sem, ctx.inc
    center_dv, r_ac = ctx.mark["center_dv"], ctx.mark["r_ac"]

    QA = ctx.ssbuf("QA_sb", [128, 32 * T], BF16)
    xhal = ctx.ssbuf("xhal_sb", [128, 2 * 256], BF16)
    hsq = ctx.ssbuf("hsq_sb", [128, 2 * 256], BF16)
    rh = ctx.ssbuf("rh_sb", [128, 256], F32)
    dmuh = ctx.ssbuf("dmuh_sb", [128, 256], BF16)
    KxT = ctx.ssbuf("KxT_sb", [128, 12 * 128], BF16)
    kraw = ctx.ssbuf("kraw_sb", [128, 12 * 128], BF16)
    Vx = kraw
    vraw = ctx.ssbuf("vraw_sb", [128, 1280], BF16)
    cosk = ctx.ssbuf("cosk_sb", [128, 1536], BF16)
    sink = ctx.ssbuf("sink_sb", [128, 1536], BF16)
    denm = ctx.ssbuf("denm_sb", [128, 12 * 128], BF16)
    alA = ctx.ssbuf("alA_sb", [128, 4 * 256], BF16)
    alB = ctx.ssbuf("alB_sb", [128, 4 * 256], BF16)
    hsel = ctx.ssbuf("hsel_sb", [128, 2], F32)
    qtmp = ctx.ssbuf("qtmp_sb", [128, 2 * T], BF16)
    kcs = qtmp[:, 0:1536]
    P0 = ctx.buf["scr2T"]
    rbw = ctx.buf["oscr"]
    batmp = hsq
    cosq = cosk[:, 256:1280]
    sinq = sink[:, 256:1280]

    ctx.static_load(cosk[:], ck_h.ap())
    ctx.static_load(sink[:], sk_h.ap())
    ctx.static_load(denm[:], denm_h.ap())
    ctx.static_load(alA[:], alA_h.ap())
    ctx.static_load(alB[:], alB_h.ap())
    ctx.static_load(hsel[:], hsel_h.ap())

    wk_ld = V("ga", 16)
    wv_ld = V("gb", 16)

    @ctx.on("gpsimd")
    def _(g):
        g.dma_start(out=wstr[:, 0:4096].rearrange("p (a f) -> p a f", f=128),
                    in_=wk_h.ap()[0]).then_inc(S["ga"], 16)
        g.dma_start(out=wstr[:, 4096:8192].rearrange("p (a f) -> p a f", f=128),
                    in_=wv_h.ap()[0]).then_inc(S["gb"], 16)

    h1_ld = [V("ha" if i % 2 == 0 else "hb", 16) for i in range(32)]
    hsq_ac = [V("ac") for _ in range(32)]
    hst_pe = [V("pe") for _ in range(32)]

    @ctx.on("gpsimd")
    def _(g):
        for i in range(32):
            if i >= 2:
                g.wait_ge(S["pe"], hst_pe[i - 2])
            g.dma_start(out=xhal[:, (i % 2) * 256:(i % 2) * 256 + 256],
                        in_=xhalo_h.ap()[i * 128:(i + 1) * 128, :]
                        ).then_inc(S["ha" if i % 2 == 0 else "hb"], 16)

    @ctx.on("scalar")
    def _(a):
        for i in range(32):
            a.wait_ge(S["ha" if i % 2 == 0 else "hb"], h1_ld[i])
            if i >= 2:
                a.wait_ge(S["pe"], hst_pe[i - 2])
            a.activation(hsq[:, (i % 2) * 256:(i % 2) * 256 + 256],
                         xhal[:, (i % 2) * 256:(i % 2) * 256 + 256],
                         AF.Square).then_inc(S["ac"], 1)

    @ctx.on("tensor")
    def _(t):
        for i in range(32):
            t.wait_ge(S["ac"], hsq_ac[i])
            t.matmul(pC2[:, 0:256], ones[:], xhal[:, (i % 2) * 256:(i % 2) * 256 + 256],
                     start=(i == 0), stop=(i == 31))
            ins = t.matmul(pC2[:, 512:768], ones[:],
                           hsq[:, (i % 2) * 256:(i % 2) * 256 + 256],
                           start=(i == 0), stop=(i == 31))
            ins.then_inc(S["pe"], 1)

    vh = {k: V("dv") for k in ["dmu", "ex2", "d2", "var"]}
    ah = {k: V("ac") for k in ["dmub", "ln", "r"]}

    @ctx.on("vector")
    def _(v):
        v.wait_ge(S["pe"], hst_pe[31])
        v.tensor_scalar(tmpT[:, 0:256], pC2[:, 0:256], 1.0 / D, None,
                        ALU.mult).then_inc(S["dv"], 1)
        v.tensor_scalar(rh[:], pC2[:, 512:768], 1.0 / D, None, ALU.mult).then_inc(S["dv"], 1)
        v.wait_ge(S["dv"], vh["ex2"])
        v.tensor_tensor(tmpT[:, 256:512], tmpT[:, 0:256], tmpT[:, 0:256],
                        ALU.mult).then_inc(S["dv"], 1)
        v.wait_ge(S["dv"], vh["d2"])
        v.tensor_tensor(rh[:], rh[:], tmpT[:, 256:512], ALU.subtract).then_inc(S["dv"], 1)

    @ctx.on("scalar")
    def _(a):
        a.wait_ge(S["dv"], vh["dmu"])
        a.activation(dmuh[:], tmpT[:, 0:256], AF.Identity).then_inc(S["ac"], 1)
        a.wait_ge(S["dv"], vh["var"])
        a.activation(tmpT[:, 256:512], rh[:], AF.Ln,
                     bias=ctx.buf["eps_sb"][:, 0:1]).then_inc(S["ac"], 1)
        a.wait_ge(S["ac"], ah["ln"])
        a.activation(rh[:], tmpT[:, 256:512], AF.Exp, scale=-0.5).then_inc(S["ac"], 1)

    kvo = V("pe")
    kvv = V("pe")

    @ctx.on("tensor")
    def _(t):
        t.wait_ge(S["dv"], center_dv)
        t.wait_ge(S["ga"], wk_ld)
        for dc in range(32):
            for half in range(2):
                ins = t.matmul(pA2[:, half * 512: half * 512 + 512],
                               wstr[:, dc * 128:(dc + 1) * 128],
                               xcen[:, dc * T + half * 512: dc * T + half * 512 + 512],
                               start=(dc == 0), stop=(dc == 31))
        ins.then_inc(S["pe"], 1)
        t.wait_ge(S["gb"], wv_ld)
        t.wait_ge(S["dv"], vh["ex2"])
        for dc in range(32):
            for half in range(2):
                ins = t.matmul(pC2[:, half * 512: half * 512 + 512],
                               wstr[:, 4096 + dc * 128: 4096 + (dc + 1) * 128],
                               xcen[:, dc * T + half * 512: dc * T + half * 512 + 512],
                               start=(dc == 0), stop=(dc == 31))
        ins.then_inc(S["pe"], 1)

    h2_ld = [V("ha" if i % 2 == 0 else "hb", 16) for i in range(32)]
    h2_cen = [V("dv") for _ in range(32)]
    h2_pe = [V("pe") for _ in range(32)]

    @ctx.on("gpsimd")
    def _(g):
        for i in range(32):
            if i < 2:
                g.wait_ge(S["pe"], hst_pe[30 + i])
            else:
                g.wait_ge(S["pe"], h2_pe[i - 2])
            g.dma_start(out=xhal[:, (i % 2) * 256:(i % 2) * 256 + 256],
                        in_=xhalo_h.ap()[i * 128:(i + 1) * 128, :]
                        ).then_inc(S["ha" if i % 2 == 0 else "hb"], 16)

    @ctx.on("vector")
    def _(v):
        v.wait_ge(S["ac"], ah["dmub"])
        for i in range(32):
            v.wait_ge(S["ha" if i % 2 == 0 else "hb"], h2_ld[i])
            v.tensor_tensor(hsq[:, (i % 2) * 256:(i % 2) * 256 + 256],
                            xhal[:, (i % 2) * 256:(i % 2) * 256 + 256],
                            dmuh[:], ALU.subtract).then_inc(S["dv"], 1)

    @ctx.on("tensor")
    def _(t):
        for i in range(32):
            t.wait_ge(S["dv"], h2_cen[i])
            t.matmul(pB2[:, 0:256], wstr[:, i * 128:(i + 1) * 128],
                     hsq[:, (i % 2) * 256:(i % 2) * 256 + 256],
                     start=(i == 0), stop=(i == 31))
            ins = t.matmul(pB2[:, 512:768], wstr[:, 4096 + i * 128: 4096 + (i + 1) * 128],
                           hsq[:, (i % 2) * 256:(i % 2) * 256 + 256],
                           start=(i == 0), stop=(i == 31))
            ins.then_inc(S["pe"], 1)

    halo_pe = h2_pe[31]

    kv_dv = {k: V("dv") for k in ["ko", "vo", "khs", "klo", "khi", "vhs"]}

    @ctx.on("vector")
    def _(v):
        v.wait_ge(S["ac"], r_ac)
        v.wait_ge(S["ac"], ah["r"])
        v.wait_ge(S["pe"], kvo)
        v.tensor_tensor(kraw[:, 256:1280], pA2[:, 0:T], r[:], ALU.mult).then_inc(S["dv"], 1)
        v.wait_ge(S["pe"], kvv)
        v.tensor_tensor(vraw[:, 0:T], pC2[:, 0:T], r[:], ALU.mult).then_inc(S["dv"], 1)
        v.wait_ge(S["pe"], halo_pe)
        v.tensor_tensor(rbw[:, 0:256], pB2[:, 0:256], rh[:], ALU.mult).then_inc(S["dv"], 1)
        v.wait_ge(S["dv"], kv_dv["khs"])
        v.tensor_scalar(kraw[:, 0:256], rbw[:, 0:256], hsel[:, 0:1], None,
                        ALU.mult).then_inc(S["dv"], 1)
        v.tensor_scalar(kraw[:, 1280:1536], rbw[:, 0:256], hsel[:, 1:2], None,
                        ALU.mult).then_inc(S["dv"], 1)
        v.tensor_tensor(vraw[:, T:1280], pB2[:, 512:768], rh[:], ALU.mult).then_inc(S["dv"], 1)

    krot = [V("pe") for _ in range(3)]
    kct = [V("po") for _ in range(3)]
    kt1 = []
    kx = []
    for c in range(3):
        kt1.append(V("dv"))
        kx.append(V("dv"))

    @ctx.on("tensor")
    def _(t):
        t.wait_ge(S["dv"], kv_dv["vhs"])
        for c in range(3):
            if c == 2:
                t.wait_ge(S["dv"], kt1[0])
            t.matmul(pB2[:, (c % 2) * 512:(c % 2) * 512 + 512], prott[:],
                     kraw[:, c * 512:(c + 1) * 512], start=True,
                     stop=True).then_inc(S["pe"], 1)

    @ctx.on("gpsimd")
    def _(g):
        g.wait_ge(S["dv"], kv_dv["vhs"])
        for c in range(3):
            g.tensor_tensor(kcs[:, c * 512:(c + 1) * 512], kraw[:, c * 512:(c + 1) * 512],
                            cosk[:, c * 512:(c + 1) * 512], ALU.mult).then_inc(S["po"], 1)

    @ctx.on("vector")
    def _(v):
        for c in range(3):
            v.wait_ge(S["pe"], krot[c])
            v.tensor_tensor(P0[:, (c % 2) * 512:(c % 2) * 512 + 512],
                            pB2[:, (c % 2) * 512:(c % 2) * 512 + 512],
                            sink[:, c * 512:(c + 1) * 512], ALU.mult).then_inc(S["dv"], 1)
            v.wait_ge(S["po"], kct[c])
            v.wait_ge(S["dv"], kt1[c])
            v.tensor_tensor(KxT[:, c * 512:(c + 1) * 512],
                            P0[:, (c % 2) * 512:(c % 2) * 512 + 512],
                            kcs[:, c * 512:(c + 1) * 512], ALU.add).then_inc(S["dv"], 1)

    vt_pe = [V("pe") for _ in range(12)]
    vt_ac = [V("ac") for _ in range(12)]
    vsel = [V("dv") for _ in range(4)]

    @ctx.on("tensor")
    def _(t):
        t.wait_ge(S["dv"], kx[2])
        for j in range(12):
            if 2 <= j < 10:
                src = vraw[:, (j - 2) * 128:(j - 1) * 128]
            else:
                src = vraw[:, T + (j % 2) * 128: T + (j % 2) * 128 + 128]
            if j >= 2:
                t.wait_ge(S["ac"], vt_ac[j - 2])
            t.transpose(pEb[:, (j % 2) * T:(j % 2) * T + 128], src,
                        id_sb[:]).then_inc(S["pe"], 1)

    @ctx.on("scalar")
    def _(a):
        for j in range(12):
            a.wait_ge(S["pe"], vt_pe[j])
            if j < 2:
                a.wait_ge(S["dv"], kx[2])
            a.activation(Vx[:, j * 128:(j + 1) * 128],
                         pEb[:, (j % 2) * T:(j % 2) * T + 128],
                         AF.Identity).then_inc(S["ac"], 1)

    @ctx.on("vector")
    def _(v):
        for n, j in enumerate([0, 1, 10, 11]):
            v.wait_ge(S["ac"], vt_ac[j])
            v.tensor_scalar(Vx[:, j * 128:(j + 1) * 128], Vx[:, j * 128:(j + 1) * 128],
                            hsel[:, 0:1] if j < 2 else hsel[:, 1:2], None,
                            ALU.mult).then_inc(S["dv"], 1)

    wq_ld = [V(("ga" if i % 2 == 0 else "gb"), 16) for i in range(32)]
    hm_list = []
    for head in range(32):
        hm_list.append({"qpe": V("pe"), "rot1": V("pe"), "rot2": V("pe")})
    for head in range(32):
        hm = hm_list[head]
        hm["qev"] = V("dv")
        hm["t1a"] = V("dv")
        hm["t1b"] = V("dv")
        hm["qra"] = V("dv")
        hm["qrb"] = V("dv")
    for head in range(32):
        hm_list[head]["t2"] = V("po")

    @ctx.on("gpsimd")
    def _(g):
        for head in range(-2, 32):
            if head >= 0:
                hm = hm_list[head]
                g.wait_ge(S["dv"], hm["qev"])
                if head >= 1:
                    g.wait_ge(S["dv"], hm_list[head - 1]["qrb"])
                g.tensor_tensor(P0[:, 512:1024],
                                qtmp[:, (head % 2) * T:(head % 2) * T + 512],
                                cosq[:, 0:512], ALU.mult)
                g.tensor_tensor(P0[:, 1536:2048],
                                qtmp[:, (head % 2) * T + 512:(head % 2) * T + T],
                                cosq[:, 512:1024], ALU.mult).then_inc(S["po"], 1)
            mc = head + 2
            if 0 <= mc < 32:
                if mc == 0:
                    g.wait_ge(S["pe"], kvo)
                    g.wait_ge(S["pe"], halo_pe)
                elif mc == 1:
                    g.wait_ge(S["pe"], kvv)
                    g.wait_ge(S["pe"], halo_pe)
                else:
                    g.wait_ge(S["pe"], hm_list[mc - 2]["qpe"])
                g.dma_start(out=wstr[:, (mc % 2) * 4096:(mc % 2) * 4096 + 4096]
                            .rearrange("p (a f) -> p a f", f=128),
                            in_=wq_h.ap()[mc]).then_inc(S["ga" if mc % 2 == 0 else "gb"], 16)

    @ctx.on("tensor")
    def _(t):
        for head in range(32):
            hm = hm_list[head]
            t.wait_ge(S["ga" if head % 2 == 0 else "gb"], wq_ld[head])
            if head == 0:
                t.wait_ge(S["dv"], kv_dv["klo"])
            elif head == 1:
                t.wait_ge(S["dv"], kv_dv["vo"])
            else:
                t.wait_ge(S["dv"], hm_list[head - 2]["qev"])
            ps = pA2 if head % 2 == 0 else pC2
            for dc in range(32):
                for half in range(2):
                    ins = t.matmul(ps[:, half * 512: half * 512 + 512],
                                   wstr[:, (head % 2) * 4096 + dc * 128:(head % 2) * 4096 + (dc + 1) * 128],
                                   xcen[:, dc * T + half * 512: dc * T + half * 512 + 512],
                                   start=(dc == 0), stop=(dc == 31))
            ins.then_inc(S["pe"], 1)
            t.wait_ge(S["dv"], hm["qev"])
            if head == 0:
                t.wait_ge(S["ac"], vt_ac[11])
                t.wait_ge(S["dv"], kx[2])
            else:
                t.wait_ge(S["dv"], hm_list[head - 1]["t1a"])
            t.matmul(pB2[:, 0:512], prott[:],
                     qtmp[:, (head % 2) * T:(head % 2) * T + 512],
                     start=True, stop=True).then_inc(S["pe"], 1)
            if head >= 1:
                t.wait_ge(S["dv"], hm_list[head - 1]["t1b"])
            t.matmul(pB2[:, 512:1024], prott[:],
                     qtmp[:, (head % 2) * T + 512:(head % 2) * T + T],
                     start=True, stop=True).then_inc(S["pe"], 1)

    @ctx.on("vector")
    def _(v):
        for head in range(32):
            hm = hm_list[head]
            v.wait_ge(S["pe"], hm["qpe"])
            if head >= 2:
                v.wait_ge(S["po"], hm_list[head - 2]["t2"])
            else:
                v.wait_ge(S["po"], kct[2])
                v.wait_ge(S["dv"], kx[2])
            ps = pA2 if head % 2 == 0 else pC2
            v.tensor_tensor(qtmp[:, (head % 2) * T:(head % 2) * T + T],
                            ps[:, 0:T], r[:], ALU.mult).then_inc(S["dv"], 1)
            v.wait_ge(S["pe"], hm["rot1"])
            if head >= 1:
                v.wait_ge(S["dv"], hm_list[head - 1]["qra"])
            v.tensor_tensor(P0[:, 0:512], pB2[:, 0:512],
                            sinq[:, 0:512], ALU.mult).then_inc(S["dv"], 1)
            v.wait_ge(S["pe"], hm["rot2"])
            if head >= 1:
                v.wait_ge(S["dv"], hm_list[head - 1]["qrb"])
            v.tensor_tensor(P0[:, T:T + 512], pB2[:, 512:1024],
                            sinq[:, 512:1024], ALU.mult).then_inc(S["dv"], 1)
            v.wait_ge(S["po"], hm["t2"])
            v.wait_ge(S["dv"], hm["t1b"])
            v.tensor_tensor(QA[:, head * T: head * T + 512],
                            P0[:, 0:512], P0[:, 512:1024], ALU.add).then_inc(S["dv"], 1)
            v.wait_ge(S["dv"], hm["qra"])
            v.tensor_tensor(QA[:, head * T + 512: head * T + T],
                            P0[:, T:T + 512], P0[:, 1536:2048], ALU.add).then_inc(S["dv"], 1)

    qa_done = hm_list[31]["qrb"]

    wo_ld = [V(("ga" if i % 2 == 0 else "gb"), 16) for i in range(2)]

    @ctx.on("gpsimd")
    def _(g):
        for oc in range(2):
            g.wait_ge(S["pe"], hm_list[30 + oc]["qpe"])
            g.dma_start(out=wstr[:, (oc % 2) * 4096:(oc % 2) * 4096 + 4096]
                        .rearrange("p (a f) -> p a f", f=128),
                        in_=wo_h.ap()[oc]).then_inc(S["ga" if oc % 2 == 0 else "gb"], 16)

    NW = 32 * 4 * 8
    spe = [None] * NW
    dpe = [None] * NW
    sig = [None] * NW
    for it in range(NW + 1):
        if it < NW:
            spe[it] = V("pe")
        if it >= 1:
            dpe[it - 1] = V("pe")
    wmk = [[{} for _ in range(4)] for _ in range(32)]
    for head in range(32):
        for blk in range(4):
            bm = wmk[head][blk]
            bm["rb1"] = V("dv")
            bm["rb2"] = V("dv")
            bm["w1"] = V("dv")
            bm["w2"] = V("dv")
            bm["att"] = V("dv")
            bm["po1"] = V("po")
            bm["po2"] = V("po")
    for u in range(NW):
        sig[u] = V("ac")

    def _widx(u):
        head = u // 32
        blk = (u // 8) % 4
        kidx = u % 8
        return head, blk, kidx

    @ctx.on("tensor")
    def _(t):
        t.wait_ge(S["dv"], qa_done)
        for it in range(NW + 1):
            if it < NW:
                u = it
                head, blk, kidx = _widx(u)
                wsel, kc = kidx // 4, kidx % 4
                c = 2 * blk + 2 * wsel + kc
                if u >= 2:
                    t.wait_ge(S["ac"], sig[u - 2])
                t.matmul(pA2[:, (u % 2) * 512:(u % 2) * 512 + 256],
                         KxT[:, c * 128:(c + 1) * 128],
                         QA[:, head * T + blk * 256: head * T + blk * 256 + 256],
                         start=True, stop=True).then_inc(S["pe"], 1)
            if it >= 1:
                u = it - 1
                head, blk, kidx = _widx(u)
                wsel, kc = kidx // 4, kidx % 4
                c = 2 * blk + 2 * wsel + kc
                t.wait_ge(S["ac"], sig[u])
                if kidx == 0 and u >= 8:
                    t.wait_ge(S["dv"], wmk[(u - 8) // 32][((u - 8) // 8) % 4]["att"])
                t.matmul(pB2[:, wsel * 512: wsel * 512 + 256],
                         denm[:, c * 128:(c + 1) * 128],
                         P0[:, (u % 2) * T:(u % 2) * T + 256],
                         start=(kc == 0), stop=(kc == 3))
                t.matmul(pC2[:, wsel * 512: wsel * 512 + 256],
                         Vx[:, c * 128:(c + 1) * 128],
                         P0[:, (u % 2) * T:(u % 2) * T + 256],
                         start=(kc == 0), stop=(kc == 3)).then_inc(S["pe"], 1)

    @ctx.on("scalar")
    def _(a):
        for u in range(NW):
            a.wait_ge(S["pe"], spe[u])
            if u >= 2:
                a.wait_ge(S["pe"], dpe[u - 2])
            elif u == 0:
                a.wait_ge(S["dv"], qa_done)
            a.activation(P0[:, (u % 2) * T:(u % 2) * T + 256],
                         pA2[:, (u % 2) * 512:(u % 2) * 512 + 256],
                         AF.Sigmoid, scale=SCALE128).then_inc(S["ac"], 1)

    @ctx.on("vector")
    def _(v):
        for head in range(32):
            for blk in range(4):
                bm = wmk[head][blk]
                base = (head * 4 + blk) * 8
                v.wait_ge(S["pe"], dpe[base + 3])
                v.reciprocal(rbw[:, 0:256], pB2[:, 0:256]).then_inc(S["dv"], 1)
                v.wait_ge(S["pe"], dpe[base + 7])
                v.reciprocal(rbw[:, 256:512], pB2[:, 512:768]).then_inc(S["dv"], 1)
                v.wait_ge(S["po"], bm["po1"])
                v.tensor_tensor(batmp[:, 0:256], pC2[:, 0:256],
                                rbw[:, 512:768], ALU.mult).then_inc(S["dv"], 1)
                v.wait_ge(S["po"], bm["po2"])
                v.tensor_tensor(batmp[:, 256:512], pC2[:, 512:768],
                                rbw[:, 768:1024], ALU.mult).then_inc(S["dv"], 1)
                v.wait_ge(S["dv"], bm["w2"])
                v.tensor_tensor(QA[:, head * T + blk * 256: head * T + blk * 256 + 256],
                                batmp[:, 0:256], batmp[:, 256:512],
                                ALU.add).then_inc(S["dv"], 1)

    @ctx.on("gpsimd")
    def _(g):
        for head in range(32):
            for blk in range(4):
                bm = wmk[head][blk]
                g.wait_ge(S["dv"], bm["rb1"])
                g.tensor_tensor(rbw[:, 512:768], rbw[:, 0:256],
                                alB[:, blk * 256:(blk + 1) * 256], ALU.mult).then_inc(S["po"], 1)
                g.wait_ge(S["dv"], bm["rb2"])
                g.tensor_tensor(rbw[:, 768:1024], rbw[:, 256:512],
                                alA[:, blk * 256:(blk + 1) * 256], ALU.mult).then_inc(S["po"], 1)

    att_done = wmk[31][3]["att"]

    wo_ld += [V(("ga" if i % 2 == 0 else "gb"), 16) for i in range(2, 32)]
    ope = [V("pe") for _ in range(32)]
    res_dv = [V("dv") for _ in range(32)]

    @ctx.on("gpsimd")
    def _(g):
        for oc in range(2, 32):
            g.wait_ge(S["pe"], ope[oc - 2])
            g.dma_start(out=wstr[:, (oc % 2) * 4096:(oc % 2) * 4096 + 4096]
                        .rearrange("p (a f) -> p a f", f=128),
                        in_=wo_h.ap()[oc]).then_inc(S["ga" if oc % 2 == 0 else "gb"], 16)

    @ctx.on("tensor")
    def _(t):
        t.wait_ge(S["dv"], att_done)
        for oc in range(32):
            t.wait_ge(S["ga" if oc % 2 == 0 else "gb"], wo_ld[oc])
            if oc >= 2:
                t.wait_ge(S["dv"], res_dv[oc - 2])
            ps = pA2 if oc % 2 == 0 else pC2
            for hd in range(32):
                for half in range(2):
                    ins = t.matmul(ps[:, half * 512: half * 512 + 512],
                                   wstr[:, (oc % 2) * 4096 + hd * 128:(oc % 2) * 4096 + (hd + 1) * 128],
                                   QA[:, hd * T + half * 512: hd * T + half * 512 + 512],
                                   start=(hd == 0), stop=(hd == 31))
            ins.then_inc(S["pe"], 1)

    @ctx.on("vector")
    def _(v):
        for oc in range(32):
            v.wait_ge(S["pe"], ope[oc])
            ps = pA2 if oc % 2 == 0 else pC2
            v.tensor_tensor(xcen[:, oc * T:(oc + 1) * T], xcen[:, oc * T:(oc + 1) * T],
                            ps[:, 0:T], ALU.add).then_inc(S["dv"], 1)

    ctx.mark["xcen_done"] = res_dv[-1]





def emit_l0_ffn(ctx, c1t_h, c2T_h, w1tt_h, w2tt_h, ffnacc_h):
    nc = ctx.nc
    xcen, r, rbf = ctx.buf["xcen"], ctx.buf["r"], ctx.buf["rbf"]
    ones, id_sb = ctx.buf["ones128"], ctx.buf["id_sb"]
    pA2, pB2, pC2, pEb = ctx.buf["pA2"], ctx.buf["pB2"], ctx.buf["pC2"], ctx.buf["pEb"]
    wstr = ctx.buf["wstr"]
    scr = ctx.buf["scr2T"]
    S, V = ctx.sem, ctx.inc
    center_dv, r_ac = ctx.mark["center_dv"], ctx.mark["r_ac"]

    c1t = ctx.ssbuf("c1t_sb", [128, 32 * 16], BF16)
    u_sb = ctx.ssbuf("u_sb", [16, T], BF16)
    c2s = ctx.ssbuf("c2s_sb", [16, 2 * 2048], BF16)
    hmg = ctx.ssbuf("hmg_sb", [128, 28 * T], BF16)
    zrb = ctx.ssbuf("zrb_sb", [128, 2 * T], BF16)
    mk = ctx.ssbuf("mk_sb", [128, 2 * T], BF16)
    mhat = ctx.ssbuf("mhat_sb", [128, 2 * T], BF16)
    itf = ctx.ssbuf("itf_sb", [128, 2 * T], BF16)
    m_tm = ctx.ssbuf("mtm_sb", [128, 8 * 64], F32)
    m_tb = ctx.ssbuf("mtb_sb", [128, 8 * 64], BF16)
    mT = ctx.ssbuf("mT_sb", [64, T], BF16)
    fstg = ctx.ssbuf("fstg_sb", [128, 2 * T], BF16)

    sd_tot = ctx.stage_load_emit([(c1t[:].rearrange("p (a f) -> p a f", f=16), c1t_h.ap())])

    NG = 4
    GI = 28

    u_pe = V("pe")
    u_dv = V("dv")
    m0_dv = V("dv")

    @ctx.on("tensor")
    def _(t):
        t.wait_ge(S["dv"], center_dv)
        t.wait_ge(S["sd"], sd_tot)
        for dc in range(32):
            for half in range(2):
                ins = t.matmul(pA2[0:16, half * 512: half * 512 + 512],
                               c1t[:, dc * 16:(dc + 1) * 16],
                               xcen[:, dc * T + half * 512: dc * T + half * 512 + 512],
                               start=(dc == 0), stop=(dc == 31))
        ins.then_inc(S["pe"], 1)

    @ctx.on("vector")
    def _(v):
        v.wait_ge(S["pe"], u_pe)
        v.tensor_copy(u_sb[:], pA2[0:16, 0:T]).then_inc(S["dv"], 1)
        v.memset(m_tm[:], 0.0).then_inc(S["dv"], 1)

    c2a_ld = [V("ka" if k % 2 == 0 else "kb", 16) for k in range(7)]
    it_pe = [[None] * 32 for _ in range(7)]
    it_ac = [[None] * 32 for _ in range(7)]
    it_dv = [[None] * 32 for _ in range(7)]
    for k in range(7):
        for z_ in range(32):
            it_pe[k][z_] = V("pe")
    for k in range(7):
        for z_ in range(32):
            it_ac[k][z_] = V("ac")
    for k in range(7):
        for z_ in range(32):
            it_dv[k][z_] = V("dv")

    @ctx.on("gpsimd")
    def _(g):
        for k in range(7):
            if k >= 2:
                g.wait_ge(S["pe"], it_pe[k - 2][31])
            g.dma_start(out=c2s[:, (k % 2) * 2048:(k % 2) * 2048 + 2048],
                        in_=c2T_h.ap()[:, k * 2048:(k + 1) * 2048]
                        ).then_inc(S["ka" if k % 2 == 0 else "kb"], 16)

    @ctx.on("tensor")
    def _(t):
        t.wait_ge(S["dv"], u_dv)
        for k in range(7):
            t.wait_ge(S["ka" if k % 2 == 0 else "kb"], c2a_ld[k])
            for tc in range(8):
                for q in range(4):
                    z_ = tc * 4 + q
                    if k * 32 + z_ >= 2:
                        pk, pz = divmod(k * 32 + z_ - 2, 32)
                        t.wait_ge(S["ac"], it_ac[pk][pz])
                    t.matmul(pC2[:, (z_ % 2) * 512:(z_ % 2) * 512 + 512],
                             u_sb[:, tc * 128:(tc + 1) * 128],
                             c2s[:, (k % 2) * 2048 + q * 512:(k % 2) * 2048 + q * 512 + 512],
                             start=True, stop=True).then_inc(S["pe"], 1)

    @ctx.on("scalar")
    def _(a):
        for k in range(7):
            for z_ in range(32):
                a.wait_ge(S["pe"], it_pe[k][z_])
                if k * 32 + z_ >= 4:
                    pk, pz = divmod(k * 32 + z_ - 4, 32)
                    a.wait_ge(S["dv"], it_dv[pk][pz])
                a.activation(scr[:, (z_ % 4) * 512:(z_ % 4) * 512 + 512],
                             pC2[:, (z_ % 2) * 512:(z_ % 2) * 512 + 512],
                             AF.Identity).then_inc(S["ac"], 1)

    @ctx.on("vector")
    def _(v):
        v.wait_ge(S["dv"], m0_dv)
        for k in range(7):
            for z_ in range(32):
                tc, q = divmod(z_, 4)
                v.wait_ge(S["ac"], it_ac[k][z_])
                blk0 = k * 4 + q
                v.reduce_max(m_tm[:, tc * 64 + (k * 4 + q) * 2: tc * 64 + (k * 4 + q) * 2 + 2].unsqueeze(2),
                             scr[:, (z_ % 4) * 512:(z_ % 4) * 512 + 512]
                             .rearrange("p (s j) -> p s j", s=2),
                             axis=AX.X).then_inc(S["dv"], 1)

    mcast_dv = V("dv")
    mt_pe = [V("pe") for _ in range(8)]
    mt_ac = [V("ac") for _ in range(8)]

    @ctx.on("vector")
    def _(v):
        v.wait_ge(S["dv"], it_dv[6][31])
        v.tensor_copy(m_tb[:], m_tm[:]).then_inc(S["dv"], 1)

    @ctx.on("tensor")
    def _(t):
        t.wait_ge(S["dv"], mcast_dv)
        for tc in range(8):
            if tc >= 2:
                t.wait_ge(S["ac"], mt_ac[tc - 2])
            t.transpose(pEb[0:64, (tc % 2) * T:(tc % 2) * T + 128],
                        m_tb[:, tc * 64:(tc + 1) * 64], id_sb[:]).then_inc(S["pe"], 1)

    @ctx.on("scalar")
    def _(a):
        for tc in range(8):
            a.wait_ge(S["pe"], mt_pe[tc])
            a.activation(mT[:, tc * 128:(tc + 1) * 128],
                         pEb[0:64, (tc % 2) * T:(tc % 2) * T + 128],
                         AF.Identity).then_inc(S["ac"], 1)

    mt_done = mt_ac[-1]

    w1_ld = [None] * 112
    c2b_ld = [V("ka" if k % 2 == 0 else "kb", 16) for k in range(7)]
    mh_pe = [None] * 56
    mh_ac = [None] * 56
    itb_pe = [None] * 112
    z_pe = [None] * 112
    rel_ac = [None] * 112
    cmp_dv = [None] * 112
    mkr_po = [None] * 112
    hm_dv = [None] * 112
    o_pe = [[None] * 32 for _ in range(NG)]
    o_dv = [[None] * 32 for _ in range(NG)]
    o_st = [[None] * 32 for _ in range(NG)]
    w2_ld = [[None] * 32 for _ in range(NG)]

    for g_ in range(NG):
        for i_ in range(GI):
            ic = g_ * GI + i_
            if ic % 2 == 0:
                mh_pe[ic // 2] = V("pe")
            itb_pe[ic] = V("pe")
            z_pe[ic] = V("pe")
        for oc in range(32):
            o_pe[g_][oc] = V("pe")
    for g_ in range(NG):
        for i_ in range(GI):
            w1_ld[g_ * GI + i_] = V(("ga" if (g_ * GI + i_) % 2 == 0 else "gb"), 16)
        for oc in range(32):
            w2_ld[g_][oc] = V(("ga" if oc % 2 == 0 else "gb"), 16)
    itf_ac = [None] * 112
    for g_ in range(NG):
        for i_ in range(GI):
            ic = g_ * GI + i_
            if ic % 2 == 0:
                mh_ac[ic // 2] = V("ac")
            itf_ac[ic] = V("ac")
            rel_ac[ic] = V("ac")
    for g_ in range(NG):
        for i_ in range(GI):
            ic = g_ * GI + i_
            cmp_dv[ic] = V("dv")
            hm_dv[ic] = V("dv")
        for oc in range(32):
            o_dv[g_][oc] = V("dv")
    for g_ in range(NG):
        for oc in range(32):
            o_st[g_][oc] = V("xka" if oc % 2 == 0 else "xkb", 16)

    @ctx.on("gpsimd")
    def _(g):
        kc_loaded = -1
        for g_ in range(NG):
            for i_ in range(GI):
                ic = g_ * GI + i_
                k = ic // 16
                if k > kc_loaded:
                    kc_loaded = k
                    if k >= 2:
                        g.wait_ge(S["pe"], itb_pe[k * 16 - 16 - 1])
                    else:
                        g.wait_ge(S["pe"], it_pe[6 - k][31])
                    g.dma_start(out=c2s[:, (k % 2) * 2048:(k % 2) * 2048 + 2048],
                                in_=c2T_h.ap()[:, k * 2048:(k + 1) * 2048]
                                ).then_inc(S["ka" if k % 2 == 0 else "kb"], 16)
                if ic >= 2:
                    g.wait_ge(S["pe"], z_pe[ic - 2])
                elif g_ == 0 and i_ < 2:
                    g.wait_ge(S["pe"], u_pe)
                g.dma_start(out=wstr[:, (ic % 2) * 4096:(ic % 2) * 4096 + 4096]
                            .rearrange("p (a f) -> p a f", f=128),
                            in_=w1tt_h.ap()[ic]).then_inc(S["ga" if ic % 2 == 0 else "gb"], 16)
            for oc in range(34):
                if oc < 32:
                    if oc >= 2:
                        g.wait_ge(S["pe"], o_pe[g_][oc - 2])
                    else:
                        g.wait_ge(S["pe"], z_pe[g_ * GI + GI - 2 + oc])
                    g.dma_start(out=wstr[:, (oc % 2) * 4096:(oc % 2) * 4096 + 3584]
                                .rearrange("p (a f) -> p a f", f=128),
                                in_=w2tt_h.ap()[g_][oc]).then_inc(S["ga" if oc % 2 == 0 else "gb"], 16)
                if oc >= 2:
                    ac = oc - 2
                    g.wait_ge(S["dv"], o_dv[g_][ac])
                    g.dma_start(out=ffnacc_h.ap()[ac * 128:(ac + 1) * 128, :],
                                in_=fstg[:, (ac % 2) * T:(ac % 2) * T + T],
                                accum_op=(ALU.bypass if g_ == 0 else ALU.add)
                                ).then_inc(S["xka" if ac % 2 == 0 else "xkb"], 16)

    @ctx.on("tensor")
    def _(t):
        for g_ in range(NG):
            for i_ in range(GI):
                ic = g_ * GI + i_
                j = ic // 2
                k = ic // 16
                if ic % 2 == 0:
                    t.wait_ge(S["ac"], mt_done)
                    if ic >= 1:
                        t.wait_ge(S["ac"], itf_ac[ic - 1])
                    for half in range(2):
                        ins = t.matmul(pB2[:, half * 512: half * 512 + 512],
                                       id_sb[0:56, j:j + 1].broadcast_to((56, 128)),
                                       mT[0:56, half * 512: half * 512 + 512],
                                       start=True, stop=True)
                    ins.then_inc(S["pe"], 1)
                t.wait_ge(S["ka" if k % 2 == 0 else "kb"], c2b_ld[k])
                t.wait_ge(S["ac"], mh_ac[j])
                if ic % 2 == 1:
                    t.wait_ge(S["ac"], itf_ac[ic - 1])
                for half in range(2):
                    ins = t.matmul(pB2[:, half * 512: half * 512 + 512],
                                   c2s[:, (k % 2) * 2048 + (ic % 16) * 128:(k % 2) * 2048 + (ic % 16) * 128 + 128],
                                   u_sb[:, half * 512: half * 512 + 512],
                                   start=True, stop=True)
                ins.then_inc(S["pe"], 1)
                t.wait_ge(S["ga" if ic % 2 == 0 else "gb"], w1_ld[ic])
                if ic >= 2:
                    t.wait_ge(S["ac"], rel_ac[ic - 2])
                elif g_ > 0:
                    pass
                ps = pA2 if ic % 2 == 0 else pC2
                for dc in range(32):
                    for half in range(2):
                        ins = t.matmul(ps[:, half * 512: half * 512 + 512],
                                       wstr[:, (ic % 2) * 4096 + dc * 128:(ic % 2) * 4096 + (dc + 1) * 128],
                                       xcen[:, dc * T + half * 512: dc * T + half * 512 + 512],
                                       start=(dc == 0), stop=(dc == 31))
                ins.then_inc(S["pe"], 1)
            for oc in range(32):
                t.wait_ge(S["ga" if oc % 2 == 0 else "gb"], w2_ld[g_][oc])
                if oc == 0:
                    t.wait_ge(S["dv"], hm_dv[g_ * GI + GI - 1])
                if oc >= 2:
                    t.wait_ge(S["dv"], o_dv[g_][oc - 2])
                elif g_ > 0:
                    t.wait_ge(S["dv"], o_dv[g_ - 1][30 + oc])
                else:
                    t.wait_ge(S["ac"], rel_ac[g_ * GI + GI - 2 + oc])
                ps = pA2 if oc % 2 == 0 else pC2
                for i_ in range(GI):
                    for half in range(2):
                        ins = t.matmul(ps[:, half * 512: half * 512 + 512],
                                       wstr[:, (oc % 2) * 4096 + i_ * 128:(oc % 2) * 4096 + (i_ + 1) * 128],
                                       hmg[:, i_ * T + half * 512: i_ * T + half * 512 + 512],
                                       start=(i_ == 0), stop=(i_ == GI - 1))
                ins.then_inc(S["pe"], 1)

    @ctx.on("scalar")
    def _(a):
        for g_ in range(NG):
            for i_ in range(GI):
                ic = g_ * GI + i_
                j = ic // 2
                if ic % 2 == 0:
                    a.wait_ge(S["pe"], mh_pe[j])
                    if j >= 2:
                        a.wait_ge(S["dv"], cmp_dv[2 * j - 3])
                    a.activation(mhat[:, (j % 2) * T:(j % 2) * T + T],
                                 pB2[:, 0:T], AF.Identity).then_inc(S["ac"], 1)
                a.wait_ge(S["pe"], itb_pe[ic])
                if ic >= 2:
                    a.wait_ge(S["dv"], cmp_dv[ic - 2])
                a.activation(itf[:, (ic % 2) * T:(ic % 2) * T + T],
                             pB2[:, 0:T], AF.Identity).then_inc(S["ac"], 1)
                a.wait_ge(S["pe"], z_pe[ic])
                if ic >= 2:
                    a.wait_ge(S["dv"], hm_dv[ic - 2])
                ps = pA2 if ic % 2 == 0 else pC2
                a.activation(zrb[:, (ic % 2) * T:(ic % 2) * T + T],
                             ps[:, 0:T], AF.Relu).then_inc(S["ac"], 1)

    @ctx.on("vector")
    def _(v):
        for g_ in range(NG):
            for i_ in range(GI):
                ic = g_ * GI + i_
                j = ic // 2
                v.wait_ge(S["ac"], itf_ac[ic])
                v.wait_ge(S["ac"], mh_ac[j])
                v.tensor_tensor(mk[:, (ic % 2) * T:(ic % 2) * T + T],
                                itf[:, (ic % 2) * T:(ic % 2) * T + T],
                                mhat[:, (j % 2) * T:(j % 2) * T + T],
                                ALU.is_ge).then_inc(S["dv"], 1)
                v.tensor_tensor(mk[:, (ic % 2) * T:(ic % 2) * T + T],
                                mk[:, (ic % 2) * T:(ic % 2) * T + T],
                                rbf[:], ALU.mult)
                v.wait_ge(S["ac"], rel_ac[ic])
                v.tensor_tensor(hmg[:, i_ * T:(i_ + 1) * T],
                                zrb[:, (ic % 2) * T:(ic % 2) * T + T],
                                mk[:, (ic % 2) * T:(ic % 2) * T + T],
                                ALU.mult).then_inc(S["dv"], 1)
            for oc in range(32):
                v.wait_ge(S["pe"], o_pe[g_][oc])
                if g_ * 32 + oc >= 2:
                    pg, po_ = divmod(g_ * 32 + oc - 2, 32)
                    v.wait_ge(S["xka" if po_ % 2 == 0 else "xkb"], o_st[pg][po_])
                ps = pA2 if oc % 2 == 0 else pC2
                v.tensor_copy(fstg[:, (oc % 2) * T:(oc % 2) * T + T],
                              ps[:, 0:T]).then_inc(S["dv"], 1)

    xa_tot = ctx.cnt["xka"]
    xb_tot = ctx.cnt["xkb"]
    rb_ld = [V("ra" if oc % 2 == 0 else "rb", 16) for oc in range(32)]
    sc_dv = [None] * 32
    res_dv = [None] * 32
    for oc in range(32):
        sc_dv[oc] = V("dv")
        res_dv[oc] = V("dv")

    @ctx.on("sync")
    def _(s):
        s.wait_ge(S["xka"], xa_tot)
        s.wait_ge(S["xkb"], xb_tot)
        for oc in range(32):
            if oc >= 2:
                s.wait_ge(S["dv"], sc_dv[oc - 2])
            s.dma_start(out=fstg[:, (oc % 2) * T:(oc % 2) * T + T],
                        in_=ffnacc_h.ap()[oc * 128:(oc + 1) * 128, :]
                        ).then_inc(S["ra" if oc % 2 == 0 else "rb"], 16)

    @ctx.on("vector")
    def _(v):
        for oc in range(32):
            v.wait_ge(S["ra" if oc % 2 == 0 else "rb"], rb_ld[oc])
            v.tensor_tensor(scr[:, (oc % 2) * T:(oc % 2) * T + T],
                            fstg[:, (oc % 2) * T:(oc % 2) * T + T],
                            rbf[:], ALU.mult).then_inc(S["dv"], 1)
            v.wait_ge(S["dv"], sc_dv[oc])
            v.tensor_tensor(xcen[:, oc * T:(oc + 1) * T], xcen[:, oc * T:(oc + 1) * T],
                            scr[:, (oc % 2) * T:(oc % 2) * T + T],
                            ALU.add).then_inc(S["dv"], 1)

    ctx.mark["xcen_done"] = res_dv[-1]



SCALE128 = 0.08838834764831845


def emit_l1_attn(ctx, wq_h, wk_h, wv_h, wo_h, meta_h, sh_k, sh_v, sh_flag):
    nc = ctx.nc
    xcen, r = ctx.buf["xcen"], ctx.buf["r"]
    ones, id_sb = ctx.buf["ones128"], ctx.buf["id_sb"]
    pA2, pB2, pC2, pEb = ctx.buf["pA2"], ctx.buf["pB2"], ctx.buf["pC2"], ctx.buf["pEb"]
    wstr = ctx.buf["wstr"]
    S, V = ctx.sem, ctx.inc
    center_dv, r_ac = ctx.mark["center_dv"], ctx.mark["r_ac"]

    QeT = ctx.ssbuf("QeT_sb", [128, 8 * T], BF16)
    AtT = ctx.ssbuf("AtT_sb", [128, 8 * T], BF16)
    kvbf = ctx.ssbuf("kvbf_sb", [128, 2 * T], BF16)
    vtt = ctx.ssbuf("vtt_sb", [128, 2 * T], BF16)
    Khsb = ctx.ssbuf("Khsb", [128, 2 * 2048], BF16)
    Vhsb = ctx.ssbuf("Vhsb", [128, 2 * 2048], BF16)
    PTsb = ctx.ssbuf("PTsb", [128, 4 * 512], BF16)
    rbh = ctx.ssbuf("rbh_sb", [128, 2 * 512], F32)

    def gsem(i):
        return "ga" if i % 2 == 0 else "gb"

    def xsem(i):
        return "xka" if i % 2 == 0 else "xkb"

    wk_ld = [V(gsem(i), 16) for i in range(8)]
    kpe = [V("pe") for _ in range(8)]
    kev = [V("dv") for _ in range(8)]
    k_st = [V(xsem(i), 16) for i in range(8)]

    @ctx.on("gpsimd")
    def _(g):
        for kdc in range(8):
            if kdc >= 2:
                g.wait_ge(S["pe"], kpe[kdc - 2])
            g.dma_start(out=wstr[:, (kdc % 2) * 4096:(kdc % 2) * 4096 + 4096]
                        .rearrange("p (a f) -> p a f", f=128),
                        in_=wk_h.ap()[kdc]).then_inc(S[gsem(kdc)], 16)

    @ctx.on("tensor")
    def _(t):
        t.wait_ge(S["dv"], center_dv)
        for kdc in range(8):
            t.wait_ge(S[gsem(kdc)], wk_ld[kdc])
            if kdc >= 2:
                t.wait_ge(S["dv"], kev[kdc - 2])
            ps = pA2 if kdc % 2 == 0 else pC2
            for dc in range(32):
                for half in range(2):
                    ins = t.matmul(ps[:, half * 512: half * 512 + 512],
                                   wstr[:, (kdc % 2) * 4096 + dc * 128:(kdc % 2) * 4096 + (dc + 1) * 128],
                                   xcen[:, dc * T + half * 512: dc * T + half * 512 + 512],
                                   start=(dc == 0), stop=(dc == 31))
            ins.then_inc(S["pe"], 1)

    @ctx.on("vector")
    def _(v):
        v.wait_ge(S["ac"], r_ac)
        for kdc in range(8):
            v.wait_ge(S["pe"], kpe[kdc])
            if kdc >= 2:
                v.wait_ge(S[xsem(kdc - 2)], k_st[kdc - 2])
            ps = pA2 if kdc % 2 == 0 else pC2
            v.tensor_tensor(kvbf[:, (kdc % 2) * T:(kdc % 2) * T + T],
                            ps[:, 0:T], r[:], ALU.mult).then_inc(S["dv"], 1)

    @ctx.on("sync")
    def _(g):
        with g.register("preg_k") as preg:
            g.reg_load(preg, meta_h.ap()[0:1, 0:1])
            for kdc in range(8):
                g.wait_ge(S["dv"], kev[kdc])
                src = kvbf[:, (kdc % 2) * T:(kdc % 2) * T + T]
                with g.If_ne(preg, 0):
                    g.dma_start(out=sh_k.ap()[1][kdc * 128:(kdc + 1) * 128, :],
                                in_=src).then_inc(S[xsem(kdc)], 16)
                with g.Else():
                    g.dma_start(out=sh_k.ap()[0][kdc * 128:(kdc + 1) * 128, :],
                                in_=src).then_inc(S[xsem(kdc)], 16)

    wv_ld = [V(gsem(i), 16) for i in range(8)]
    vpe = [None] * 8
    vtr = [[None] * 8 for _ in range(8)]
    vpe[0] = V("pe")
    vpe[1] = V("pe")
    for vdc in range(2, 8):
        vtr[vdc - 2] = [V("pe") for _ in range(8)]
        vpe[vdc] = V("pe")
    vtr[6] = [V("pe") for _ in range(8)]
    vtr[7] = [V("pe") for _ in range(8)]
    vev = [V("dv") for _ in range(8)]
    vte = [[V("ac") for _ in range(8)] for _ in range(8)]
    v_st = [V(xsem(i), 16) for i in range(8)]

    @ctx.on("gpsimd")
    def _(g):
        for vdc in range(8):
            if vdc < 2:
                g.wait_ge(S["pe"], kpe[6 + vdc])
            else:
                g.wait_ge(S["pe"], vpe[vdc - 2])
            g.dma_start(out=wstr[:, (vdc % 2) * 4096:(vdc % 2) * 4096 + 4096]
                        .rearrange("p (a f) -> p a f", f=128),
                        in_=wv_h.ap()[vdc]).then_inc(S[gsem(vdc)], 16)

    def _emit_v_mm(t, vdc):
        t.wait_ge(S[gsem(vdc)], wv_ld[vdc])
        if vdc >= 2:
            t.wait_ge(S["dv"], vev[vdc - 2])
        else:
            t.wait_ge(S["dv"], kev[6 + vdc])
        ps = pA2 if vdc % 2 == 0 else pC2
        for dc in range(32):
            for half in range(2):
                ins = t.matmul(ps[:, half * 512: half * 512 + 512],
                               wstr[:, (vdc % 2) * 4096 + dc * 128:(vdc % 2) * 4096 + (dc + 1) * 128],
                               xcen[:, dc * T + half * 512: dc * T + half * 512 + 512],
                               start=(dc == 0), stop=(dc == 31))
        ins.then_inc(S["pe"], 1)

    def _emit_v_tr(t, vdc):
        for tc in range(8):
            gi = vdc * 8 + tc
            t.wait_ge(S["dv"], vev[vdc])
            if gi >= 2:
                t.wait_ge(S["ac"], vte[(gi - 2) // 8][(gi - 2) % 8])
            t.transpose(pEb[:, (gi % 2) * T:(gi % 2) * T + 128],
                        kvbf[:, (vdc % 2) * T + tc * 128:(vdc % 2) * T + (tc + 1) * 128],
                        id_sb[:]).then_inc(S["pe"], 1)

    @ctx.on("tensor")
    def _(t):
        _emit_v_mm(t, 0)
        _emit_v_mm(t, 1)
        for vdc in range(2, 8):
            _emit_v_tr(t, vdc - 2)
            _emit_v_mm(t, vdc)
        _emit_v_tr(t, 6)
        _emit_v_tr(t, 7)

    @ctx.on("vector")
    def _(v):
        for vdc in range(8):
            v.wait_ge(S["pe"], vpe[vdc])
            if vdc < 2:
                v.wait_ge(S[xsem(vdc)], k_st[6 + vdc])
            else:
                v.wait_ge(S[xsem(vdc)], v_st[vdc - 2])
            ps = pA2 if vdc % 2 == 0 else pC2
            v.tensor_tensor(kvbf[:, (vdc % 2) * T:(vdc % 2) * T + T],
                            ps[:, 0:T], r[:], ALU.mult).then_inc(S["dv"], 1)

    @ctx.on("scalar")
    def _(a):
        for vdc in range(8):
            for tc in range(8):
                gi = vdc * 8 + tc
                a.wait_ge(S["pe"], vtr[vdc][tc])
                if tc == 0 and vdc >= 2:
                    a.wait_ge(S[xsem(vdc)], v_st[vdc - 2])
                a.activation(vtt[:, (vdc % 2) * T + tc * 128:(vdc % 2) * T + (tc + 1) * 128],
                             pEb[:, (gi % 2) * T:(gi % 2) * T + 128],
                             AF.Identity).then_inc(S["ac"], 1)

    @ctx.on("sync")
    def _(g):
        with g.register("preg_v") as preg:
            g.reg_load(preg, meta_h.ap()[0:1, 0:1])
            for vdc in range(8):
                g.wait_ge(S["ac"], vte[vdc][7])
                src = vtt[:, (vdc % 2) * T:(vdc % 2) * T + T].rearrange(
                    "p (c f) -> p c f", f=128)
                with g.If_ne(preg, 0):
                    g.dma_start(out=sh_v.ap()[1].rearrange("(c p) v -> p c v", p=128)
                                [:, :, vdc * 128:(vdc + 1) * 128],
                                in_=src).then_inc(S[xsem(vdc)], 16)
                with g.Else():
                    g.dma_start(out=sh_v.ap()[0].rearrange("(c p) v -> p c v", p=128)
                                [:, :, vdc * 128:(vdc + 1) * 128],
                                in_=src).then_inc(S[xsem(vdc)], 16)

    xka_total = ctx.cnt["xka"]
    xkb_total = ctx.cnt["xkb"]

    wq_ld = [V(gsem(i), 16) for i in range(32)]
    qspe = [V("pe") for _ in range(32)]
    qev = [V("dv") for _ in range(8)]

    @ctx.on("gpsimd")
    def _(g):
        for mc in range(32):
            if mc < 2:
                g.wait_ge(S["pe"], vpe[6 + mc])
            else:
                g.wait_ge(S["pe"], qspe[mc - 2])
            g.dma_start(out=wstr[:, (mc % 2) * 4096:(mc % 2) * 4096 + 4096]
                        .rearrange("p (a f) -> p a f", f=128),
                        in_=wq_h.ap()[mc]).then_inc(S[gsem(mc)], 16)

    @ctx.on("tensor")
    def _(t):
        for hkv in range(8):
            ps = pA2 if hkv % 2 == 0 else pC2
            if hkv < 2:
                t.wait_ge(S["dv"], vev[6 + hkv])
            else:
                t.wait_ge(S["dv"], qev[hkv - 2])
            for gq in range(4):
                mc = hkv * 4 + gq
                t.wait_ge(S[gsem(mc)], wq_ld[mc])
                for dc in range(32):
                    for half in range(2):
                        ins = t.matmul(ps[:, half * 512: half * 512 + 512],
                                       wstr[:, (mc % 2) * 4096 + dc * 128:(mc % 2) * 4096 + (dc + 1) * 128],
                                       xcen[:, dc * T + half * 512: dc * T + half * 512 + 512],
                                       start=(gq == 0 and dc == 0), stop=(gq == 3 and dc == 31))
                ins.then_inc(S["pe"], 1)

    @ctx.on("vector")
    def _(v):
        for hkv in range(8):
            v.wait_ge(S["pe"], qspe[hkv * 4 + 3])
            ps = pA2 if hkv % 2 == 0 else pC2
            v.tensor_tensor(QeT[:, hkv * T:(hkv + 1) * T], ps[:, 0:T], r[:],
                            ALU.mult).then_inc(S["dv"], 1)

    v_xc = V("xc")

    @ctx.on("gpsimd")
    def _(g):
        with (g.register("preg_x") as preg, g.register("nonce_x") as nonce,
              g.register("ne_x") as ne, g.register("freg_x") as freg):
            g.reg_load(preg, meta_h.ap()[0:1, 0:1])
            g.reg_load(nonce, meta_h.ap()[0:1, 1:2])
            g.wait_ge(S["xka"], xka_total)
            g.wait_ge(S["xkb"], xkb_total)
            with g.If_ne(preg, 0):
                g.reg_save(sh_flag.ap()[1:2, 0:1], nonce)
            with g.Else():
                g.reg_save(sh_flag.ap()[0:1, 0:1], nonce)
            g.reg_mov(ne, 1)
            with g.While(ne):
                with g.If_ne(preg, 0):
                    g.reg_load(freg, sh_flag.ap()[0:1, 0:1])
                with g.Else():
                    g.reg_load(freg, sh_flag.ap()[1:2, 0:1])
                g.reg_alu(ne, freg, nonce, ALU.not_equal)
            g.sem_inc(S["xc"], 1)

    kh_ld = [V("ka" if i % 2 == 0 else "kb", 64) for i in range(8)]
    att_marks = []
    for hkv in range(8):
        per_q = []
        for qh in range(2):
            m = {"exp": [], "spe": [], "dpe": [], "rb": None, "att": None}
            per_q.append(m)
        att_marks.append(per_q)
    for hkv in range(8):
        for qh in range(2):
            m = att_marks[hkv][qh]
            for kc in range(16):
                m["spe"].append(V("pe"))
                m["dpe"].append(V("pe"))
            m["rb"] = V("dv")
            m["att"] = V("dv")
    for hkv in range(8):
        for qh in range(2):
            m = att_marks[hkv][qh]
            for kc in range(16):
                m["exp"].append(V("ac"))

    @ctx.on("sync")
    def _(s):
        s.wait_ge(S["xc"], v_xc)
        for hkv in range(8):
            if hkv >= 2:
                s.wait_ge(S["dv"], att_marks[hkv - 2][1]["att"])
            hp = hkv % 2
            sem = S["ka" if hkv % 2 == 0 else "kb"]
            for slot in range(2):
                s.dma_start(out=Khsb[:, hp * 2048 + slot * T: hp * 2048 + (slot + 1) * T],
                            in_=sh_k.ap()[slot][hkv * 128:(hkv + 1) * 128, :]).then_inc(sem, 16)
                s.dma_start(out=Vhsb[:, hp * 2048 + slot * T: hp * 2048 + (slot + 1) * T]
                            .rearrange("p (c f) -> p c f", f=128),
                            in_=sh_v.ap()[slot].rearrange("(c p) v -> p c v", p=128)
                            [:, :, hkv * 128:(hkv + 1) * 128]).then_inc(sem, 16)

    @ctx.on("tensor")
    def _(t):
        t.wait_ge(S["dv"], qev[7])
        for hkv in range(8):
            hp = hkv % 2
            t.wait_ge(S["ka" if hkv % 2 == 0 else "kb"], kh_ld[hkv])
            t.wait_ge(S["dv"], qev[hkv])
            for qh in range(2):
                m = att_marks[hkv][qh]
                prev = att_marks[hkv][0] if qh == 1 else (
                    att_marks[hkv - 1][1] if hkv > 0 else None)
                for kc in range(16):
                    if kc >= 2:
                        t.wait_ge(S["ac"], m["exp"][kc - 2])
                    elif prev is not None:
                        t.wait_ge(S["ac"], prev["exp"][14 + kc])
                    if kc == 0 and hkv > 0:
                        t.wait_ge(S["dv"], att_marks[hkv - 1][qh]["att"])
                    t.matmul(pA2[:, (kc % 2) * 512:(kc % 2) * 512 + 512],
                             Khsb[:, hp * 2048 + kc * 128: hp * 2048 + (kc + 1) * 128],
                             QeT[:, hkv * T + qh * 512: hkv * T + qh * 512 + 512],
                             start=True, stop=True).then_inc(S["pe"], 1)
                    t.wait_ge(S["ac"], m["exp"][kc])
                    t.matmul(pB2[:, qh * 512: qh * 512 + 512],
                             ones[:], PTsb[:, (kc % 2) * 512:(kc % 2) * 512 + 512],
                             start=(kc == 0), stop=(kc == 15))
                    t.matmul(pC2[:, qh * 512: qh * 512 + 512],
                             Vhsb[:, hp * 2048 + kc * 128: hp * 2048 + (kc + 1) * 128],
                             PTsb[:, (kc % 2) * 512:(kc % 2) * 512 + 512],
                             start=(kc == 0), stop=(kc == 15)).then_inc(S["pe"], 1)

    @ctx.on("scalar")
    def _(a):
        prev_m = None
        for hkv in range(8):
            for qh in range(2):
                m = att_marks[hkv][qh]
                for kc in range(16):
                    a.wait_ge(S["pe"], m["spe"][kc])
                    if kc >= 2:
                        a.wait_ge(S["pe"], m["dpe"][kc - 2])
                    elif prev_m is not None:
                        a.wait_ge(S["pe"], prev_m["dpe"][14 + kc])
                    a.activation(PTsb[:, (kc % 2) * 512:(kc % 2) * 512 + 512],
                                 pA2[:, (kc % 2) * 512:(kc % 2) * 512 + 512],
                                 AF.Exp, scale=SCALE128).then_inc(S["ac"], 1)
                prev_m = m

    @ctx.on("vector")
    def _(v):
        for hkv in range(8):
            for qh in range(2):
                m = att_marks[hkv][qh]
                v.wait_ge(S["pe"], m["dpe"][15])
                v.reciprocal(rbh[:, qh * 512: qh * 512 + 512],
                             pB2[:, qh * 512: qh * 512 + 512]).then_inc(S["dv"], 1)
                v.wait_ge(S["dv"], m["rb"])
                v.tensor_tensor(AtT[:, hkv * T + qh * 512: hkv * T + qh * 512 + 512],
                                pC2[:, qh * 512: qh * 512 + 512],
                                rbh[:, qh * 512: qh * 512 + 512],
                                ALU.mult).then_inc(S["dv"], 1)

    wo_ld = [V(gsem(i), 16) for i in range(32)]
    ope = [V("pe") for _ in range(32)]
    res_dv = [V("dv") for _ in range(32)]

    @ctx.on("gpsimd")
    def _(g):
        for oc in range(32):
            if oc < 2:
                g.wait_ge(S["pe"], qspe[30 + oc])
            else:
                g.wait_ge(S["pe"], ope[oc - 2])
            g.dma_start(out=wstr[:, (oc % 2) * 4096:(oc % 2) * 4096 + 1024]
                        .rearrange("p (a f) -> p a f", f=128),
                        in_=wo_h.ap()[oc]).then_inc(S[gsem(oc)], 16)

    @ctx.on("tensor")
    def _(t):
        t.wait_ge(S["dv"], att_marks[7][1]["att"])
        for oc in range(32):
            t.wait_ge(S[gsem(oc)], wo_ld[oc])
            if oc >= 2:
                t.wait_ge(S["dv"], res_dv[oc - 2])
            ps = pA2 if oc % 2 == 0 else pC2
            for kvc in range(8):
                for half in range(2):
                    ins = t.matmul(ps[:, half * 512: half * 512 + 512],
                                   wstr[:, (oc % 2) * 4096 + kvc * 128:(oc % 2) * 4096 + (kvc + 1) * 128],
                                   AtT[:, kvc * T + half * 512: kvc * T + half * 512 + 512],
                                   start=(kvc == 0), stop=(kvc == 7))
            ins.then_inc(S["pe"], 1)

    @ctx.on("vector")
    def _(v):
        for oc in range(32):
            v.wait_ge(S["pe"], ope[oc])
            ps = pA2 if oc % 2 == 0 else pC2
            v.tensor_tensor(xcen[:, oc * T:(oc + 1) * T], xcen[:, oc * T:(oc + 1) * T],
                            ps[:, 0:T], ALU.add).then_inc(S["dv"], 1)

    ctx.mark["xcen_done"] = res_dv[-1]



def emit_output(ctx, out_h):
    xcen, mu_acc = ctx.buf["xcen"], ctx.buf["mu_acc"]
    id_sb, pEb = ctx.buf["id_sb"], ctx.buf["pEb"]
    S, V = ctx.sem, ctx.inc
    entry = ctx.mark["xcen_done"]
    out_sb = ctx.buf["oscr"]
    mutok = ctx.ssbuf("mutok_sb", [128, 8], F32)

    res_list = ctx.mark.get("res_list")
    v_mutok = emit_bcast_to_tok(ctx, mu_acc, mutok, "dv",
                                ctx.mark.get("rtok_done", entry))

    ev_marks = {}
    st_done = {}
    for tc in range(8):
        tr_pe = [V("pe") for _ in range(8)]
        ac_ev = [V("ac") for _ in range(8)]
        for dgrp in range(8):
            ev_marks[tc * 8 + dgrp] = ac_ev[dgrp]
            st_done[tc * 8 + dgrp] = V("sta" if dgrp % 2 == 0 else "stb", 16)

        @ctx.on("tensor")
        def _(t, tc=tc):
            for dgrp in range(8):
                gi = tc * 8 + dgrp
                if gi < 2:
                    t.wait_ge(S["dv"], v_mutok[-1])
                else:
                    t.wait_ge(S["ac"], ev_marks[gi - 2])
                if res_list is not None:
                    t.wait_ge(S["dv"], res_list[dgrp * 4 + 3])
                else:
                    t.wait_ge(S["dv"], entry)
                for j in range(4):
                    dc = dgrp * 4 + j
                    ins = t.transpose(pEb[:, (dgrp % 2) * T + j * 128:(dgrp % 2) * T + j * 128 + 128],
                                      xcen[:, dc * T + tc * 128: dc * T + tc * 128 + 128],
                                      id_sb[:])
                ins.then_inc(S["pe"], 1)

        @ctx.on("scalar")
        def _(a, tc=tc, tr_pe=tr_pe):
            a.wait_ge(S["dv"], v_mutok[tc])
            for dgrp in range(8):
                gi = tc * 8 + dgrp
                a.wait_ge(S["pe"], tr_pe[dgrp])
                if gi >= 2:
                    a.wait_ge(S["sta" if dgrp % 2 == 0 else "stb"], st_done[gi - 2])
                a.activation(out_sb[:, (dgrp % 2) * 512:(dgrp % 2) * 512 + 512],
                             pEb[:, (dgrp % 2) * T:(dgrp % 2) * T + 512],
                             AF.Identity, bias=mutok[:, tc:tc + 1]).then_inc(S["ac"], 1)

        @ctx.on("sync")
        def _(s, tc=tc, ac_ev=ac_ev):
            for dgrp in range(8):
                s.wait_ge(S["ac"], ac_ev[dgrp])
                s.dma_start(out=out_h.ap()[tc * 128:(tc + 1) * 128, dgrp * 512:(dgrp + 1) * 512],
                            in_=out_sb[:, (dgrp % 2) * 512:(dgrp % 2) * 512 + 512]
                            ).then_inc(S["sta" if dgrp % 2 == 0 else "stb"], 16)

    @ctx.on("sync")
    def _(s):
        s.wait_ge(S["sta"], ctx.cnt["sta"])
        s.wait_ge(S["stb"], ctx.cnt["stb"])


def emit_output_xT(ctx, outT_h):
    xcen, mu_acc = ctx.buf["xcen"], ctx.buf["mu_acc"]
    S, V = ctx.sem, ctx.inc
    entry = ctx.mark["xcen_done"]
    xo = ctx.buf["oscr"]

    v_ev = [V("dv") for _ in range(NT)]
    st = [V("sta" if i % 2 == 0 else "stb", 16) for i in range(NT)]

    @ctx.on("vector")
    def _(v):
        v.wait_ge(S["dv"], entry)
        for i in range(NT):
            if i >= 1:
                v.wait_ge(S["sta" if i % 2 == 1 else "stb"], st[i - 1])
            v.tensor_tensor(xo[:], xcen[:, i * T:(i + 1) * T], mu_acc[:],
                            ALU.add).then_inc(S["dv"], 1)

    @ctx.on("sync")
    def _(s):
        for i in range(NT):
            s.wait_ge(S["dv"], v_ev[i])
            s.dma_start(out=outT_h.ap()[i * 128:(i + 1) * 128, :],
                        in_=xo[:]).then_inc(S["sta" if i % 2 == 0 else "stb"], 16)
        s.wait_ge(S["sta"], ctx.cnt["sta"])
        s.wait_ge(S["stb"], ctx.cnt["stb"])



def build_program(stages, out_mode="tokmajor", detect_races=True):
    nc = bass.Bass("TRN2", target_bir_lowering=False,
                   detect_race_conditions=detect_races)
    ctx = Ctx(nc)
    for s in ["swa", "swb", "sg", "ga", "gb", "sc", "sd", "pe", "dv", "ac",
              "po", "sta", "stb", "xc", "xka", "xkb", "ka", "kb", "ha", "hb",
              "ra", "rb"]:
        ctx.add_sem(s)

    xT_h = nc.dram_tensor("xT", [D, T], F32, kind="ExternalInput")

    ctx.sbuf("xcen", [128, NT * T], BF16)
    ctx.sbuf("scr2T", [128, 2 * T], BF16)
    ctx.sbuf("dmu", [128, T], F32)
    ctx.sbuf("dmub", [128, T], BF16)
    ctx.sbuf("var", [128, T], F32)
    ctx.sbuf("r", [128, T], F32)
    ctx.sbuf("mu_acc", [128, T], F32)
    ctx.sbuf("tmpT", [128, T], F32)
    ctx.sbuf("oscr", [128, T], F32)
    ctx.sbuf("rbf", [128, T], BF16)
    ctx.sbuf("ones128", [128, 128], BF16)
    ctx.sbuf("id_sb", [128, 128], BF16)
    ctx.sbuf("rep_sb", [4, 64], BF16)
    ctx.sbuf("eps_sb", [128, 1], F32)
    ctx.sbuf("wstr", [128, 2 * 4096], BF16)
    ctx.sbuf("prott_sb", [128, 128], BF16)
    ctx.psum("pA2", [128, T])
    ctx.psum("pB2", [128, T])
    ctx.psum("pC2", [128, T])
    ctx.psum("pEb", [128, 2 * T], BF16)

    rep = np.zeros((4, 64), dtype=BF)
    for e in range(4):
        rep[e, e * 16:(e + 1) * 16] = BF(4.0)
    rep4_ap = nc.inline_tensor(rep, name="rep4c")
    ident_ap = nc.inline_tensor(np.eye(128).astype(BF), name="identc")
    ones_ap = nc.inline_tensor(np.ones((128, 128), dtype=BF), name="ones128c")
    eps_ap = nc.inline_tensor(np.full((128, 1), EPS, dtype=np.float32), name="epsc")
    protm = np.zeros((128, 128), dtype=BF)
    for d_ in range(64):
        protm[64 + d_, d_] = BF(-1.0)
    for d_ in range(64, 128):
        protm[d_ - 64, d_] = BF(1.0)
    prot_ap = nc.inline_tensor(protm, name="protc")

    ctx.static_load(ctx.buf["ones128"][:], ones_ap.ap())
    ctx.static_load(ctx.buf["id_sb"][:], ident_ap.ap())
    ctx.static_load(ctx.buf["rep_sb"][:], rep4_ap.ap())
    ctx.static_load(ctx.buf["eps_sb"][:], eps_ap.ap())
    ctx.static_load(ctx.buf["prott_sb"][:], prot_ap.ap())

    ctx.init_arena()
    emit_load_and_ln1(ctx, xT_h)

    handles = {}
    for si, st in enumerate(stages):
        if si > 0:
            gate = ctx.mark["xcen_done"]
            for _eng in ["sync", "gpsimd", "tensor", "vector", "scalar"]:
                def _gatef(e, gate=gate):
                    e.wait_ge(ctx.sem["dv"], gate)
                ctx.q[_eng].append(_gatef)
            ctx.arena_reset()
            emit_ln(ctx)
        if st == "l1ffn":
            wrT_h = nc.dram_tensor("wrT", [D, 4], F32, kind="ExternalInput")
            aT_h = nc.dram_tensor("aT", [D, 64], F32, kind="ExternalInput")
            bT_h = nc.dram_tensor("bT", [64, D], F32, kind="ExternalInput")
            emit_l1_ffn(ctx, wrT_h, aT_h, bT_h)
        elif st == "l0attn":
            wq_h = nc.dram_tensor("wq0t", [32, 128, 32, 128], F32, kind="ExternalInput")
            wo_h = nc.dram_tensor("wo0t", [32, 128, 32, 128], F32, kind="ExternalInput")
            wk_h = nc.dram_tensor("wk0t", [1, 128, 32, 128], F32, kind="ExternalInput")
            wv_h = nc.dram_tensor("wv0t", [1, 128, 32, 128], F32, kind="ExternalInput")
            xhalo_h = nc.dram_tensor("xhaloT", [D, 256], F32, kind="ExternalInput")
            ck_h = nc.dram_tensor("coskT", [128, 1536], BF16, kind="ExternalInput")
            sk_h = nc.dram_tensor("sinkT", [128, 1536], BF16, kind="ExternalInput")
            denm_h = nc.dram_tensor("denmI", [128, 12 * 128], BF16, kind="ExternalInput")
            alA_h = nc.dram_tensor("alAI", [128, 4 * 256], BF16, kind="ExternalInput")
            alB_h = nc.dram_tensor("alBI", [128, 4 * 256], BF16, kind="ExternalInput")
            hsel_h = nc.dram_tensor("hselI", [128, 2], F32, kind="ExternalInput")
            emit_l0_attn(ctx, wq_h, wo_h, wk_h, wv_h, xhalo_h, ck_h,
                         sk_h, denm_h, alA_h, alB_h, hsel_h)
        elif st == "l0ffn":
            c1t_h = nc.dram_tensor("c1tI", [128, 32, 16], F32, kind="ExternalInput")
            c2T_h = nc.dram_tensor("c2TI", [16, 14336], F32, kind="ExternalInput")
            w1tt_h = nc.dram_tensor("w1tt", [112, 128, 32, 128], F32, kind="ExternalInput")
            w2tt_h = nc.dram_tensor("w2tt", [4, 32, 128, 28, 128], F32, kind="ExternalInput")
            ffnacc_h = nc.dram_tensor("ffnacc", [D, T], BF16)
            emit_l0_ffn(ctx, c1t_h, c2T_h, w1tt_h, w2tt_h, ffnacc_h)
        elif st == "l1attn":
            wq_h = nc.dram_tensor("wq1t", [32, 128, 32, 128], F32, kind="ExternalInput")
            wk_h = nc.dram_tensor("wk1t", [8, 128, 32, 128], F32, kind="ExternalInput")
            wv_h = nc.dram_tensor("wv1t", [8, 128, 32, 128], F32, kind="ExternalInput")
            wo_h = nc.dram_tensor("wo1t", [32, 128, 8, 128], F32, kind="ExternalInput")
            meta_h = nc.dram_tensor("meta", [1, 8], I32, kind="ExternalInput")
            sh_k = nc.dram_tensor("sh_k", [2, 1024, 1024], BF16, addr_space="Shared")
            sh_v = nc.dram_tensor("sh_v", [2, 1024, 1024], BF16, addr_space="Shared")
            sh_flag = nc.dram_tensor("sh_flag", [2, 16], I32, addr_space="Shared")
            emit_l1_attn(ctx, wq_h, wk_h, wv_h, wo_h, meta_h, sh_k, sh_v, sh_flag)
        else:
            raise ValueError(st)

    if out_mode == "tokmajor":
        out_h = nc.dram_tensor("out", [T, D], F32, kind="ExternalOutput")
        emit_output(ctx, out_h)
    else:
        outT_h = nc.dram_tensor("outT", [D, T], F32, kind="ExternalOutput")
        emit_output_xT(ctx, outT_h)

    assert ctx.arena_base + ctx.arena_peak <= int(nc.sbuf_top), \
        (ctx.arena_base, ctx.arena_peak, int(nc.sbuf_top))

    @ctx.on("gpsimd")
    def _(g):
        pass

    def _static(g):
        for dst, src in ctx.static_loads:
            g.dma_start(out=dst, in_=src).then_inc(ctx.sem["sc"], 16)
    ctx.q["gpsimd"].insert(0, _static)

    def _scwait(e):
        e.wait_ge(ctx.sem["sc"], ctx.sc_total())
    for _eng in ["sync", "tensor", "vector", "scalar"]:
        ctx.q[_eng].insert(0, _scwait)
    ctx.q["gpsimd"].insert(1, _scwait)

    with nc.Block() as block:
        @block.sync
        def _(e):
            for f in ctx.q["sync"]:
                f(e)

        @block.gpsimd
        def _(e):
            for f in ctx.q["gpsimd"]:
                f(e)

        @block.tensor
        def _(e):
            for f in ctx.q["tensor"]:
                f(e)

        @block.vector
        def _(e):
            for f in ctx.q["vector"]:
                f(e)

        @block.scalar
        def _(e):
            for f in ctx.q["scalar"]:
                f(e)

    return nc


def host_inputs_l1ffn(x_full, A, B_, Wr):
    wrT = np.ascontiguousarray(Wr.T)
    aT = np.ascontiguousarray(A.transpose(2, 0, 1).reshape(D, 64))
    bT = np.ascontiguousarray(B_.transpose(0, 2, 1).reshape(64, D))
    maps = []
    for c in range(8):
        b, h = c // 2, c % 2
        xT = np.ascontiguousarray(x_full[b, h * T:(h + 1) * T, :].T)
        maps.append({"xT": xT, "wrT": wrT, "aT": aT, "bT": bT})
    return maps


def host_inputs_l1attn(x_full, wq, wk, wv, wo, nonce):
    wq1t = pack_lhsT(wq)
    wk1t = pack_lhsT(wk)
    wv1t = pack_lhsT(wv)
    wo1t = pack_lhsT(wo)
    maps = []
    for c in range(8):
        b, h = c // 2, c % 2
        xT = np.ascontiguousarray(x_full[b, h * T:(h + 1) * T, :].T)
        meta = np.array([[h, nonce, 0, 0, 0, 0, 0, 0]], dtype=np.int32)
        maps.append({"xT": xT, "wq1t": wq1t, "wk1t": wk1t, "wv1t": wv1t,
                     "wo1t": wo1t, "meta": meta})
    return maps


def _rope_tables():
    dh = 128
    inv_freq = 1.0 / 10000.0 ** (np.arange(0, dh, 2, dtype=np.float32) / dh)
    t = np.arange(2048, dtype=np.float32)
    f = np.outer(t, inv_freq)
    cos = np.zeros((2048, dh), dtype=np.float32)
    sin = np.zeros((2048, dh), dtype=np.float32)
    cos[:, 0::2] = np.cos(f); cos[:, 1::2] = np.cos(f)
    sin[:, 0::2] = np.sin(f); sin[:, 1::2] = np.sin(f)
    return cos, sin


def host_inputs_l0attn(x_full, Wq, Wk, Wv, Wo):
    cos, sin = _rope_tables()
    wq0t = pack_lhsT(np.ascontiguousarray(Wq.T))
    wo0t = pack_lhsT(np.ascontiguousarray(Wo.T))
    wk0t = pack_lhsT(np.ascontiguousarray(Wk.T))
    wv0t = pack_lhsT(np.ascontiguousarray(Wv.T))
    lin = (np.arange(256, dtype=np.float32) / 255.0)
    maps = []
    for c in range(8):
        b, h = c // 2, c % 2
        xT = np.ascontiguousarray(x_full[b, h * T:(h + 1) * T, :].T)
        if h == 0:
            xhalo = x_full[b, 1024:1280, :]
        else:
            xhalo = x_full[b, 768:1024, :]
        xhaloT = np.ascontiguousarray(xhalo.T)
        lo = 1024 * h - 256
        extpos = np.clip(np.arange(1536) + lo, 0, 2047)
        cosk = np.ascontiguousarray(cos[extpos].T).astype(BF)
        sink = np.ascontiguousarray(sin[extpos].T).astype(BF)
        realc = np.zeros(12, dtype=np.float32)
        for ch in range(12):
            realc[ch] = 1.0 if (ch >= 2 if h == 0 else ch < 10) else 0.0
        denm = np.repeat(realc, 128)[None, :].repeat(128, axis=0).astype(BF)
        alA = np.zeros((4, 256), dtype=np.float32)
        alB = np.zeros((4, 256), dtype=np.float32)
        for blk in range(4):
            if h == 0 and blk == 0:
                alA[blk] = 1.0
                alB[blk] = 0.0
            else:
                alA[blk] = lin
                alB[blk] = 1.0 - lin
        alA_t = np.broadcast_to(alA.reshape(1, 4 * 256), (128, 1024)).astype(BF)
        alB_t = np.broadcast_to(alB.reshape(1, 4 * 256), (128, 1024)).astype(BF)
        hsel = np.zeros((128, 2), dtype=np.float32)
        hsel[:, 0] = 1.0 if h == 1 else 0.0
        hsel[:, 1] = 1.0 if h == 0 else 0.0
        maps.append({"xT": xT, "wq0t": wq0t, "wo0t": wo0t, "wk0t": wk0t,
                     "wv0t": wv0t, "xhaloT": xhaloT, "coskT": cosk, "sinkT": sink,
                     "denmI": np.ascontiguousarray(denm),
                     "alAI": np.ascontiguousarray(alA_t),
                     "alBI": np.ascontiguousarray(alB_t), "hselI": hsel})
    return maps


def host_inputs_l0ffn(x_full, c1_w, c2_w, w1, w2t):
    c1t = np.ascontiguousarray(c1_w.T.reshape(32, 128, 16).transpose(1, 0, 2))
    c2T = np.ascontiguousarray(c2_w.T)
    w1tt = pack_lhsT(np.ascontiguousarray(w1.T))
    w2tt = np.ascontiguousarray(
        w2t.reshape(4, 28, 128, 32, 128).transpose(0, 3, 2, 1, 4))
    maps = []
    for c in range(8):
        b, h = c // 2, c % 2
        xT = np.ascontiguousarray(x_full[b, h * T:(h + 1) * T, :].T)
        maps.append({"xT": xT, "c1tI": c1t, "c2TI": c2T, "w1tt": w1tt,
                     "w2tt": w2tt})
    return maps


def _assemble_xfull(res, key, featmaj):
    x = np.zeros((4, 2048, D), dtype=np.float32)
    for c in range(8):
        b, h = c // 2, c % 2
        o = res.results[c][key]
        x[b, h * T:(h + 1) * T, :] = o.T if featmaj else o
    return x


def host_inputs_fused(inputs, nonce):
    x = np.ascontiguousarray(inputs["L0_attn_x"], dtype=np.float32)
    m0 = host_inputs_l0attn(x, inputs["L0_attn_Wq"], inputs["L0_attn_Wk"],
                            inputs["L0_attn_Wv"], inputs["L0_attn_Wo"])
    m1 = host_inputs_l0ffn(x, inputs["L0_ffn_c1_w"], inputs["L0_ffn_c2_w"],
                           inputs["L0_ffn_w1"], inputs["L0_ffn_w2t"])
    m2 = host_inputs_l1attn(x, inputs["L1_attn_wq"], inputs["L1_attn_wk"],
                            inputs["L1_attn_wv"], inputs["L1_attn_wo"], nonce)
    m3 = host_inputs_l1ffn(x, inputs["L1_ffn_A"], inputs["L1_ffn_B"],
                           inputs["L1_ffn_Wr"])
    return [{**m0[c], **m1[c], **m2[c], **m3[c]} for c in range(8)]


def kernel(**inputs):
    nonce = int(np.random.default_rng().integers(1, 2 ** 30))
    nc = build_program(["l0attn", "l0ffn", "l1attn", "l1ffn"],
                       out_mode="tokmajor")
    maps = host_inputs_fused(inputs, nonce)
    res = run_bass_kernel_spmd(nc, maps, core_ids=list(range(8)))
    return _assemble_xfull(res, "out", False)


def kernel_unfused(**inputs):
    x = np.ascontiguousarray(inputs["L0_attn_x"], dtype=np.float32)
    cores = list(range(8))

    nc1 = build_program(["l0attn"], out_mode="xT")
    maps = host_inputs_l0attn(x, inputs["L0_attn_Wq"], inputs["L0_attn_Wk"],
                              inputs["L0_attn_Wv"], inputs["L0_attn_Wo"])
    res = run_bass_kernel_spmd(nc1, maps, core_ids=cores)
    x = _assemble_xfull(res, "outT", True)

    nc2 = build_program(["l0ffn"], out_mode="xT")
    maps = host_inputs_l0ffn(x, inputs["L0_ffn_c1_w"], inputs["L0_ffn_c2_w"],
                             inputs["L0_ffn_w1"], inputs["L0_ffn_w2t"])
    res = run_bass_kernel_spmd(nc2, maps, core_ids=cores)
    x = _assemble_xfull(res, "outT", True)

    nonce = int(np.random.default_rng().integers(1, 2 ** 30))
    nc3 = build_program(["l1attn"], out_mode="xT")
    maps = host_inputs_l1attn(x, inputs["L1_attn_wq"], inputs["L1_attn_wk"],
                              inputs["L1_attn_wv"], inputs["L1_attn_wo"], nonce)
    res = run_bass_kernel_spmd(nc3, maps, core_ids=cores)
    x = _assemble_xfull(res, "outT", True)

    nc4 = build_program(["l1ffn"], out_mode="tokmajor")
    maps = host_inputs_l1ffn(x, inputs["L1_ffn_A"], inputs["L1_ffn_B"],
                             inputs["L1_ffn_Wr"])
    res = run_bass_kernel_spmd(nc4, maps, core_ids=cores)
    return _assemble_xfull(res, "out", False)
